# version 08
# baseline (speedup 1.0000x reference)
import numpy as np
import concourse.bass as bass
import concourse.mybir as mybir
from concourse.bass_utils import run_bass_kernel_spmd

F32 = mybir.dt.float32
BF16 = mybir.dt.bfloat16
FP8 = mybir.dt.float8e4
I32 = mybir.dt.int32
U8 = mybir.dt.uint8
AF = mybir.ActivationFunctionType
ALU = mybir.AluOpType
AX = mybir.AxisListType

L = 2048
D = 1024
DI = 2048
NST = 16
DTR = 64
G8 = 8
G16 = 16
LC0 = 256
LC = 512
EPS_LN = 1e-5
STAGE = "full"
DEBUG = False
DEBUG_H2 = False
RACECHECK = False
CUT = 99
NOGW = False
NCORES = 8


class Ctx:
    def __init__(self, nc):
        self.nc = nc
        self.sems = {}
        self.cnt = {}
        self.consts = {}
        self.const_tile = None
        self.pools = {}

    def cst(self, value):
        if value not in self.consts:
            i = len(self.consts)
            assert i < 32
            self.inc(self.nc.vector.memset(self.const_tile[:, i:i + 1], float(value)), "V")
            self.consts[value] = self.const_tile[:, i:i + 1]
        return self.consts[value]

    def sem(self, name):
        if name not in self.sems:
            self.sems[name] = self.nc.alloc_semaphore("sem_" + name)
            self.cnt[name] = 0
        return self.sems[name]

    def inc(self, instr, name, amt=1):
        instr.then_inc(self.sem(name), amt)
        self.cnt[name] += amt
        return self.cnt[name]

    def wait(self, eng, name, val=None):
        self.sem(name)
        v = self.cnt[name] if val is None else val
        if v > 0:
            eng.wait_ge(self.sems[name], v)

    def mkpool(self, name, n):
        self.pools[name] = [[[self.nc.alloc_semaphore(f"dp_{name}{i}"), 0] for i in range(n)], 0]

    def dgrp(self, q, transfers, pool, slow=False):
        pl = self.pools[pool]
        slot = pl[1]
        sem, used = pl[0][slot]
        pl[1] = (slot + 1) % len(pl[0])
        if used:
            q.wait_ge(sem, used * 16)
        for out, in_ in transfers:
            q.dma_start(out=out, in_=in_, allow_slow_non_contiguous=slow).then_inc(sem, 16)
            used += 1
        pl[0][slot][1] = used
        return (sem, used * 16)

    def dwait(self, eng, tok):
        eng.wait_ge(tok[0], tok[1])

    def vsync(self, instr):
        v = self.inc(instr, "V")
        self.nc.vector.wait_ge(self.sems["V"], v)
        return v

    @staticmethod
    def colsT(tile, flat, ncols):
        return [(tile[:, g:g + 1], flat[g * 128:(g + 1) * 128].rearrange("(p o) -> p o", o=1))
                for g in range(ncols)]

    @staticmethod
    def colsS(flat, tile, ncols):
        return [(flat[g * 128:(g + 1) * 128].rearrange("(p o) -> p o", o=1), tile[:, g:g + 1])
                for g in range(ncols)]


def emit_ln_stats(ctx, sb, x_src_hbm, rstd_flat, mrstd_flat, sflat, extra=None):
    nc = ctx.nc
    NCh = L // LC
    xb, sqb, sums = sb["statx"], sb["statsq"], sb["sums2"]
    ps0, ps1 = sb["ps_st0"], sb["ps_st1"]
    ctx.wait(nc.tensor, "V")
    t1mark = []
    st_tok = None
    st_all = []
    for c in range(NCh):
        W = slice(c * LC, (c + 1) * LC)
        for g in range(G8):
            h = g % 2
            xs = xb[:, h * LC:(h + 1) * LC]
            sq = sqb[:, h * LC:(h + 1) * LC]
            if len(t1mark) >= 2:
                ctx.wait(nc.sync, "T", t1mark[-2])
            tok = ctx.dgrp(nc.sync, [(xs, x_src_hbm[g * 128:(g + 1) * 128, W])], "ld")
            ctx.dwait(nc.scalar, tok)
            if len(t1mark) >= 2:
                ctx.wait(nc.scalar, "T", t1mark[-2])
            sqm = ctx.inc(nc.scalar.activation(sq, xs, AF.Square), "S")
            ctx.dwait(nc.tensor, tok)
            ctx.inc(nc.tensor.matmul(ps0[0:1, 0:LC], sb["ones_f32"][:, 0:1], xs,
                                     start=(g == 0), stop=(g == G8 - 1)), "T")
            ctx.wait(nc.tensor, "S", sqm)
            t1mark.append(ctx.inc(nc.tensor.matmul(ps1[0:1, 0:LC], sb["ones_f32"][:, 0:1], sq,
                                                   start=(g == 0), stop=(g == G8 - 1)), "T"))
        ctx.wait(nc.scalar, "T")
        if st_tok is not None:
            ctx.dwait(nc.scalar, st_tok)
        ctx.inc(nc.scalar.copy(sums[0:1, 0:LC], ps0[0:1, 0:LC]), "S")
        se = ctx.inc(nc.scalar.copy(sums[0:1, LC:2 * LC], ps1[0:1, 0:LC]), "S")
        ctx.wait(nc.tensor, "S", se)
        ctx.wait(nc.sync, "S", se)
        st_tok = ctx.dgrp(nc.sync, [
            (sflat[c * LC:(c + 1) * LC].unsqueeze(0), sums[0:1, 0:LC]),
            (sflat[L + c * LC:L + (c + 1) * LC].unsqueeze(0), sums[0:1, LC:2 * LC])], "st")
        st_all.append(st_tok)
    for t_ in st_all:
        ctx.dwait(nc.sync, t_)
    _ln_tail(ctx, sb, rstd_flat, mrstd_flat, sflat, extra)


def _ln_tail(ctx, sb, rstd_flat, mrstd_flat, sflat, extra=None):
    nc = ctx.nc
    rowA, rowB = sb["scanA"][0:1, :], sb["scanB"][0:1, :]
    tok = ctx.dgrp(nc.sync, [(rowA[0:1, 0:2 * L], sflat[:].unsqueeze(0))], "ld")
    ctx.dwait(nc.vector, tok)
    mu_row = rowB[0:1, 0:L]
    var_row = rowB[0:1, L:2 * L]
    nc.vector.tensor_scalar(mu_row, rowA[0:1, 0:L], 1.0 / D, None, ALU.mult)
    nc.vector.tensor_scalar(var_row, rowA[0:1, L:2 * L], 1.0 / D, None, ALU.mult)
    nc.vector.tensor_tensor(rowA[0:1, 0:L], mu_row, mu_row, ALU.mult)
    v = ctx.inc(nc.vector.tensor_tensor(var_row, var_row, rowA[0:1, 0:L], ALU.subtract), "V")
    ctx.wait(nc.scalar, "V", v)
    s = ctx.inc(nc.scalar.activation(rowA[0:1, L:2 * L], var_row, AF.Sqrt, bias=ctx.cst(EPS_LN)[0:1, :]), "S")
    ctx.wait(nc.vector, "S", s)
    rstd_row = rowA[0:1, 0:L]
    v = ctx.vsync(nc.vector.reciprocal(rstd_row, rowA[0:1, L:2 * L]))
    mr_row = var_row
    v = ctx.inc(nc.vector.tensor_tensor(mr_row, mu_row, rstd_row, ALU.mult), "V")
    ctx.wait(nc.sync, "V", v)
    tok = ctx.dgrp(nc.sync, [(rstd_flat[:].unsqueeze(0), rstd_row),
                             (mrstd_flat[:].unsqueeze(0), mr_row)], "st")
    if extra is not None:
        extra(tok, dict(rstd=rstd_row, mu=mu_row, mrstd=mr_row, rowB=rowB, rowA=rowA))
    else:
        ctx.dwait(nc.sync, tok)


def emit_l0a(ctx, sb, hb):
    nc = ctx.nc
    NCh = L // LC0
    NS = L // LC
    xin = sb["xin"]
    ctx.vsync(nc.vector.memset(sb["carry"].bitcast(BF16)[:, 0:256], 0.0))
    vA = sb["xin"].bitcast(BF16)
    vB = sb["delta"].bitcast(BF16)

    def h1sl(g, cs):
        t = vA if g < 4 else vB
        return t[:, (g % 4) * 2048 + cs.start:(g % 4) * 2048 + cs.stop]

    rb4, mb4 = sb["statx"][:, 0:LC], sb["statx"][:, LC:2 * LC]
    for s in range(NS):
        W = slice(s * LC, (s + 1) * LC)
        ctx.wait(nc.sync, "V")
        btok = ctx.dgrp(nc.sync, [(rb4, hb["st_r1"][W].unsqueeze(0).partition_broadcast(128)),
                                  (mb4, hb["st_m1"][W].unsqueeze(0).partition_broadcast(128))], "ld")
        ctx.dwait(nc.vector, btok)
        for g in range(G8):
            xsl = sb["statsq"][:, (g % 2) * LC:((g % 2) + 1) * LC]
            ctx.wait(nc.sync, "V")
            xtok = ctx.dgrp(nc.sync, [(xsl, hb["x0T"][g * 128:(g + 1) * 128, W])], "ld")
            ctx.dwait(nc.vector, xtok)
            nc.vector.tensor_tensor(sb["stage2"][:, :], xsl, rb4, ALU.mult)
            ctx.inc(nc.vector.tensor_tensor(h1sl(g, W), sb["stage2"][:, :], mb4, ALU.subtract), "V")

    wbuf, psW = sb["wbuf"], sb["ps_w"]
    h0b = sb["h0"]
    ctx.wait(nc.tensor, "V")
    st_tok4 = [None, None, None, None]
    bank_rel = [None, None]
    tw_mark = []
    ei = 0
    for mt in range(32):
        wb = mt % 2
        if len(tw_mark) >= 2:
            ctx.wait(nc.sync, "TW", tw_mark[-2])
        if mt == 0:
            ctx.dwait(nc.sync, hb["w2d_tok"])
        wtok = ctx.dgrp(nc.sync, [(wbuf[:, wb * 1024:(wb + 1) * 1024].rearrange("p (kg m) -> p kg m", kg=8),
                                   hb["winbf"][:, mt * 128:(mt + 1) * 128].rearrange("(kg p) m -> p kg m", p=128))],
                        "ld")
        ctx.dwait(nc.tensor, wtok)
        tw = None
        for sp in range(NS):
            Wsp = slice(sp * LC, (sp + 1) * LC)
            pbuf = (mt * NS + sp) % 2
            if bank_rel[pbuf] is not None:
                ctx.wait(nc.tensor, bank_rel[pbuf][0], bank_rel[pbuf][1])
            for k in range(G8):
                mm = nc.tensor.matmul(psW[pbuf][:, 0:LC],
                                      wbuf[:, wb * 1024 + k * 128:wb * 1024 + (k + 1) * 128],
                                      h1sl(k, Wsp), start=(k == 0), stop=(k == G8 - 1))
            tw = ctx.inc(mm, "TW")
            ctx.wait(nc.scalar, "TW", tw)
            slot = ei % 4
            ei += 1
            if st_tok4[slot] is not None:
                ctx.dwait(nc.scalar, st_tok4[slot])
            dst = h0b[:, slot * 512:(slot + 1) * 512]
            if mt < 16:
                se = ctx.inc(nc.scalar.copy(dst, psW[pbuf][:, 0:LC]), "SE")
            else:
                se = ctx.inc(nc.scalar.activation(dst, psW[pbuf][:, 0:LC], AF.Silu), "SE")
            bank_rel[pbuf] = ("SE", se)
            ctx.wait(nc.gpsimd, "SE", se)
            tgt = hb["xscr"] if mt < 16 else hb["zscr"]
            st_tok4[slot] = ctx.dgrp(nc.gpsimd, [(tgt[(mt % 16) * 128:(mt % 16 + 1) * 128, Wsp], dst)], "gw")
        tw_mark.append(tw)
    for t_ in st_tok4:
        if t_ is not None:
            ctx.dwait(nc.sync, t_)
            ctx.dwait(nc.gpsimd, t_)

    xinb = vA[:, 0:G16 * (LC0 + 3)]
    zsb = sb["zsilu"].bitcast(BF16)[:, 0:G16 * LC0]
    ctx.wait(nc.vector, "TW")
    ctx.vsync(nc.vector.memset(xinb[:, :], 0.0))
    dv_rel = [None, None]
    gp_rel = [0, 0]
    fix_mark = [0, 0]
    pend = None
    ystore = None
    for c in range(NCh):
        W = slice(c * LC0, (c + 1) * LC0)
        t1h = sb["tmpc"]
        ctx.wait(nc.sync, "V")
        xtok = ctx.dgrp(nc.sync, [(xinb[:, g * (LC0 + 3) + 3:(g + 1) * (LC0 + 3)],
                                   hb["xscr"][g * 128:(g + 1) * 128, W]) for g in range(G16)], "ld")
        ztok = ctx.dgrp(nc.sync, [(zsb[:, :].rearrange("p (g l) -> p g l", g=16),
                                   hb["zscr"][:, W].rearrange("(g p) l -> p g l", p=128))], "ld")
        ctx.dwait(nc.vector, xtok)
        ctx.dwait(nc.vector, ztok)
        if ystore is not None:
            ctx.dwait(nc.vector, ystore)
        xc, xcb, dwk, psX = sb["xc"], sb["xcb"], sb["dwk"], sb["ps_x"]
        t1s = [sb["tmpc"], sb["yg"]]
        ctx.wait(nc.vector, "SE")
        ctx.wait(nc.tensor, "SE")
        conv_s = [None, None]
        for g in range(G16):
            base = g * (LC0 + 3)
            t1 = t1s[g % 2]
            if conv_s[g % 2] is not None:
                ctx.wait(nc.vector, "S", conv_s[g % 2])
            nc.vector.tensor_scalar(t1[:, :], xinb[:, base:base + LC0], dwk[:, g * 4:g * 4 + 1], None, ALU.mult)
            for i in (1, 2, 3):
                ins = nc.vector.scalar_tensor_tensor(t1[:, :], xinb[:, base + i:base + i + LC0],
                                                     dwk[:, g * 4 + i:g * 4 + i + 1], t1[:, :],
                                                     ALU.mult, ALU.add)
            v = ctx.inc(ins, "V")
            ctx.wait(nc.scalar, "V", v)
            s = ctx.inc(nc.scalar.activation(xc[:, g * LC0:(g + 1) * LC0], t1[:, :], AF.Silu,
                                             bias=sb["dwb"][:, g:g + 1]), "S")
            conv_s[g % 2] = s
            ctx.wait(nc.scalar, "S", s)
            s2 = ctx.inc(nc.scalar.copy(xcb[:, g * LC0:(g + 1) * LC0], xc[:, g * LC0:(g + 1) * LC0]), "S")
            ctx.wait(nc.tensor, "S", s2)
            mm = nc.tensor.matmul(psX[0:96, 0:LC0], sb["wx_sb"][:, g * 96:(g + 1) * 96],
                                  xcb[:, g * LC0:(g + 1) * LC0], start=(g == 0), stop=(g == G16 - 1))
        ctx.inc(mm, "T")
        for g in range(G16):
            base = g * (LC0 + 3)
            ctx.inc(nc.vector.tensor_copy(xinb[:, base:base + 3], xinb[:, base + LC0:base + LC0 + 3]), "V")
        dt16, bc96 = sb["dt16"], sb["bc96"]
        ctx.wait(nc.scalar, "T")
        ctx.inc(nc.scalar.copy(dt16[:, :], psX[0:64, 0:LC0]), "SE")
        se = ctx.inc(nc.scalar.copy(bc96[64:96, 0:LC0], psX[64:96, 0:LC0]), "SE")
        ctx.wait(nc.gpsimd, "SE", se)
        fl_tok = ctx.dgrp(nc.gpsimd, [(hb["Bflat"][:, W], bc96[64:80, 0:LC0]),
                                      (hb["Cflat"][:, W], bc96[80:96, 0:LC0])], "gw")
        if CUT < 5:
            continue
        psD, delta = sb["ps_d"], sb["delta"]
        s_mark = []
        ctx.wait(nc.tensor, "SE")
        for g in range(G16):
            if len(s_mark) >= 2:
                ctx.wait(nc.tensor, "S", s_mark[-2])
            t = ctx.inc(nc.tensor.matmul(psD[g % 2][:, 0:LC0], sb["wdt_sb"][:, g * 128:(g + 1) * 128],
                                         dt16[:, :], start=True, stop=True), "T")
            ctx.wait(nc.scalar, "T", t)
            nc.scalar.activation(psD[g % 2][:, 0:LC0], psD[g % 2][:, 0:LC0], AF.Exp,
                                 bias=sb["dtbias"][:, g:g + 1])
            s_mark.append(ctx.inc(nc.scalar.activation(delta[:, g * LC0:(g + 1) * LC0],
                                                       psD[g % 2][:, 0:LC0], AF.Ln, bias=1.0), "S"))
        Bb, Cb = sb["Bb"], sb["Cb"]
        ctx.dwait(nc.sync, fl_tok)
        ctx.wait(nc.sync, "V")
        bc_tok = ctx.dgrp(nc.sync, [
            (sb["Bb"].bitcast(BF16)[:, 0:4096].rearrange("p (n l) -> p n l", n=16),
             hb["Bflat"][:, W].unsqueeze(0).partition_broadcast(128)),
            (sb["Cb"].bitcast(BF16)[:, 0:4096].rearrange("p (n l) -> p n l", n=16),
             hb["Cflat"][:, W].unsqueeze(0).partition_broadcast(128))], "ld")
        if CUT < 6:
            continue
        yg, yout = sb["yg"], sb["yout"]
        dAv = sb["scanA"].bitcast(BF16)
        dBv = sb["scanB"].bitcast(BF16)
        dAb = [dAv[:, 0:4096], dAv[:, 4096:8192]]
        dBb = [dBv[:, 0:4096], dBv[:, 4096:8192]]
        cscr = sb["carry"].bitcast(BF16)
        carry_b = cscr[:, 0:256]
        du_b = cscr[:, 256:512]
        tfix_b = sb["tfix"].bitcast(BF16)[:, 0:16]
        Bb3 = sb["Bb"].bitcast(BF16)[:, 0:4096].rearrange("p (n l) -> p n l", n=16)
        Cb3 = sb["Cb"].bitcast(BF16)[:, 0:4096].rearrange("p (n l) -> p n l", n=16)
        ctx.wait(nc.vector, "S")
        ctx.dwait(nc.vector, bc_tok)
        ctx.dwait(nc.gpsimd, bc_tok)
        for g in range(G16):
            gs = slice(g * LC0, (g + 1) * LC0)
            buf = g % 2
            dA = dAb[buf]
            dA3 = dA.rearrange("p (n l) -> p n l", n=16)
            dB3 = dBb[buf].rearrange("p (n l) -> p n l", n=16)
            cg = carry_b[:, g * 16:(g + 1) * 16]
            if dv_rel[buf] is not None:
                ctx.wait(nc.scalar, "V", dv_rel[buf])
                ctx.wait(nc.scalar, "P", gp_rel[buf])
            for n in range(NST):
                s = ctx.inc(nc.scalar.activation(dA3[:, n, :], delta[:, gs], AF.Exp,
                                                 scale=sb["Aneg"][:, g * 16 + n:g * 16 + n + 1]), "S")
            nc.vector.tensor_tensor(du_b[:, :], delta[:, gs], xc[:, gs], ALU.mult)
            dbu = ctx.inc(nc.vector.tensor_tensor(
                dB3[:, :, :], du_b[:, :].unsqueeze(1).broadcast_to([128, 16, LC0]),
                Bb3[:, :, :], ALU.mult), "V")
            ctx.wait(nc.gpsimd, "S", s)
            ctx.wait(nc.gpsimd, "V", dbu)
            nc.gpsimd.tensor_tensor(tfix_b[:, :], dA3[:, :, 0], cg, ALU.mult)
            nc.gpsimd.tensor_tensor(dB3[:, :, 0], dB3[:, :, 0], tfix_b[:, :], ALU.add)
            fix_mark[buf] = ctx.inc(nc.gpsimd.memset(dA3[:, :, 0], 0.0), "P")
            if pend is not None:
                pend()
                pend = None

            def mk_post(g=g, buf=buf, dA=dA, dA3=dA3, dB3=dB3, cg=cg, gs=gs):
                def post():
                    ctx.wait(nc.vector, "P", fix_mark[buf])
                    vscan = ctx.inc(nc.vector.tensor_tensor_scan(dA[:, :], dA[:, :], dBb[buf][:, :],
                                                                 0.0, ALU.mult, ALU.add), "V")
                    ctx.wait(nc.gpsimd, "V", vscan)
                    gp_rel[buf] = ctx.inc(nc.gpsimd.tensor_copy(cg, dA3[:, :, LC0 - 1]), "P")
                    dv_rel[buf] = ctx.inc(nc.vector.tensor_tensor(dB3[:, :, :], dA3[:, :, :],
                                                                  Cb3[:, :, :], ALU.mult), "V")
                    nc.vector.tensor_tensor(dB3[:, 0:8, :], dB3[:, 0:8, :], dB3[:, 8:16, :], ALU.add)
                    nc.vector.tensor_tensor(dB3[:, 0:4, :], dB3[:, 0:4, :], dB3[:, 4:8, :], ALU.add)
                    nc.vector.tensor_tensor(dB3[:, 0:2, :], dB3[:, 0:2, :], dB3[:, 2:4, :], ALU.add)
                    nc.vector.tensor_tensor(yg[:, :], dB3[:, 0, :], dB3[:, 1, :], ALU.add)
                    nc.vector.scalar_tensor_tensor(yg[:, :], xc[:, gs], sb["dskip"][:, g:g + 1],
                                                   yg[:, :], ALU.mult, ALU.add)
                    ctx.inc(nc.vector.tensor_tensor(yout[:, gs], yg[:, :], zsb[:, gs],
                                                    ALU.mult), "V")
                return post
            pend = mk_post()
        pend()
        pend = None
        vy = ctx.cnt["V"]
        ctx.wait(nc.gpsimd, "V", vy)
        ystore = ctx.dgrp(nc.gpsimd, [(hb["youtscr"][:, W].rearrange("(g p) l -> p g l", p=128),
                                       yout[:, :].rearrange("p (g l) -> p g l", g=16))], "gw")
    ctx.dwait(nc.sync, ystore)

    def ytsl(k, cs):
        t = [vA, vB, sb["xc"].bitcast(BF16), sb["zsilu"].bitcast(BF16)][k // 4]
        return t[:, (k % 4) * 2048 + cs.start:(k % 4) * 2048 + cs.stop]

    ctx.wait(nc.sync, "V")
    ytoks = []
    for k in range(G16):
        ytoks.append(ctx.dgrp(nc.sync, [(ytsl(k, slice(0, L)),
                                         hb["youtscr"][k * 128:(k + 1) * 128, :])], "ld"))
    for t_ in ytoks:
        ctx.dwait(nc.tensor, t_)
    wbuf2, psW2 = sb["wbuf2"], sb["ps_w"]
    w_tw = [None, None]
    bank2 = [None, None]
    res_tok = [None, None]
    xres_rel = [None, None]
    for m in range(G8):
        wb = m % 2
        if w_tw[wb] is not None:
            ctx.wait(nc.sync, "TW", w_tw[wb])
        wtok = ctx.dgrp(nc.sync, [(wbuf2[:, wb * 2048:(wb + 1) * 2048].rearrange("p (kg m) -> p kg m", kg=16),
                                   hb["woutbf"][:, m * 128:(m + 1) * 128].rearrange("(kg p) m -> p kg m", p=128))],
                        "ld")
        ctx.dwait(nc.tensor, wtok)
        tw = None
        for sp in range(NS):
            Wsp = slice(sp * LC, (sp + 1) * LC)
            pbuf = (m * NS + sp) % 2
            if xres_rel[pbuf] is not None:
                ctx.wait(nc.sync, xres_rel[pbuf][0], xres_rel[pbuf][1])
            xr = ctx.dgrp(nc.sync, [(sb["statsq"][:, pbuf * LC:(pbuf + 1) * LC],
                                     hb["x0T"][m * 128:(m + 1) * 128, Wsp])], "ld")
            if bank2[pbuf] is not None:
                ctx.wait(nc.tensor, bank2[pbuf][0], bank2[pbuf][1])
            for k in range(G16):
                mm = nc.tensor.matmul(psW2[pbuf][:, 0:LC],
                                      wbuf2[:, wb * 2048 + k * 128:wb * 2048 + (k + 1) * 128],
                                      ytsl(k, Wsp), start=(k == 0), stop=(k == G16 - 1))
            tw = ctx.inc(mm, "TW")
            ctx.wait(nc.vector, "TW", tw)
            ctx.dwait(nc.vector, xr)
            if res_tok[pbuf] is not None:
                ctx.dwait(nc.vector, res_tok[pbuf])
            ve = ctx.inc(nc.vector.tensor_tensor(sb["statx"][:, pbuf * LC:(pbuf + 1) * LC],
                                                 sb["statsq"][:, pbuf * LC:(pbuf + 1) * LC],
                                                 psW2[pbuf][:, 0:LC], ALU.add), "VE")
            bank2[pbuf] = ("VE", ve)
            xres_rel[pbuf] = ("VE", ve)
            ctx.wait(nc.gpsimd, "VE", ve)
            res_tok[pbuf] = ctx.dgrp(nc.gpsimd, [(hb["xA"][m * 128:(m + 1) * 128, Wsp],
                                                  sb["statx"][:, pbuf * LC:(pbuf + 1) * LC])], "gw")
        w_tw[wb] = tw
    for t_ in res_tok:
        if t_ is not None:
            ctx.dwait(nc.sync, t_)
            ctx.dwait(nc.gpsimd, t_)


def emit_l0f(ctx, sb, hb):
    nc = ctx.nc
    NCh = L // LC
    emit_ln_stats(ctx, sb, hb["xA"], hb["st_r2"], hb["st_m2"], hb["sflat"])
    rowA = sb["scanA"][0:1, :]
    rowB = sb["scanB"][0:1, :]
    sc = sb["sc"]
    ctx.vsync(nc.vector.memset(sc[0:1, :], 0.0))
    wst = [sb["Bb"], sb["Cb"]]
    wtok = ctx.dgrp(nc.sync, [(wst[kt // 4][:, (kt % 4) * 1024:(kt % 4 + 1) * 1024],
                               hb["w0f"][kt * 128:(kt + 1) * 128, :]) for kt in range(8)], "ld")
    ctx.dwait(nc.vector, wtok)
    am = sb["tfix"]
    for kt in range(8):
        ctx.vsync(nc.vector.tensor_reduce(am[:, kt:kt + 1], wst[kt // 4][:, (kt % 4) * 1024:(kt % 4 + 1) * 1024],
                                axis=AX.X, op=ALU.max, apply_absolute_value=True))
    v = ctx.vsync(nc.vector.tensor_reduce(am[:, 8:9], am[:, 0:8], axis=AX.X, op=ALU.max))
    ctx.wait(nc.gpsimd, "V", v)
    t1 = ctx.dgrp(nc.gpsimd, [(hb["pmax"][0:128], am[:, 8:9])], "gw")
    ctx.dwait(nc.sync, t1)
    t2 = ctx.dgrp(nc.sync, [(sb["cc_row"][0:1, 0:128], hb["pmax"][0:128].unsqueeze(0))], "ld")
    ctx.dwait(nc.vector, t2)
    ctx.vsync(nc.vector.tensor_reduce(sc[0:1, 0:1], sb["cc_row"][0:1, 0:128], axis=AX.X, op=ALU.max))
    ctx.vsync(nc.vector.tensor_scalar(sc[0:1, 1:2], sc[0:1, 0:1], 1e-12, None, ALU.max))
    ctx.vsync(nc.vector.reciprocal(sc[0:1, 2:3], sc[0:1, 1:2]))
    ctx.vsync(nc.vector.tensor_scalar(sc[0:1, 3:4], sc[0:1, 2:3], 448.0, None, ALU.mult))
    ctx.vsync(nc.vector.tensor_scalar(sc[0:1, 4:5], sc[0:1, 3:4], 448.0, None, ALU.min))
    ctx.vsync(nc.vector.tensor_scalar(sc[0:1, 5:6], sc[0:1, 4:5], 0.5, None, ALU.mult))
    v = ctx.vsync(nc.vector.reciprocal(sc[0:1, 6:7], sc[0:1, 4:5]))
    ctx.wait(nc.sync, "V", v)
    t3 = ctx.dgrp(nc.sync, [(hb["scal"][0:16].unsqueeze(0), sc[0:1, 0:16])], "st")
    ctx.dwait(nc.sync, t3)
    t4 = ctx.dgrp(nc.sync, [(sb["scb"][:, :], hb["scal"][0:16].unsqueeze(0).partition_broadcast(128))], "ld")
    ctx.dwait(nc.vector, t4)
    for kt in range(8):
        wsl = wst[kt // 4][:, (kt % 4) * 1024:(kt % 4 + 1) * 1024]
        nc.vector.tensor_scalar(sb["q8t"][:, 0:1024], wsl, sb["scb"][:, 5:6], None, ALU.mult)
        v = ctx.inc(nc.vector.tensor_copy(sb["wqb"][:, kt * 1024:(kt + 1) * 1024], sb["q8t"][:, 0:1024]), "V")
    xcur, t1h = sb["xin"][:, 0:G8 * LC], sb["tmpc2"]
    rb, mb = sb["rb2"], sb["mb2"]
    ctx.vsync(nc.vector.memset(sb["amaxx"][:, :], 0.0))
    nacc = 0
    for c in range(NCh):
        W = slice(c * LC, (c + 1) * LC)
        ctx.wait(nc.sync, "V")
        xtok = ctx.dgrp(nc.sync, [(xcur[:, g * LC:(g + 1) * LC],
                                   hb["xA"][g * 128:(g + 1) * 128, W]) for g in range(G8)], "ld")
        btok = ctx.dgrp(nc.sync, [(rb[:, :], hb["st_r2"][W].unsqueeze(0).partition_broadcast(128)),
                                  (mb[:, :], hb["st_m2"][W].unsqueeze(0).partition_broadcast(128))], "ld")
        ctx.dwait(nc.vector, xtok)
        ctx.dwait(nc.vector, btok)
        for g in range(G8):
            gs = slice(g * LC, (g + 1) * LC)
            nc.vector.tensor_tensor(t1h[:, :], xcur[:, gs], rb[:, :], ALU.mult)
            nc.vector.tensor_tensor(t1h[:, :], t1h[:, :], mb[:, :], ALU.subtract)
            v = ctx.vsync(nc.vector.tensor_reduce(sb["amaxx"][:, 1 + nacc % 15:2 + nacc % 15], t1h[:, :],
                                                axis=AX.X, op=ALU.max, apply_absolute_value=True))
            nacc += 1
        if nacc >= 15:
            v = ctx.vsync(nc.vector.tensor_reduce(sb["amaxx"][:, 0:1], sb["amaxx"][:, 0:16],
                                                axis=AX.X, op=ALU.max))
            nacc = 0
    v = ctx.vsync(nc.vector.tensor_reduce(sb["amaxx"][:, 0:1], sb["amaxx"][:, 0:1 + nacc],
                                        axis=AX.X, op=ALU.max))
    ctx.wait(nc.gpsimd, "V", v)
    t1 = ctx.dgrp(nc.gpsimd, [(hb["pmax"][0:128], sb["amaxx"][:, 0:1])], "gw")
    ctx.dwait(nc.gpsimd, t1)
    t2 = ctx.dgrp(nc.gpsimd, [(hb["cc_in"][:, :], hb["pmax"][0:128].unsqueeze(0))], "gw")
    ctx.dwait(nc.gpsimd, t2)
    ctx.inc(nc.gpsimd.collective_compute("AllReduce", ALU.max, replica_groups=[list(range(NCORES))],
                                         ins=[hb["cc_in"][:, :]], outs=[hb["cc_out"][:, :]]), "CC")
    ctx.wait(nc.sync, "CC")
    t2 = ctx.dgrp(nc.sync, [(sb["cc_row"][0:1, 0:128], hb["cc_out"][:, :])], "ld")
    ctx.dwait(nc.vector, t2)
    ctx.vsync(nc.vector.tensor_reduce(sc[0:1, 8:9], sb["cc_row"][0:1, 0:128], axis=AX.X, op=ALU.max))
    ctx.vsync(nc.vector.tensor_scalar(sc[0:1, 9:10], sc[0:1, 8:9], 1e-12, None, ALU.max))
    ctx.vsync(nc.vector.reciprocal(sc[0:1, 10:11], sc[0:1, 9:10]))
    ctx.vsync(nc.vector.tensor_scalar(sc[0:1, 11:12], sc[0:1, 10:11], 448.0, None, ALU.mult))
    ctx.vsync(nc.vector.tensor_scalar(sc[0:1, 12:13], sc[0:1, 11:12], 448.0, None, ALU.min))
    ctx.vsync(nc.vector.tensor_scalar(sc[0:1, 13:14], sc[0:1, 12:13], 0.5, None, ALU.mult))
    ctx.vsync(nc.vector.reciprocal(sc[0:1, 14:15], sc[0:1, 12:13]))
    ctx.vsync(nc.vector.tensor_tensor(sc[0:1, 15:16], sc[0:1, 14:15], sc[0:1, 6:7], ALU.mult))
    ctx.vsync(nc.vector.tensor_scalar(sc[0:1, 15:16], sc[0:1, 15:16], 4.0, None, ALU.mult))
    nc.vector.tensor_scalar(rowA[0:1, L:2 * L], rowA[0:1, 0:L], sc[0:1, 13:14], None, ALU.mult)
    v = ctx.inc(nc.vector.tensor_scalar(rowB[0:1, 0:L], rowB[0:1, L:2 * L],
                                        sc[0:1, 13:14], None, ALU.mult), "V")
    ctx.wait(nc.sync, "V", v)
    t3 = ctx.dgrp(nc.sync, [(hb["st_r2b"][:].unsqueeze(0), rowA[0:1, L:2 * L]),
                            (hb["st_m2b"][:].unsqueeze(0), rowB[0:1, 0:L]),
                            (hb["scal"][0:16].unsqueeze(0), sc[0:1, 0:16])], "st")
    ctx.dwait(nc.sync, t3)
    t4 = ctx.dgrp(nc.sync, [(sb["scb"][:, :], hb["scal"][0:16].unsqueeze(0).partition_broadcast(128))], "ld")
    ctx.dwait(nc.vector, t4)
    ctx.dwait(nc.scalar, t4)
    if DEBUG:
        ctx.wait(nc.gpsimd, "V")
        dt_ = ctx.dgrp(nc.gpsimd, [(hb["d_wq"][kt * 128:(kt + 1) * 128, :],
                                    sb["wqb"][:, kt * 1024:(kt + 1) * 1024]) for kt in range(8)], "gw")
        ctx.dwait(nc.gpsimd, dt_)
    xcur = sb["xcur2"]
    psW = sb["ps_w"]
    ctx.wait(nc.tensor, "SE")
    ctx.wait(nc.tensor, "VE")
    bank_rel = [None, None]
    res_tok = [None, None]
    for c in range(NCh):
        W = slice(c * LC, (c + 1) * LC)
        ctx.wait(nc.sync, "V")
        ctx.wait(nc.sync, "VE")
        xtok = ctx.dgrp(nc.sync, [(xcur[:, g * LC:(g + 1) * LC],
                                   hb["xA"][g * 128:(g + 1) * 128, W]) for g in range(G8)], "ld")
        btok = ctx.dgrp(nc.sync, [(rb[:, :], hb["st_r2b"][W].unsqueeze(0).partition_broadcast(128)),
                                  (mb[:, :], hb["st_m2b"][W].unsqueeze(0).partition_broadcast(128))], "ld")
        ctx.dwait(nc.vector, xtok)
        ctx.dwait(nc.vector, btok)
        for g in range(G8):
            gs = slice(g * LC, (g + 1) * LC)
            nc.vector.tensor_tensor(t1h[:, :], xcur[:, gs], rb[:, :], ALU.mult)
            nc.vector.tensor_tensor(sb["q8h"][:, gs], t1h[:, :], mb[:, :], ALU.subtract)
            v = ctx.inc(nc.vector.tensor_copy(sb["hqb"][:, gs], sb["q8h"][:, gs]), "V")
        if DEBUG and c == 0:
            ctx.wait(nc.gpsimd, "V")
            dt_ = ctx.dgrp(nc.gpsimd, [(hb["d_hq"][g * 128:(g + 1) * 128, :],
                                        sb["hqb"][:, g * LC:(g + 1) * LC]) for g in range(G8)], "gw")
            ctx.dwait(nc.gpsimd, dt_)
        ctx.wait(nc.tensor, "V")
        for m in range(G8):
            buf = m % 2
            if bank_rel[buf] is not None:
                ctx.wait(nc.tensor, bank_rel[buf][0], bank_rel[buf][1])
            for k in range(G8):
                mm = nc.tensor.matmul(psW[buf][:, 0:LC],
                                      sb["wqb"][:, k * 1024 + m * 128:k * 1024 + (m + 1) * 128],
                                      sb["hqb"][:, k * LC:(k + 1) * LC], start=(k == 0), stop=(k == G8 - 1))
            ctx.inc(mm, "TW")
            ctx.wait(nc.scalar, "TW")
            se = ctx.inc(nc.scalar.activation(sb["obf"][:, buf * LC:(buf + 1) * LC], psW[buf][:, 0:LC],
                                              AF.Identity, bias=sb["b0f"][:, m:m + 1], scale=sb["scb"][:, 15:16]), "SE")
            bank_rel[buf] = ("SE", se)
            ctx.wait(nc.vector, "SE", se)
            if res_tok[buf] is not None:
                ctx.dwait(nc.vector, res_tok[buf])
            ve = ctx.inc(nc.vector.tensor_tensor(sb["stage3"][:, buf * LC:(buf + 1) * LC],
                                                 xcur[:, m * LC:(m + 1) * LC],
                                                 sb["obf"][:, buf * LC:(buf + 1) * LC], ALU.add), "VE")
            ctx.wait(nc.sync, "VE", ve)
            res_tok[buf] = ctx.dgrp(nc.sync, [(hb["xB"][m * 128:(m + 1) * 128, W],
                                               sb["stage3"][:, buf * LC:(buf + 1) * LC])], "st")
    for t_ in res_tok:
        if t_ is not None:
            ctx.dwait(nc.sync, t_)


def emit_l1a(ctx, sb, hb):
    nc = ctx.nc
    v_l0f_done = ctx.cnt["V"]
    NS = L // LC
    NQ = L // 128
    emit_ln_stats(ctx, sb, hb["xB"], hb["st_r3"], hb["st_m3"], hb["sflat"])
    h2a = sb["scanA"].bitcast(BF16)
    h2b = sb["scanB"].bitcast(BF16)

    def h2sl(k, cs):
        t = h2a if k < 4 else h2b
        return t[:, (k % 4) * 2048 + cs.start:(k % 4) * 2048 + cs.stop]

    xs2, rb, mb = sb["statx"], sb["rb2"], sb["mb2"]
    for s in range(NS):
        W = slice(s * LC, (s + 1) * LC)
        ctx.wait(nc.sync, "V")
        btok = ctx.dgrp(nc.sync, [(rb[:, :], hb["st_r3"][W].unsqueeze(0).partition_broadcast(128)),
                                  (mb[:, :], hb["st_m3"][W].unsqueeze(0).partition_broadcast(128))], "ld")
        ctx.dwait(nc.vector, btok)
        for g in range(G8):
            hh = g % 2
            xsl = xs2[:, hh * LC:(hh + 1) * LC]
            ctx.wait(nc.sync, "V")
            xtok = ctx.dgrp(nc.sync, [(xsl, hb["xB"][g * 128:(g + 1) * 128, W])], "ld")
            ctx.dwait(nc.vector, xtok)
            nc.vector.tensor_tensor(sb["tmpc2"][:, :], xsl, rb[:, :], ALU.mult)
            wslice = h2sl(g, W)

            ctx.inc(nc.vector.tensor_tensor(wslice, sb["tmpc2"][:, :], mb[:, :], ALU.subtract), "V")
            if (DEBUG_H2 & 16) and g == 7:
                ctx.wait(nc.sync, "V")
                d_ = ctx.dgrp(nc.sync, [(hb["d_dbg"][0], xsl), (hb["d_dbg"][1], sb["tmpc2"][:, 0:LC]),
                                        (hb["d_dbg"][2], rb[:, 0:LC]), (hb["d_dbg"][3], mb[:, 0:LC])], "ld")
                ctx.dwait(nc.sync, d_)
                ctx.wait(nc.gpsimd, "V")
                dg_ = ctx.dgrp(nc.gpsimd, [(hb["d_dbg"][4], wslice)], "gw")
                ctx.dwait(nc.gpsimd, dg_)
    v_h2_done = ctx.cnt["V"]
    if DEBUG_H2 & 1:
        ctx.wait(nc.gpsimd, "V")
        dtk = ctx.dgrp(nc.gpsimd, [(hb["d_h2f"][g], h2sl(g, slice(0, L))) for g in range(G8)], "gw")
        ctx.dwait(nc.gpsimd, dtk)
    wv = sb["Bb"].bitcast(BF16)
    vA = sb["xin"].bitcast(BF16)
    vB = sb["delta"].bitcast(BF16)
    ctx.wait(nc.gpsimd, "V")
    wvtok = ctx.dgrp(nc.gpsimd, [(wv[:, :].rearrange("p (k f) -> p k f", k=8),
                                  hb["wv"][:, :].rearrange("(k p) f -> p k f", p=128))], "gw")
    for t in range(16):
        ctx.inc(nc.vector.memset(vA[:, t * 520:t * 520 + 520].rearrange("p (h e) -> p h e", e=65)[:, :, 64:65], 1.0), "V")
        ctx.inc(nc.vector.memset(vB[:, t * 520:t * 520 + 520].rearrange("p (h e) -> p h e", e=65)[:, :, 64:65], 1.0), "V")
    ctx.dwait(nc.tensor, wvtok)
    ctx.wait(nc.tensor, "V")
    ctx.wait(nc.tensor, "SE")
    ctx.wait(nc.tensor, "VE")
    tw_pre_v = ctx.cnt["TW"]
    psV, ev_mark = [sb["ps_w"][0], sb["ps_w"][1]], []
    for lt in range(NQ):
        for nv in range(2):
            buf = (lt * 2 + nv) % 2
            if len(ev_mark) >= 2:
                ctx.wait(nc.tensor, "SE", ev_mark[-2])
            for k in range(G8):
                mm = nc.tensor.matmul(psV[buf][:, 0:LC],
                                      h2sl(k, slice(lt * 128, (lt + 1) * 128)),
                                      wv[:, k * 1024 + nv * 512:k * 1024 + (nv + 1) * 512],
                                      start=(k == 0), stop=(k == G8 - 1))
            t = ctx.inc(mm, "TW")
            ctx.wait(nc.scalar, "TW", t)
            dst = (vA if nv == 0 else vB)[:, lt * 520:(lt + 1) * 520].rearrange("p (h e) -> p h e", e=65)[:, :, 0:64]
            ev_mark.append(ctx.inc(nc.scalar.copy(dst, psV[buf][:, 0:LC].rearrange("p (h e) -> p h e", e=64)), "SE"))
    if DEBUG_H2 & 2:
        ctx.wait(nc.gpsimd, "SE")
        dtk = ctx.dgrp(nc.gpsimd, [(hb["d_v"][0], vA[:, 0:NQ * 520]), (hb["d_v"][1], vB[:, 0:NQ * 520])], "gw")
        ctx.dwait(nc.gpsimd, dtk)
    qf = sb["Bb"].bitcast(BF16)
    kf = sb["Cb"].bitcast(BF16)
    wqp = sb["yout"].bitcast(BF16) if False else sb["yout"]
    wkp = sb["wbuf2"]
    ctx.wait(nc.tensor, "SE")
    ctx.wait(nc.gpsimd, "TW", tw_pre_v)
    wqtok = ctx.dgrp(nc.gpsimd, [(wqp[:, :].rearrange("p (k f) -> p k f", k=8),
                                  hb["wqp"][:, :].rearrange("(k p) f -> p k f", p=128)),
                                 (wkp[:, :].rearrange("p (k f) -> p k f", k=8),
                                  hb["wkp"][:, :].rearrange("(k p) f -> p k f", p=128))], "gw")
    ctx.dwait(nc.tensor, wqtok)
    qk_mark = list(ev_mark[-2:])
    for which, wt, dst in ((0, wqp, qf), (1, wkp, kf)):
        for t in range(4):
            for s in range(NS):
                buf = len(qk_mark) % 2
                if len(qk_mark) >= 2:
                    ctx.wait(nc.tensor, "SE", qk_mark[-2])
                for k in range(G8):
                    mm = nc.tensor.matmul(psV[buf][:, 0:LC], wt[:, k * 512 + t * 128:k * 512 + (t + 1) * 128],
                                          h2sl(k, slice(s * LC, (s + 1) * LC)), start=(k == 0), stop=(k == G8 - 1))
                tm = ctx.inc(mm, "TW")
                ctx.wait(nc.scalar, "TW", tm)
                qk_mark.append(ctx.inc(nc.scalar.copy(dst[:, t * 2048 + s * LC:t * 2048 + (s + 1) * LC],
                                                      psV[buf][:, 0:LC]), "SE"))
    if DEBUG_H2 & 4:
        ctx.wait(nc.gpsimd, "SE")
        dtk = ctx.dgrp(nc.gpsimd, [(hb["d_qf"][t], qf[:, t * 2048:t * 2048 + L]) for t in range(4)] +
                                  [(hb["d_kf"][t], kf[:, t * 2048:t * 2048 + L]) for t in range(4)], "gw")
        ctx.dwait(nc.gpsimd, dtk)
    psS = [sb["ps_w"][0], sb["ps_w"][1], sb["ps_x"]]
    psN = [sb["ps_st0"], sb["ps_st1"]]
    afb = [sb["stage2"][:, 0:LC], sb["statx"][:, 0:LC], sb["statsq"][:, 0:LC]]
    atT = sb["atT"]
    atS = [atT[:, 0:512], atT[:, 512:1024], sb["Aneg"].bitcast(BF16)[:, 0:512]]
    import os
    _m = os.environ.get('ND3', 'sxa')
    NDs = 3 if 's' in _m else 2
    NDf = 3 if 'x' in _m else 2
    NDa = 3 if 'a' in _m else 2
    ND = min(NDs, NDf, NDa)
    c_sq, c_bi = 1.0 / np.sqrt(32.0), 1.0 / np.sqrt(2.0)
    sq_i = 0
    nm_mark = {}
    nm_at = {}
    av_at = {}
    pend_nums = []
    ctx.wait(nc.scalar, "V")
    ctx.wait(nc.tensor, "V")
    ctx.wait(nc.tensor, "SE")
    ctx.wait(nc.tensor, "S")
    QB = NQ // 4
    for qb in range(QB):
        qsl = slice(qb * 512, (qb + 1) * 512)
        for hd in range(16):
            off = 32 * (hd % 4)
            tq = hd // 4
            pnb = (qb * 16 + hd) % 2
            pn = psN[pnb]
            if nm_mark.get(pnb) is not None:
                ctx.wait(nc.tensor, "YV", nm_mark[pnb])
            kfh = kf[:, tq * 2048:(tq + 1) * 2048]
            qfh = qf[:, tq * 2048:(tq + 1) * 2048]
            for kc in range(4 * qb + 4):
                bufS = sq_i % NDs
                bufF = sq_i % NDf
                bufA = sq_i % NDa
                if sq_i >= ND:
                    ctx.wait(nc.tensor, "SQE", sq_i - ND)
                mm = nc.tensor.matmul(psS[bufS][:, 0:512],
                                      kfh[off:off + 16, kc * 128:(kc + 1) * 128],
                                      qfh[off:off + 16, qsl],
                                      start=True, stop=True, tile_position=(off, 0))
                tm = ctx.inc(mm, "T")
                ctx.wait(nc.scalar, "T", tm)
                if sq_i >= ND:
                    ctx.wait(nc.scalar, "AV", av_at[sq_i - ND])
                s1 = ctx.inc(nc.scalar.activation(afb[bufF], psS[bufS][:, 0:512],
                                                  AF.Square, bias=ctx.cst(c_bi), scale=c_sq), "SQE")
                ctx.wait(nc.vector, "SQE", s1)
                if sq_i >= NDa and (sq_i - NDa) in nm_at:
                    ctx.wait(nc.vector, "NM", nm_at[sq_i - NDa])
                adst = atS[bufA]
                jb = kc - 4 * qb
                av = ctx.inc(nc.vector.tensor_scalar(adst, afb[bufF], 0.5, None, ALU.add), "AV")
                if jb >= 0:
                    av = ctx.inc(nc.vector.scalar_tensor_tensor(
                        adst[:, jb * 128:(jb + 1) * 128], afb[bufF][:, jb * 128:(jb + 1) * 128],
                        ctx.cst(0.5), sb["masktri"][:, :], ALU.add, ALU.mult), "AV")
                av_at[sq_i] = av
                vsl = (vA if hd < 8 else vB)[:, kc * 520 + (hd % 8) * 65:kc * 520 + (hd % 8) * 65 + 65]

                def mk_num(kc=kc, bufA=bufA, av=av, vsl=vsl, si=sq_i, qb=qb, pn=pn):
                    def emit_num():
                        ctx.wait(nc.tensor, "AV", av)
                        last_nm = None
                        for j in range(4):
                            if kc <= 4 * qb + j:
                                last_nm = nc.tensor.matmul(pn[:, j * 65:j * 65 + 65],
                                                           atS[bufA][:, j * 128:(j + 1) * 128],
                                                           vsl, start=(kc == 0 and j == 0),
                                                           stop=(kc == 4 * qb + 3 and j == 3))
                        nm_at[si] = ctx.inc(last_nm, "NM")
                    return emit_num
                pend_nums.append(mk_num())
                if len(pend_nums) > 2:
                    pend_nums.pop(0)()
                sq_i += 1
            while pend_nums:
                pend_nums.pop(0)()
            tn = ctx.cnt["NM"]
            ctx.wait(nc.scalar, "NM", tn)
            sd = ctx.inc(nc.scalar.copy(sb["denf"][:, 0:4].rearrange("p (j o) -> p j o", o=1),
                                        pn[:, 0:260].rearrange("p (j e) -> p j e", e=65)[:, :, 64:65]), "S")
            ctx.wait(nc.vector, "S", sd)
            ctx.vsync(nc.vector.tensor_scalar(sb["denf"][:, 4:8], sb["denf"][:, 0:4], 1e-12, None, ALU.add))
            ctx.vsync(nc.vector.reciprocal(sb["denf"][:, 8:12], sb["denf"][:, 4:8]))
            yvv = None
            for j in range(4):
                qc = 4 * qb + j
                ysl = (sb["ysbA"] if hd < 8 else sb["ysbB"])[:, qc * 512 + (hd % 8) * 64:qc * 512 + (hd % 8) * 64 + 64]
                yvv = ctx.inc(nc.vector.tensor_scalar(ysl, pn[:, j * 65:j * 65 + 64],
                                                      sb["denf"][:, 8 + j:9 + j], None, ALU.mult), "YV")
            nc.vector.wait_ge(ctx.sems["YV"], yvv)
            nm_mark[pnb] = yvv
    yf = sb["xcb"]
    psT = [sb["ps_x"], sb["ps_d"][0], sb["ps_d"][1], sb["ps_st0"]]
    wo = sb["wbuf"]
    xres = sb["statsq"]
    ctx.wait(nc.sync, "V", v_h2_done)
    ctx.wait(nc.sync, "SQE")
    ctx.wait(nc.gpsimd, "TW")
    ctx.wait(nc.tensor, "YV")
    ctx.wait(nc.tensor, "SQE")
    tr_mark = []
    res_tok = [None, None]
    wo_rel = [None, None]
    wo_tw = [None, None]
    tr_i = 0
    for s in range(NS):
        W = slice(s * LC, (s + 1) * LC)
        for lt in range(4 * s, 4 * s + 4):
            for mt in range(8):
                ysrc = (sb["ysbA"] if mt < 4 else sb["ysbB"])
                h0_ = (2 * mt) % 8
                src_ap = ysrc[:, lt * 512 + h0_ * 64:lt * 512 + h0_ * 64 + 128]
                buf = tr_i % 4
                if len(tr_mark) >= 4:
                    ctx.wait(nc.tensor, "TRE", tr_mark[-4])
                mm = nc.tensor.transpose(psT[buf].bitcast(BF16)[:, 0:128], src_ap, sb["identb"][:, :])
                tm = ctx.inc(mm, "T")
                ctx.wait(nc.scalar, "T", tm)
                tr_mark.append(ctx.inc(nc.scalar.copy(yf[:, mt * 512 + (lt % 4) * 128:mt * 512 + (lt % 4) * 128 + 128],
                                                      psT[buf].bitcast(BF16)[:, 0:128]), "TRE"))
                tr_i += 1
        ctx.wait(nc.tensor, "TRE")
        for m in range(G8):
            buf = m % 2
            if wo_tw[buf] is not None:
                ctx.wait(nc.sync, "TW", wo_tw[buf])
            wotok = ctx.dgrp(nc.sync, [(wo[:, buf * 1024:(buf + 1) * 1024].rearrange("p (k f) -> p k f", k=8),
                                        hb["wobf"][:, m * 128:(m + 1) * 128].rearrange("(k p) f -> p k f", p=128))], "ld")
            if wo_rel[buf] is not None:
                ctx.wait(nc.sync, wo_rel[buf][0], wo_rel[buf][1])
            xtok = ctx.dgrp(nc.sync, [(xres[:, buf * LC:(buf + 1) * LC], hb["xB"][m * 128:(m + 1) * 128, W])], "ld")
            ctx.dwait(nc.tensor, wotok)
            if wo_rel[buf] is not None:
                ctx.wait(nc.tensor, wo_rel[buf][0], wo_rel[buf][1])
            for k in range(G8):
                mm = nc.tensor.matmul(psV[buf][:, 0:LC], wo[:, buf * 1024 + k * 128:buf * 1024 + (k + 1) * 128],
                                      yf[:, k * 512:(k + 1) * 512], start=(k == 0), stop=(k == G8 - 1))
            tm = ctx.inc(mm, "TW")
            wo_tw[buf] = tm
            ctx.wait(nc.vector, "TW", tm)
            ctx.dwait(nc.vector, xtok)
            if res_tok[buf] is not None:
                ctx.dwait(nc.vector, res_tok[buf])
            ve = ctx.inc(nc.vector.tensor_tensor(sb["stage4"][:, buf * LC:(buf + 1) * LC],
                                                 xres[:, buf * LC:(buf + 1) * LC],
                                                 psV[buf][:, 0:LC], ALU.add), "VE")
            wo_rel[buf] = ("VE", ve)
            ctx.wait(nc.sync, "VE", ve)
            res_tok[buf] = ctx.dgrp(nc.sync, [(hb["xA"][m * 128:(m + 1) * 128, W],
                                               sb["stage4"][:, buf * LC:(buf + 1) * LC])], "st")
        ctx.wait(nc.tensor, "VE")
    for t_ in res_tok:
        if t_ is not None:
            ctx.dwait(nc.sync, t_)



def emit_l1f(ctx, sb, hb):
    nc = ctx.nc
    NS = L // LC
    ctx.wait(nc.tensor, "YV")
    ctx.wait(nc.tensor, "S")
    ctx.wait(nc.tensor, "VE")
    ctx.wait(nc.tensor, "TRE")
    ctx.wait(nc.tensor, "SQE")
    ctx.wait(nc.sync, "VE")
    ctx.wait(nc.vector, "NM")
    emit_ln_stats(ctx, sb, hb["xA"], hb["st_r4"], hb["st_m4"], hb["sflat"])
    ctx.inc(nc.vector.memset(sb["ones_bf"][:, :], 1.0), "V")
    ctx.inc(nc.vector.memset(sb["cc_row"][0:1, :], 1.0), "V")
    smtok = ctx.dgrp(nc.sync, [(sb["scb"][:, :], hb["g1b1"][:, :]),
                               (sb["bb1sb"][:, :], hb["bb1h"][:, :]),
                               (sb["amaxx"][:, 0:8], hb["bb2hd"][:, :])], "ld")
    ctx.dwait(nc.vector, smtok)
    ctx.inc(nc.vector.tensor_scalar(sb["bb2h"][:, :], sb["amaxx"][:, 0:8], 0.5, None, ALU.mult), "V")

    vA = sb["xin"].bitcast(BF16)
    vB = sb["delta"].bitcast(BF16)

    def hpsl(g, cs):
        t = vA if g < 4 else vB
        return t[:, (g % 4) * 2048 + cs.start:(g % 4) * 2048 + cs.stop]

    ysbA, ysbB = sb["ysbA"], sb["ysbB"]

    def hhsl(g, cs):
        t = ysbA if g < 4 else ysbB
        return t[:, (g % 4) * 2048 + cs.start:(g % 4) * 2048 + cs.stop]

    rb4, mb4 = sb["statx"][:, 0:LC], sb["statx"][:, LC:2 * LC]
    sqb_t = sb["xc"]
    ps0, ps1 = sb["ps_st0"], sb["ps_st1"]
    t1mark = []
    st_all = []
    st_tok = None
    ctx.wait(nc.tensor, "V")
    for s in range(NS):
        W = slice(s * LC, (s + 1) * LC)
        ctx.wait(nc.sync, "V")
        btok = ctx.dgrp(nc.sync, [(rb4, hb["st_r4"][W].unsqueeze(0).partition_broadcast(128)),
                                  (mb4, hb["st_m4"][W].unsqueeze(0).partition_broadcast(128))], "ld")
        ctx.dwait(nc.vector, btok)
        for g in range(G8):
            h = g % 2
            xsl = sb["statsq"][:, h * LC:(h + 1) * LC]
            sq = sqb_t[:, h * LC:(h + 1) * LC]
            ctx.wait(nc.sync, "V")
            xtok = ctx.dgrp(nc.sync, [(xsl, hb["xA"][g * 128:(g + 1) * 128, W])], "ld")
            ctx.dwait(nc.vector, xtok)
            nc.vector.tensor_tensor(sb["stage2"][:, :], xsl, rb4, ALU.mult)
            hv = ctx.inc(nc.vector.tensor_tensor(hpsl(g, W), sb["stage2"][:, :], mb4, ALU.subtract), "V")
            ctx.wait(nc.scalar, "V", hv)
            if len(t1mark) >= 2:
                ctx.wait(nc.scalar, "T", t1mark[-2])
            sqm = ctx.inc(nc.scalar.activation(sq, hpsl(g, W), AF.Square), "S")
            ctx.wait(nc.tensor, "V", hv)
            ctx.inc(nc.tensor.matmul(ps0[0:1, 0:LC], sb["ones_bf"][:, 0:1], hpsl(g, W),
                                     start=(g == 0), stop=(g == G8 - 1)), "T")
            ctx.wait(nc.tensor, "S", sqm)
            t1mark.append(ctx.inc(nc.tensor.matmul(ps1[0:1, 0:LC], sb["ones_f32"][:, 0:1], sq,
                                                   start=(g == 0), stop=(g == G8 - 1)), "T"))
        ctx.wait(nc.scalar, "T")
        if st_tok is not None:
            ctx.dwait(nc.scalar, st_tok)
        ctx.inc(nc.scalar.copy(sb["sums2"][0:1, 0:LC], ps0[0:1, 0:LC]), "S")
        se = ctx.inc(nc.scalar.copy(sb["sums2"][0:1, LC:2 * LC], ps1[0:1, 0:LC]), "S")
        ctx.wait(nc.tensor, "S", se)
        ctx.wait(nc.sync, "S", se)
        st_tok = ctx.dgrp(nc.sync, [
            (hb["sflat"][s * LC:(s + 1) * LC].unsqueeze(0), sb["sums2"][0:1, 0:LC]),
            (hb["sflat"][L + s * LC:L + (s + 1) * LC].unsqueeze(0), sb["sums2"][0:1, LC:2 * LC])], "st")
        st_all.append(st_tok)
    for t_ in st_all:
        ctx.dwait(nc.sync, t_)
    _ln_tail(ctx, sb, hb["st_r5"], hb["st_m5"], hb["sflat"])

    w1t = [sb["scanA"].bitcast(BF16), sb["scanB"].bitcast(BF16),
           sb["Bb"].bitcast(BF16), sb["Cb"].bitcast(BF16)]

    def w1sl(kb, cs):
        return w1t[kb // 2][:, (kb % 2) * 4096 + cs.start:(kb % 2) * 4096 + cs.stop]

    xq = sb["xcur"].bitcast(BF16)
    psB = sb["ps_w"]
    c128 = ctx.cst(128.0)
    bc_rel = [None, None]
    s_rel = None
    for kb in range(G8):
        ctx.wait(nc.gpsimd, "VE")
        qtok = ctx.dgrp(nc.gpsimd, [(xq[:, :], hb["q1h"][kb * 128:(kb + 1) * 128, :])], "gw")
        for piece in range(4):
            if s_rel is not None:
                ctx.wait(nc.sync, "T", s_rel)
            stok = ctx.dgrp(nc.sync, [(sb["sums2"][0:1, :],
                                       hb["s1h"][kb, piece * 1024:(piece + 1) * 1024].unsqueeze(0))], "ld")
            ctx.dwait(nc.tensor, stok)
            for half in range(2):
                buf = half
                if bc_rel[buf] is not None:
                    ctx.wait(nc.tensor, bc_rel[buf][0], bc_rel[buf][1])
                tm = ctx.inc(nc.tensor.matmul(psB[buf][:, 0:512], sb["cc_row"][0:1, :],
                                              sb["sums2"][0:1, half * 512:(half + 1) * 512],
                                              start=True, stop=True), "T")
                s_rel = tm
                ctx.wait(nc.vector, "T", tm)
                ctx.dwait(nc.vector, qtok)
                cs = slice(piece * 1024 + half * 512, piece * 1024 + half * 512 + 512)
                ve = ctx.inc(nc.vector.scalar_tensor_tensor(w1sl(kb, cs), xq[:, cs], c128,
                                                            psB[buf][:, 0:512],
                                                            ALU.subtract, ALU.mult), "VE")
                bc_rel[buf] = ("VE", ve)

    w2st = []
    h0b = sb["h0"]
    h0_rel = [None, None]
    for kb2 in range(32):
        q = kb2 % 4
        ctx.wait(nc.gpsimd, "VE")
        qtok = ctx.dgrp(nc.gpsimd, [(xq[:, q * 1024:(q + 1) * 1024],
                                     hb["q2h"][kb2 * 128:(kb2 + 1) * 128, :])], "gw")
        if s_rel is not None:
            ctx.wait(nc.sync, "T", s_rel)
        stok = ctx.dgrp(nc.sync, [(sb["sums2"][0:1, :], hb["s2h"][kb2, :].unsqueeze(0))], "ld")
        ctx.dwait(nc.tensor, stok)
        hbuf = kb2 % 2
        if h0_rel[hbuf] is not None:
            ctx.dwait(nc.vector, h0_rel[hbuf])
        ve = None
        for half in range(2):
            buf = half
            if bc_rel[buf] is not None:
                ctx.wait(nc.tensor, bc_rel[buf][0], bc_rel[buf][1])
            tm = ctx.inc(nc.tensor.matmul(psB[buf][:, 0:512], sb["cc_row"][0:1, :],
                                          sb["sums2"][0:1, half * 512:(half + 1) * 512],
                                          start=True, stop=True), "T")
            s_rel = tm
            ctx.wait(nc.vector, "T", tm)
            ctx.dwait(nc.vector, qtok)
            ve = ctx.inc(nc.vector.scalar_tensor_tensor(
                h0b[:, hbuf * 1024 + half * 512:hbuf * 1024 + half * 512 + 512],
                xq[:, q * 1024 + half * 512:q * 1024 + half * 512 + 512], c128,
                psB[buf][:, 0:512], ALU.subtract, ALU.mult), "VE")
            bc_rel[buf] = ("VE", ve)
        ctx.wait(nc.sync, "VE", ve)
        tok = ctx.dgrp(nc.sync, [(hb["w2scr"][kb2 * 128:(kb2 + 1) * 128, :],
                                  h0b[:, hbuf * 1024:(hbuf + 1) * 1024])], "st")
        h0_rel[hbuf] = tok
        w2st.append(tok)

    for s in range(NS):
        W = slice(s * LC, (s + 1) * LC)
        ctx.wait(nc.sync, "V")
        btok = ctx.dgrp(nc.sync, [(rb4, hb["st_r5"][W].unsqueeze(0).partition_broadcast(128)),
                                  (mb4, hb["st_m5"][W].unsqueeze(0).partition_broadcast(128))], "ld")
        ctx.dwait(nc.vector, btok)
        for g in range(G8):
            nc.vector.tensor_tensor(sb["stage2"][:, :], hpsl(g, W), rb4, ALU.mult)
            nc.vector.tensor_tensor(sb["stage2"][:, :], sb["stage2"][:, :], mb4, ALU.subtract)
            ctx.inc(nc.vector.tensor_scalar(hhsl(g, W), sb["stage2"][:, :],
                                            sb["scb"][:, g:g + 1], sb["scb"][:, 8 + g:9 + g],
                                            ALU.mult, ALU.add), "V")

    xcurb = xq
    q8hb = sb["q8h"].bitcast(BF16)

    def h3sl(j):
        if j < 8:
            return sb["yout"][:, j * 512:(j + 1) * 512]
        if j < 16:
            return sb["xcb"][:, (j - 8) * 512:(j - 7) * 512]
        if j < 24:
            return xcurb[:, (j - 16) * 512:(j - 15) * 512]
        if j < 28:
            return h0b[:, (j - 24) * 512:(j - 23) * 512]
        return q8hb[:, (j - 28) * 512:(j - 27) * 512]

    psZ = [sb["ps_st0"], sb["ps_st1"]]
    ctx.wait(nc.tensor, "V")
    ctx.wait(nc.tensor, "VE")
    ctx.wait(nc.scalar, "VE")
    ctx.wait(nc.gpsimd, "TW")
    ctx.wait(nc.sync, "V")
    for t_ in w2st:
        ctx.dwait(nc.gpsimd, t_)
    ctx.dwait(nc.scalar, w2st[-1])
    ctx.dwait(nc.scalar, w2st[-2])
    bank_rel = [None, None]
    w_tw = [None, None, None]
    wslots = [sb["wbuf2"][:, 0:2048], sb["wbuf2"][:, 2048:4096], sb["wbuf"][:, 0:2048]]
    wi = 0
    xres_rel = [None, None]
    fc2_rel = [None, None]
    out_slots = [sb["stage2"][:, :], sb["atT"].bitcast(F32)[:, 0:LC]]
    out_gate = [None, None]
    st_out = []
    for s in range(NS):
        W = slice(s * LC, (s + 1) * LC)
        for m in range(32):
            buf = m % 2
            if bank_rel[buf] is not None:
                ctx.wait(nc.tensor, bank_rel[buf][0], bank_rel[buf][1])
            for kb in range(G8):
                mm = nc.tensor.matmul(psB[buf][:, 0:LC], w1sl(kb, slice(m * 128, (m + 1) * 128)),
                                      hhsl(kb, W), start=(kb == 0), stop=(kb == G8 - 1))
            tm = ctx.inc(mm, "TW")
            ctx.wait(nc.scalar, "TW", tm)
            se = ctx.inc(nc.scalar.activation(h3sl(m), psB[buf][:, 0:LC], AF.Silu,
                                              bias=sb["bb1sb"][:, m:m + 1]), "SE")
            bank_rel[buf] = ("SE", se)
        se_done = ctx.cnt["SE"]
        for m2 in range(G8):
            buf2 = m2 % 2
            if xres_rel[buf2] is not None:
                ctx.wait(nc.sync, xres_rel[buf2][0], xres_rel[buf2][1])
            xr = ctx.dgrp(nc.sync, [(sb["statsq"][:, buf2 * LC:(buf2 + 1) * LC],
                                     hb["xA"][m2 * 128:(m2 + 1) * 128, W])], "ld")
            tm2 = None
            for half in range(2):
                wb = wi % 3
                wi += 1
                if w_tw[wb] is not None:
                    ctx.wait(nc.gpsimd, "TW", w_tw[wb])
                wtok = ctx.dgrp(nc.gpsimd, [
                    (wslots[wb].rearrange("p (k f) -> p k f", f=128),
                     hb["w2scr"][half * 2048:(half + 1) * 2048,
                                 m2 * 128:(m2 + 1) * 128].rearrange("(k p) f -> p k f", p=128))], "gw")
                ctx.dwait(nc.tensor, wtok)
                if half == 0:
                    ctx.wait(nc.tensor, "SE", se_done)
                    if fc2_rel[buf2] is not None:
                        ctx.wait(nc.tensor, fc2_rel[buf2][0], fc2_rel[buf2][1])
                for t in range(16):
                    kb2 = half * 16 + t
                    mm = nc.tensor.matmul(psZ[buf2][:, 0:LC],
                                          wslots[wb][:, t * 128:(t + 1) * 128],
                                          h3sl(kb2), start=(kb2 == 0), stop=(kb2 == 31))
                tm2 = ctx.inc(mm, "TW")
                w_tw[wb] = tm2
            ctx.wait(nc.vector, "TW", tm2)
            ctx.dwait(nc.vector, xr)
            u = sb["statx"][:, buf2 * LC:(buf2 + 1) * LC]
            ve = ctx.inc(nc.vector.scalar_tensor_tensor(u, psZ[buf2][:, 0:LC], ctx.cst(0.5),
                                                        sb["statsq"][:, buf2 * LC:(buf2 + 1) * LC],
                                                        ALU.mult, ALU.add), "VE")
            fc2_rel[buf2] = ("VE", ve)
            xres_rel[buf2] = ("VE", ve)
            if out_gate[buf2] is not None:
                ctx.dwait(nc.vector, out_gate[buf2])
            yv = ctx.inc(nc.vector.scalar_tensor_tensor(out_slots[buf2], hpsl(m2, W),
                                                        sb["bb2h"][:, m2:m2 + 1], u,
                                                        ALU.add, ALU.add), "YV")
            ctx.wait(nc.sync, "YV", yv)
            out_gate[buf2] = ctx.dgrp(nc.sync, [(hb["out"][m2 * 128:(m2 + 1) * 128, W], out_slots[buf2])], "st")
            st_out.append(out_gate[buf2])
    for t_ in st_out[-4:]:
        ctx.dwait(nc.sync, t_)


def build(stage="full"):
    nc = bass.Bass("TRN2", target_bir_lowering=False, detect_race_conditions=RACECHECK)
    ctx = Ctx(nc)
    ctx.mkpool("ld", 8)
    ctx.mkpool("st", 6)
    ctx.mkpool("gw", 4)
    hb = {}
    hb["x0T"] = nc.dram_tensor("x0T", [D, L], F32, kind="ExternalInput").ap()
    hb["win"] = nc.dram_tensor("win", [D, 2 * DI], F32, kind="ExternalInput").ap()
    hb["wx"] = nc.dram_tensor("wx", [DI, 96], F32, kind="ExternalInput").ap()
    hb["wdt"] = nc.dram_tensor("wdt", [DTR, DI], F32, kind="ExternalInput").ap()
    hb["dtb"] = nc.dram_tensor("dtb", [128, 16], F32, kind="ExternalInput").ap()
    hb["wout"] = nc.dram_tensor("wout", [DI, D], F32, kind="ExternalInput").ap()
    hb["winbf"] = nc.dram_tensor("winbf", [D, 2 * DI], BF16).ap()
    hb["xscr"] = nc.dram_tensor("xscr", [DI, L], BF16).ap()
    hb["zscr"] = nc.dram_tensor("zscr", [DI, L], BF16).ap()
    hb["youtscr"] = nc.dram_tensor("youtscr", [DI, L], BF16).ap()
    hb["woutbf"] = nc.dram_tensor("woutbf", [DI, D], BF16).ap()
    hb["wobf"] = nc.dram_tensor("wobf", [D, D], BF16).ap()
    hb["dwk"] = nc.dram_tensor("dwk", [128, 64], F32, kind="ExternalInput").ap()
    hb["dwb"] = nc.dram_tensor("dwb", [128, 16], F32, kind="ExternalInput").ap()
    hb["alog"] = nc.dram_tensor("alog", [128, 256], F32, kind="ExternalInput").ap()
    hb["dskip"] = nc.dram_tensor("dskip", [128, 16], F32, kind="ExternalInput").ap()
    hb["out"] = nc.dram_tensor("out", [D, L], F32, kind="ExternalOutput").ap()
    hb["xA"] = nc.dram_tensor("xA", [D, L], F32).ap()
    hb["xB"] = nc.dram_tensor("xB", [D, L], F32).ap()
    hb["Bflat"] = nc.dram_tensor("Bflat", [NST, L], BF16).ap()
    hb["Cflat"] = nc.dram_tensor("Cflat", [NST, L], BF16).ap()
    kd = "ExternalOutput" if DEBUG else "Internal"
    hb["st_r1"] = nc.dram_tensor("st_r1", [L], F32, kind=kd).ap()
    hb["st_r2"] = nc.dram_tensor("st_r2", [L], F32).ap()
    hb["st_m2"] = nc.dram_tensor("st_m2", [L], F32).ap()
    hb["st_r2b"] = nc.dram_tensor("st_r2b", [L], F32).ap()
    hb["st_r3"] = nc.dram_tensor("st_r3", [L], F32).ap()
    hb["st_m3"] = nc.dram_tensor("st_m3", [L], F32).ap()
    hb["st_r4"] = nc.dram_tensor("st_r4", [L], F32).ap()
    hb["st_m4"] = nc.dram_tensor("st_m4", [L], F32).ap()
    hb["st_r5"] = nc.dram_tensor("st_r5", [L], F32).ap()
    hb["st_m5"] = nc.dram_tensor("st_m5", [L], F32).ap()
    hb["q1h"] = nc.dram_tensor("q1h", [D, 4096], U8, kind="ExternalInput").ap()
    hb["s1h"] = nc.dram_tensor("s1h", [8, 4096], F32, kind="ExternalInput").ap()
    hb["q2h"] = nc.dram_tensor("q2h", [4096, D], U8, kind="ExternalInput").ap()
    hb["s2h"] = nc.dram_tensor("s2h", [32, D], F32, kind="ExternalInput").ap()
    hb["g1b1"] = nc.dram_tensor("g1b1", [128, 16], F32, kind="ExternalInput").ap()
    hb["bb1h"] = nc.dram_tensor("bb1h", [128, 32], F32, kind="ExternalInput").ap()
    hb["bb2hd"] = nc.dram_tensor("bb2hd", [128, 8], F32, kind="ExternalInput").ap()
    hb["w2scr"] = nc.dram_tensor("w2scr", [4096, D], BF16).ap()
    hb["wv"] = nc.dram_tensor("wv", [D, D], F32, kind="ExternalInput").ap()
    hb["wo"] = nc.dram_tensor("wo", [D, D], F32, kind="ExternalInput").ap()
    hb["wqp"] = nc.dram_tensor("wqp", [D, 512], F32, kind="ExternalInput").ap()
    hb["wkp"] = nc.dram_tensor("wkp", [D, 512], F32, kind="ExternalInput").ap()
    hb["maskt"] = nc.dram_tensor("maskt", [128, 128], F32, kind="ExternalInput").ap()
    hb["identb_h"] = nc.dram_tensor("identb_h", [128, 128], F32, kind="ExternalInput").ap()
    hb["st_m2b"] = nc.dram_tensor("st_m2b", [L], F32).ap()
    hb["pmax"] = nc.dram_tensor("pmax", [128], F32).ap()
    hb["scal"] = nc.dram_tensor("scal", [16], F32, kind=kd).ap()
    hb["cc_in"] = nc.dram_tensor("cc_in", [1, 128], F32).ap()
    hb["cc_out"] = nc.dram_tensor("cc_out", [1, 128], F32).ap()
    hb["w0f"] = nc.dram_tensor("w0f", [D, D], F32, kind="ExternalInput").ap()
    hb["b0fT"] = nc.dram_tensor("b0fT", [128, 8], F32, kind="ExternalInput").ap()
    hb["st_m1"] = nc.dram_tensor("st_m1", [L], F32, kind=kd).ap()
    hb["sflat"] = nc.dram_tensor("sflat", [2 * L], F32, kind=kd).ap()
    if DEBUG:
        for n in ("d_xc", "d_delta", "d_z", "d_yout", "d_h0", "d_xin"):
            hb[n] = nc.dram_tensor(n, [DI, L], F32, kind="ExternalOutput").ap()
        hb["d_wq"] = nc.dram_tensor("d_wq", [D, D], F32, kind="ExternalOutput").ap()
    if DEBUG_H2:
        hb["d_h2"] = nc.dram_tensor("d_h2", [128, L], F32, kind="ExternalOutput").ap()
        hb["d_h2f"] = nc.dram_tensor("d_h2f", [8, 128, L], F32, kind="ExternalOutput").ap()
        hb["d_v"] = nc.dram_tensor("d_v", [2, 128, (L // 128) * 520], F32, kind="ExternalOutput").ap()
        hb["d_qf"] = nc.dram_tensor("d_qf", [4, 128, L], F32, kind="ExternalOutput").ap()
        hb["d_kf"] = nc.dram_tensor("d_kf", [4, 128, L], F32, kind="ExternalOutput").ap()
        hb["d_ysb"] = nc.dram_tensor("d_ysb", [2, 128, (L // 128) * 512], F32, kind="ExternalOutput").ap()
        hb["d_dbg"] = nc.dram_tensor("d_dbg", [5, 128, LC], F32, kind="ExternalOutput").ap()

        hb["d_hq"] = nc.dram_tensor("d_hq", [D, LC], F32, kind="ExternalOutput").ap()

    sb = {}
    def sbuf(name, shape, dt=F32):
        sb[name] = nc.alloc_sbuf_tensor("sb_" + name, shape, dt).ap()

    sbuf("ones_f32", [128, 1])
    sbuf("consts", [128, 32])
    ctx.const_tile = sb["consts"]
    for c_ in (EPS_LN, 1e-12, 448.0, 0.5, 224.0, 128.0):
        ctx.cst(c_)
    sbuf("Aneg", [128, 256]); sbuf("dwk", [128, 64]); sbuf("dwb", [128, 16])
    sbuf("dtbias", [128, 16]); sbuf("dskip", [128, 16])
    sbuf("wx_sb", [128, 16 * 96], BF16); sbuf("wdt_sb", [64, DI], BF16)
    sbuf("statx", [128, 2 * LC]); sbuf("statsq", [128, 2 * LC]); sbuf("sums2", [1, 2 * LC])
    sbuf("xcur", [128, G8 * LC0]); sbuf("stage2", [128, 2 * LC0]); sbuf("h0", [128, G8 * LC0], BF16)
    sbuf("rb", [128, LC0]); sbuf("mb", [128, LC0])
    sbuf("xin", [128, 4160]); sbuf("zsilu", [128, G16 * LC0])
    sbuf("xc", [128, G16 * LC0]); sbuf("xcb", [128, G16 * LC0], BF16)
    sbuf("delta", [128, 4160]); sbuf("tmpc", [128, LC0]); sbuf("tfix", [128, 16])
    sbuf("wbuf", [128, 2 * 1024], BF16); sbuf("wbuf2", [128, 2 * 2048], BF16)
    sbuf("dt16", [64, LC0], BF16); sbuf("bc96", [96, LC0])
    sbuf("Bb", [128, 16 * LC0]); sbuf("Cb", [128, 16 * LC0])
    sbuf("scanA", [128, 16 * LC0]); sbuf("scanB", [128, 16 * LC0])
    sbuf("carry", [128, 256]); sbuf("yg", [128, LC0]); sbuf("yout", [128, G16 * LC0], BF16)
    sbuf("sc", [1, 16]); sbuf("cc_row", [1, 128]); sbuf("amaxx", [128, 16]); sbuf("scb", [128, 16])
    sbuf("attnf", [128, 256]); sbuf("atT", [128, 1024], BF16); sbuf("denf", [128, 12])
    sbuf("masktri", [128, 128]); sbuf("identb", [128, 128], BF16)
    sb["ysbA"] = sb["zsilu"].bitcast(BF16)
    sb["ysbB"] = sb["xc"].bitcast(BF16)
    sb["stage4"] = sb["statx"]
    sbuf("q8t", [128, 1024], FP8); sbuf("q8h", [128, G8 * LC], FP8)
    sbuf("b0f", [128, 8])
    sbuf("ones_bf", [128, 1], BF16); sbuf("bb1sb", [128, 32]); sbuf("bb2h", [128, 8])
    sb["xcur2"] = sb["scanA"]
    sb["tmpc2"] = sb["statx"][:, 0:LC]
    sb["rb2"] = sb["statsq"][:, 0:LC]
    sb["mb2"] = sb["statsq"][:, LC:2 * LC]
    sb["hqb"] = sb["xc"].bitcast(BF16)
    sb["wqb"] = sb["delta"].bitcast(BF16)
    sb["obf"] = sb["h0"]
    sb["stage3"] = sb["zsilu"]

    sb["ps_st0"] = nc.alloc_psum_tensor("ps_st0", [128, 512], F32).ap()
    sb["ps_st1"] = nc.alloc_psum_tensor("ps_st1", [128, 512], F32).ap()
    sb["ps_w"] = [nc.alloc_psum_tensor(f"ps_w{i}", [128, 512], F32).ap() for i in range(2)]
    sb["ps_x"] = nc.alloc_psum_tensor("ps_x", [128, 512], F32).ap()
    sb["ps_d"] = [nc.alloc_psum_tensor(f"ps_d{i}", [128, 256], F32).ap() for i in range(2)]

    ctx.vsync(nc.vector.memset(sb["ones_f32"][:, :], 1.0))
    small = [(sb["b0f"][:, :], hb["b0fT"][:, :]),
             (sb["Aneg"][:, :], hb["alog"][:, :]),
             (sb["dwk"][:, :], hb["dwk"][:, :]),
             (sb["dwb"][:, :], hb["dwb"][:, :]),
             (sb["dtbias"][:, :], hb["dtb"][:, :]),
             (sb["dskip"][:, :], hb["dskip"][:, :])]
    small.append((sb["masktri"][:, :], hb["maskt"][:, :]))
    tok_small = ctx.dgrp(nc.sync, small, "ld")
    idtok = ctx.dgrp(nc.gpsimd, [(sb["identb"][:, :], hb["identb_h"][:, :])], "gw")
    ctx.dwait(nc.tensor, idtok)
    if not NOGW:
        hb["w2d_tok"] = ctx.dgrp(ctx.nc.gpsimd, [(hb["winbf"], hb["win"]), (hb["woutbf"], hb["wout"]),
                                                 (hb["wobf"], hb["wo"])], "gw")
        wtok = ctx.dgrp(nc.gpsimd, [
            (sb["wx_sb"][:, :].rearrange("p (g f) -> p g f", f=96),
             hb["wx"].rearrange("(g p) f -> p g f", p=128)),
            (sb["wdt_sb"][:, :], hb["wdt"][:, :])], "gw")
    ctx.dwait(nc.scalar, tok_small)
    s = ctx.inc(nc.scalar.activation(sb["Aneg"][:, :], sb["Aneg"][:, :], AF.Exp), "S")
    ctx.wait(nc.vector, "S", s)
    ctx.inc(nc.vector.tensor_scalar(sb["Aneg"][:, :], sb["Aneg"][:, :], -1.0, None, ALU.mult), "V")
    if not NOGW:
        ctx.dwait(nc.tensor, wtok)

    if CUT >= 1:
        emit_ln_stats(ctx, sb, hb["x0T"], hb["st_r1"], hb["st_m1"], hb["sflat"]) if CUT < 2 else None
    if CUT >= 2:
        emit_ln_stats(ctx, sb, hb["x0T"], hb["st_r1"], hb["st_m1"], hb["sflat"])
        emit_l0a(ctx, sb, hb)
    if stage in ("l0f", "l1a", "l1f", "full"):
        emit_l0f(ctx, sb, hb)
    if stage in ("l1a", "l1f", "full"):
        emit_l1a(ctx, sb, hb)
    if stage in ("l1f", "full"):
        emit_l1f(ctx, sb, hb)

    copy_src = {"l0a": "xA", "l0f": "xB", "l1a": "xA"}.get(stage)
    if copy_src is not None:
        tcp = sb["statx"]
        for c in range(L // LC):
            W = slice(c * LC, (c + 1) * LC)
            for g in range(G8):
                tok = ctx.dgrp(nc.sync, [(tcp[:, 0:LC], hb[copy_src][g * 128:(g + 1) * 128, W])], "ld")
                ctx.dwait(nc.sync, tok)
                tok = ctx.dgrp(nc.sync, [(hb["out"][g * 128:(g + 1) * 128, W], tcp[:, 0:LC])], "st")
                ctx.dwait(nc.sync, tok)
    return nc


def host_prep(inputs, core):
    m = {}
    m["x0T"] = np.ascontiguousarray(np.asarray(inputs["L0_attn_hidden_states"])[core].T)
    m["win"] = np.asarray(inputs["L0_attn_Win"])
    m["wx"] = np.asarray(inputs["L0_attn_Wx"])
    m["wdt"] = np.asarray(inputs["L0_attn_Wdt"])
    m["dtb"] = np.ascontiguousarray(np.asarray(inputs["L0_attn_dt_bias"]).reshape(16, 128).T)
    m["wout"] = np.asarray(inputs["L0_attn_Wout"])
    m["dwk"] = np.ascontiguousarray(
        np.asarray(inputs["L0_attn_dw_kernel"]).T.reshape(16, 128, 4).transpose(1, 0, 2).reshape(128, 64))
    m["dwb"] = np.ascontiguousarray(np.asarray(inputs["L0_attn_dw_bias"]).reshape(16, 128).T)
    m["alog"] = np.ascontiguousarray(
        np.asarray(inputs["L0_attn_A_log"]).reshape(16, 128, 16).transpose(1, 0, 2).reshape(128, 256))
    m["w0f"] = np.ascontiguousarray(np.asarray(inputs["L0_ffn_weight"]).T)
    m["wv"] = np.ascontiguousarray(np.asarray(inputs["L1_attn_Wv"]).T)
    m["wo"] = np.ascontiguousarray(np.asarray(inputs["L1_attn_Wo"]).T)
    m["q1h"] = np.ascontiguousarray(np.asarray(inputs["L1_ffn_fc1_q"]).astype(np.uint8))
    m["s1h"] = np.asarray(inputs["L1_ffn_fc1_scales"])
    m["q2h"] = np.ascontiguousarray(np.asarray(inputs["L1_ffn_fc2_q"]).astype(np.uint8))
    m["s2h"] = np.asarray(inputs["L1_ffn_fc2_scales"])
    m["g1b1"] = np.ascontiguousarray(np.concatenate([
        np.asarray(inputs["L1_ffn_ln_gamma"]).reshape(8, 128).T,
        np.asarray(inputs["L1_ffn_ln_beta"]).reshape(8, 128).T], axis=1))
    m["bb1h"] = np.ascontiguousarray(np.asarray(inputs["L1_ffn_fc1_bias"]).reshape(32, 128).T)
    m["bb2hd"] = np.ascontiguousarray(np.asarray(inputs["L1_ffn_fc2_bias"]).reshape(8, 128).T)
    def padqk(w):
        wt = np.asarray(w).T.reshape(1024, 16, 16)
        out = np.zeros((1024, 16, 32), np.float32)
        out[:, :, :16] = wt
        return np.ascontiguousarray(out.reshape(1024, 512))
    m["wqp"] = padqk(inputs["L1_attn_Wq"])
    m["wkp"] = padqk(inputs["L1_attn_Wk"])
    m["maskt"] = np.triu(np.ones((128, 128), np.float32))
    m["identb_h"] = np.eye(128, dtype=np.float32)
    m["b0fT"] = np.ascontiguousarray(np.asarray(inputs["L0_ffn_bias"]).reshape(8, 128).T)
    m["dskip"] = np.ascontiguousarray(np.asarray(inputs["L0_attn_Dskip"]).reshape(16, 128).T)
    return m


def kernel(**inputs):
    nc = build(STAGE)
    in_maps = [host_prep(inputs, b) for b in range(8)]
    res = run_bass_kernel_spmd(nc, in_maps, core_ids=list(range(8)))
    out = np.stack([np.ascontiguousarray(r["out"].T) for r in res.results])
    return out.astype(np.float32)


# version 10
# speedup vs baseline: 1.0121x; 1.0121x over previous
import numpy as np
import concourse.bass as bass
import concourse.mybir as mybir
from concourse.bass_utils import run_bass_kernel_spmd

F32 = mybir.dt.float32
BF16 = mybir.dt.bfloat16
FP8 = mybir.dt.float8e4
I32 = mybir.dt.int32
U8 = mybir.dt.uint8
AF = mybir.ActivationFunctionType
ALU = mybir.AluOpType
AX = mybir.AxisListType

L = 2048
D = 1024
DI = 2048
NST = 16
DTR = 64
G8 = 8
G16 = 16
LC0 = 256
LC = 512
EPS_LN = 1e-5
STAGE = "full"
DEBUG = False
DEBUG_H2 = False
RACECHECK = False
CUT = 99
NOGW = False
NCORES = 8


class Ctx:
    def __init__(self, nc):
        self.nc = nc
        self.sems = {}
        self.cnt = {}
        self.consts = {}
        self.const_tile = None
        self.pools = {}

    def cst(self, value):
        if value not in self.consts:
            i = len(self.consts)
            assert i < 32
            self.inc(self.nc.vector.memset(self.const_tile[:, i:i + 1], float(value)), "V")
            self.consts[value] = self.const_tile[:, i:i + 1]
        return self.consts[value]

    def sem(self, name):
        if name not in self.sems:
            self.sems[name] = self.nc.alloc_semaphore("sem_" + name)
            self.cnt[name] = 0
        return self.sems[name]

    def inc(self, instr, name, amt=1):
        instr.then_inc(self.sem(name), amt)
        self.cnt[name] += amt
        return self.cnt[name]

    def wait(self, eng, name, val=None):
        self.sem(name)
        v = self.cnt[name] if val is None else val
        if v > 0:
            eng.wait_ge(self.sems[name], v)

    def mkpool(self, name, n):
        self.pools[name] = [[[self.nc.alloc_semaphore(f"dp_{name}{i}"), 0] for i in range(n)], 0]

    def dgrp(self, q, transfers, pool, slow=False):
        pl = self.pools[pool]
        slot = pl[1]
        sem, used = pl[0][slot]
        pl[1] = (slot + 1) % len(pl[0])
        if used:
            q.wait_ge(sem, used * 16)
        for out, in_ in transfers:
            q.dma_start(out=out, in_=in_, allow_slow_non_contiguous=slow).then_inc(sem, 16)
            used += 1
        pl[0][slot][1] = used
        return (sem, used * 16)

    def dwait(self, eng, tok):
        eng.wait_ge(tok[0], tok[1])

    def vsync(self, instr):
        v = self.inc(instr, "V")
        self.nc.vector.wait_ge(self.sems["V"], v)
        return v

    @staticmethod
    def colsT(tile, flat, ncols):
        return [(tile[:, g:g + 1], flat[g * 128:(g + 1) * 128].rearrange("(p o) -> p o", o=1))
                for g in range(ncols)]

    @staticmethod
    def colsS(flat, tile, ncols):
        return [(flat[g * 128:(g + 1) * 128].rearrange("(p o) -> p o", o=1), tile[:, g:g + 1])
                for g in range(ncols)]


def emit_ln_stats(ctx, sb, x_src_hbm, rstd_flat, mrstd_flat, sflat, extra=None):
    nc = ctx.nc
    NCh = L // LC
    xb, sqb, sums = sb["statx"], sb["statsq"], sb["sums2"]
    ps0, ps1 = sb["ps_st0"], sb["ps_st1"]
    ctx.wait(nc.tensor, "V")
    t1mark = []
    st_tok = None
    st_all = []
    for c in range(NCh):
        W = slice(c * LC, (c + 1) * LC)
        for g in range(G8):
            h = g % 2
            xs = xb[:, h * LC:(h + 1) * LC]
            sq = sqb[:, h * LC:(h + 1) * LC]
            if len(t1mark) >= 2:
                ctx.wait(nc.sync, "T", t1mark[-2])
            tok = ctx.dgrp(nc.sync, [(xs, x_src_hbm[g * 128:(g + 1) * 128, W])], "ld")
            ctx.dwait(nc.scalar, tok)
            if len(t1mark) >= 2:
                ctx.wait(nc.scalar, "T", t1mark[-2])
            sqm = ctx.inc(nc.scalar.activation(sq, xs, AF.Square), "S")
            ctx.dwait(nc.tensor, tok)
            ctx.inc(nc.tensor.matmul(ps0[0:1, 0:LC], sb["ones_f32"][:, 0:1], xs,
                                     start=(g == 0), stop=(g == G8 - 1)), "T")
            ctx.wait(nc.tensor, "S", sqm)
            t1mark.append(ctx.inc(nc.tensor.matmul(ps1[0:1, 0:LC], sb["ones_f32"][:, 0:1], sq,
                                                   start=(g == 0), stop=(g == G8 - 1)), "T"))
        ctx.wait(nc.scalar, "T")
        if st_tok is not None:
            ctx.dwait(nc.scalar, st_tok)
        ctx.inc(nc.scalar.copy(sums[0:1, 0:LC], ps0[0:1, 0:LC]), "S")
        se = ctx.inc(nc.scalar.copy(sums[0:1, LC:2 * LC], ps1[0:1, 0:LC]), "S")
        ctx.wait(nc.tensor, "S", se)
        ctx.wait(nc.sync, "S", se)
        st_tok = ctx.dgrp(nc.sync, [
            (sflat[c * LC:(c + 1) * LC].unsqueeze(0), sums[0:1, 0:LC]),
            (sflat[L + c * LC:L + (c + 1) * LC].unsqueeze(0), sums[0:1, LC:2 * LC])], "st")
        st_all.append(st_tok)
    for t_ in st_all:
        ctx.dwait(nc.sync, t_)
    _ln_tail(ctx, sb, rstd_flat, mrstd_flat, sflat, extra)


def _ln_tail(ctx, sb, rstd_flat, mrstd_flat, sflat, extra=None):
    nc = ctx.nc
    rowA, rowB = sb["scanA"][0:1, :], sb["scanB"][0:1, :]
    tok = ctx.dgrp(nc.sync, [(rowA[0:1, 0:2 * L], sflat[:].unsqueeze(0))], "ld")
    ctx.dwait(nc.vector, tok)
    mu_row = rowB[0:1, 0:L]
    var_row = rowB[0:1, L:2 * L]
    nc.vector.tensor_scalar(mu_row, rowA[0:1, 0:L], 1.0 / D, None, ALU.mult)
    nc.vector.tensor_scalar(var_row, rowA[0:1, L:2 * L], 1.0 / D, None, ALU.mult)
    nc.vector.tensor_tensor(rowA[0:1, 0:L], mu_row, mu_row, ALU.mult)
    v = ctx.inc(nc.vector.tensor_tensor(var_row, var_row, rowA[0:1, 0:L], ALU.subtract), "V")
    ctx.wait(nc.scalar, "V", v)
    s = ctx.inc(nc.scalar.activation(rowA[0:1, L:2 * L], var_row, AF.Sqrt, bias=ctx.cst(EPS_LN)[0:1, :]), "S")
    ctx.wait(nc.vector, "S", s)
    rstd_row = rowA[0:1, 0:L]
    v = ctx.vsync(nc.vector.reciprocal(rstd_row, rowA[0:1, L:2 * L]))
    mr_row = var_row
    v = ctx.inc(nc.vector.tensor_tensor(mr_row, mu_row, rstd_row, ALU.mult), "V")
    ctx.wait(nc.sync, "V", v)
    tok = ctx.dgrp(nc.sync, [(rstd_flat[:].unsqueeze(0), rstd_row),
                             (mrstd_flat[:].unsqueeze(0), mr_row)], "st")
    if extra is not None:
        extra(tok, dict(rstd=rstd_row, mu=mu_row, mrstd=mr_row, rowB=rowB, rowA=rowA))
    else:
        ctx.dwait(nc.sync, tok)


def emit_l0a(ctx, sb, hb):
    nc = ctx.nc
    NCh = L // LC0
    NS = L // LC
    xin = sb["xin"]
    ctx.vsync(nc.vector.memset(sb["carry"].bitcast(BF16)[:, 0:256], 0.0))
    vA = sb["xin"].bitcast(BF16)
    vB = sb["delta"].bitcast(BF16)

    def h1sl(g, cs):
        t = vA if g < 4 else vB
        return t[:, (g % 4) * 2048 + cs.start:(g % 4) * 2048 + cs.stop]

    rb4, mb4 = sb["statx"][:, 0:LC], sb["statx"][:, LC:2 * LC]
    for s in range(NS):
        W = slice(s * LC, (s + 1) * LC)
        ctx.wait(nc.sync, "V")
        btok = ctx.dgrp(nc.sync, [(rb4, hb["st_r1"][W].unsqueeze(0).partition_broadcast(128)),
                                  (mb4, hb["st_m1"][W].unsqueeze(0).partition_broadcast(128))], "ld")
        ctx.dwait(nc.vector, btok)
        for g in range(G8):
            xsl = sb["statsq"][:, (g % 2) * LC:((g % 2) + 1) * LC]
            ctx.wait(nc.sync, "V")
            xtok = ctx.dgrp(nc.sync, [(xsl, hb["x0T"][g * 128:(g + 1) * 128, W])], "ld")
            ctx.dwait(nc.vector, xtok)
            nc.vector.tensor_tensor(sb["stage2"][:, :], xsl, rb4, ALU.mult)
            ctx.inc(nc.vector.tensor_tensor(h1sl(g, W), sb["stage2"][:, :], mb4, ALU.subtract), "V")

    wbuf, psW = sb["wbuf"], sb["ps_w"]
    h0b = sb["h0"]
    ctx.wait(nc.tensor, "V")
    st_tok4 = [None, None, None, None]
    bank_rel = [None, None]
    tw_mark = []
    ei = 0
    for mt in range(32):
        wb = mt % 2
        if len(tw_mark) >= 2:
            ctx.wait(nc.sync, "TW", tw_mark[-2])
        if mt == 0:
            ctx.dwait(nc.sync, hb["w2d_tok"])
        wtok = ctx.dgrp(nc.sync, [(wbuf[:, wb * 1024:(wb + 1) * 1024].rearrange("p (kg m) -> p kg m", kg=8),
                                   hb["winbf"][:, mt * 128:(mt + 1) * 128].rearrange("(kg p) m -> p kg m", p=128))],
                        "ld")
        ctx.dwait(nc.tensor, wtok)
        tw = None
        for sp in range(NS):
            Wsp = slice(sp * LC, (sp + 1) * LC)
            pbuf = (mt * NS + sp) % 2
            if bank_rel[pbuf] is not None:
                ctx.wait(nc.tensor, bank_rel[pbuf][0], bank_rel[pbuf][1])
            for k in range(G8):
                mm = nc.tensor.matmul(psW[pbuf][:, 0:LC],
                                      wbuf[:, wb * 1024 + k * 128:wb * 1024 + (k + 1) * 128],
                                      h1sl(k, Wsp), start=(k == 0), stop=(k == G8 - 1))
            tw = ctx.inc(mm, "TW")
            ctx.wait(nc.scalar, "TW", tw)
            slot = ei % 4
            ei += 1
            if st_tok4[slot] is not None:
                ctx.dwait(nc.scalar, st_tok4[slot])
            dst = h0b[:, slot * 512:(slot + 1) * 512]
            if mt < 16:
                se = ctx.inc(nc.scalar.copy(dst, psW[pbuf][:, 0:LC]), "SE")
            else:
                se = ctx.inc(nc.scalar.activation(dst, psW[pbuf][:, 0:LC], AF.Silu), "SE")
            bank_rel[pbuf] = ("SE", se)
            ctx.wait(nc.gpsimd, "SE", se)
            tgt = hb["xscr"] if mt < 16 else hb["zscr"]
            st_tok4[slot] = ctx.dgrp(nc.gpsimd, [(tgt[(mt % 16) * 128:(mt % 16 + 1) * 128, Wsp], dst)], "gw")
        tw_mark.append(tw)
    for t_ in st_tok4:
        if t_ is not None:
            ctx.dwait(nc.sync, t_)
            ctx.dwait(nc.gpsimd, t_)

    xinb = vA[:, 0:G16 * (LC0 + 3)]
    zsb = sb["zsilu"].bitcast(BF16)[:, 0:G16 * LC0]
    ctx.wait(nc.vector, "TW")
    ctx.vsync(nc.vector.memset(xinb[:, :], 0.0))
    dv_rel = [None, None]
    gp_rel = [0, 0]
    fix_mark = [0, 0]
    pend = None
    ystore = None
    for c in range(NCh):
        W = slice(c * LC0, (c + 1) * LC0)
        t1h = sb["tmpc"]
        ctx.wait(nc.sync, "V")
        xtok = ctx.dgrp(nc.sync, [(xinb[:, g * (LC0 + 3) + 3:(g + 1) * (LC0 + 3)],
                                   hb["xscr"][g * 128:(g + 1) * 128, W]) for g in range(G16)], "ld")
        ztok = ctx.dgrp(nc.sync, [(zsb[:, :].rearrange("p (g l) -> p g l", g=16),
                                   hb["zscr"][:, W].rearrange("(g p) l -> p g l", p=128))], "ld")
        ctx.dwait(nc.vector, xtok)
        ctx.dwait(nc.vector, ztok)
        if ystore is not None:
            ctx.dwait(nc.vector, ystore)
        xc, xcb, dwk, psX = sb["xc"], sb["xcb"], sb["dwk"], sb["ps_x"]
        t1s = [sb["tmpc"], sb["yg"]]
        ctx.wait(nc.vector, "SE")
        ctx.wait(nc.tensor, "SE")
        conv_s = [None, None]
        for g in range(G16):
            base = g * (LC0 + 3)
            t1 = t1s[g % 2]
            if conv_s[g % 2] is not None:
                ctx.wait(nc.vector, "S", conv_s[g % 2])
            nc.vector.tensor_scalar(t1[:, :], xinb[:, base:base + LC0], dwk[:, g * 4:g * 4 + 1], None, ALU.mult)
            for i in (1, 2, 3):
                ins = nc.vector.scalar_tensor_tensor(t1[:, :], xinb[:, base + i:base + i + LC0],
                                                     dwk[:, g * 4 + i:g * 4 + i + 1], t1[:, :],
                                                     ALU.mult, ALU.add)
            v = ctx.inc(ins, "V")
            ctx.wait(nc.scalar, "V", v)
            s = ctx.inc(nc.scalar.activation(xc[:, g * LC0:(g + 1) * LC0], t1[:, :], AF.Silu,
                                             bias=sb["dwb"][:, g:g + 1]), "S")
            conv_s[g % 2] = s
            ctx.wait(nc.scalar, "S", s)
            s2 = ctx.inc(nc.scalar.copy(xcb[:, g * LC0:(g + 1) * LC0], xc[:, g * LC0:(g + 1) * LC0]), "S")
            ctx.wait(nc.tensor, "S", s2)
            mm = nc.tensor.matmul(psX[0:96, 0:LC0], sb["wx_sb"][:, g * 96:(g + 1) * 96],
                                  xcb[:, g * LC0:(g + 1) * LC0], start=(g == 0), stop=(g == G16 - 1))
        ctx.inc(mm, "T")
        for g in range(G16):
            base = g * (LC0 + 3)
            ctx.inc(nc.vector.tensor_copy(xinb[:, base:base + 3], xinb[:, base + LC0:base + LC0 + 3]), "V")
        dt16, bc96 = sb["dt16"], sb["bc96"]
        ctx.wait(nc.scalar, "T")
        ctx.inc(nc.scalar.copy(dt16[:, :], psX[0:64, 0:LC0]), "SE")
        se = ctx.inc(nc.scalar.copy(bc96[64:96, 0:LC0], psX[64:96, 0:LC0]), "SE")
        ctx.wait(nc.gpsimd, "SE", se)
        fl_tok = ctx.dgrp(nc.gpsimd, [(hb["Bflat"][:, W], bc96[64:80, 0:LC0]),
                                      (hb["Cflat"][:, W], bc96[80:96, 0:LC0])], "gw")
        if CUT < 5:
            continue
        psD, delta = sb["ps_d"], sb["delta"]
        s_mark = []
        ctx.wait(nc.tensor, "SE")
        for g in range(G16):
            if len(s_mark) >= 2:
                ctx.wait(nc.tensor, "S", s_mark[-2])
            t = ctx.inc(nc.tensor.matmul(psD[g % 2][:, 0:LC0], sb["wdt_sb"][:, g * 128:(g + 1) * 128],
                                         dt16[:, :], start=True, stop=True), "T")
            ctx.wait(nc.scalar, "T", t)
            nc.scalar.activation(psD[g % 2][:, 0:LC0], psD[g % 2][:, 0:LC0], AF.Exp,
                                 bias=sb["dtbias"][:, g:g + 1])
            s_mark.append(ctx.inc(nc.scalar.activation(delta[:, g * LC0:(g + 1) * LC0],
                                                       psD[g % 2][:, 0:LC0], AF.Ln, bias=1.0), "S"))
        Bb, Cb = sb["Bb"], sb["Cb"]
        ctx.dwait(nc.sync, fl_tok)
        ctx.wait(nc.sync, "V")
        bc_tok = ctx.dgrp(nc.sync, [
            (sb["Bb"].bitcast(BF16)[:, 0:4096].rearrange("p (n l) -> p n l", n=16),
             hb["Bflat"][:, W].unsqueeze(0).partition_broadcast(128)),
            (sb["Cb"].bitcast(BF16)[:, 0:4096].rearrange("p (n l) -> p n l", n=16),
             hb["Cflat"][:, W].unsqueeze(0).partition_broadcast(128))], "ld")
        if CUT < 6:
            continue
        yg, yout = sb["yg"], sb["yout"]
        dAv = sb["scanA"].bitcast(BF16)
        dBv = sb["scanB"].bitcast(BF16)
        dAb = [dAv[:, 0:4096], dAv[:, 4096:8192]]
        dBb = [dBv[:, 0:4096], dBv[:, 4096:8192]]
        cscr = sb["carry"].bitcast(BF16)
        carry_b = cscr[:, 0:256]
        du_b = cscr[:, 256:512]
        tfix_b = sb["tfix"].bitcast(BF16)[:, 0:16]
        Bb3 = sb["Bb"].bitcast(BF16)[:, 0:4096].rearrange("p (n l) -> p n l", n=16)
        Cb3 = sb["Cb"].bitcast(BF16)[:, 0:4096].rearrange("p (n l) -> p n l", n=16)
        ctx.wait(nc.vector, "S")
        ctx.dwait(nc.vector, bc_tok)
        ctx.dwait(nc.gpsimd, bc_tok)
        for g in range(G16):
            gs = slice(g * LC0, (g + 1) * LC0)
            buf = g % 2
            dA = dAb[buf]
            dA3 = dA.rearrange("p (n l) -> p n l", n=16)
            dB3 = dBb[buf].rearrange("p (n l) -> p n l", n=16)
            cg = carry_b[:, g * 16:(g + 1) * 16]
            if dv_rel[buf] is not None:
                ctx.wait(nc.scalar, "V", dv_rel[buf])
                ctx.wait(nc.scalar, "P", gp_rel[buf])
            for n in range(NST):
                s = ctx.inc(nc.scalar.activation(dA3[:, n, :], delta[:, gs], AF.Exp,
                                                 scale=sb["Aneg"][:, g * 16 + n:g * 16 + n + 1]), "S")
            nc.vector.tensor_tensor(du_b[:, :], delta[:, gs], xc[:, gs], ALU.mult)
            dbu = ctx.inc(nc.vector.tensor_tensor(
                dB3[:, :, :], du_b[:, :].unsqueeze(1).broadcast_to([128, 16, LC0]),
                Bb3[:, :, :], ALU.mult), "V")
            ctx.wait(nc.gpsimd, "S", s)
            ctx.wait(nc.gpsimd, "V", dbu)
            nc.gpsimd.tensor_tensor(tfix_b[:, :], dA3[:, :, 0], cg, ALU.mult)
            nc.gpsimd.tensor_tensor(dB3[:, :, 0], dB3[:, :, 0], tfix_b[:, :], ALU.add)
            fix_mark[buf] = ctx.inc(nc.gpsimd.memset(dA3[:, :, 0], 0.0), "P")
            if pend is not None:
                pend()
                pend = None

            def mk_post(g=g, buf=buf, dA=dA, dA3=dA3, dB3=dB3, cg=cg, gs=gs):
                def post():
                    ctx.wait(nc.vector, "P", fix_mark[buf])
                    vscan = ctx.inc(nc.vector.tensor_tensor_scan(dA[:, :], dA[:, :], dBb[buf][:, :],
                                                                 0.0, ALU.mult, ALU.add), "V")
                    ctx.wait(nc.gpsimd, "V", vscan)
                    gp_rel[buf] = ctx.inc(nc.gpsimd.tensor_copy(cg, dA3[:, :, LC0 - 1]), "P")
                    dv_rel[buf] = ctx.inc(nc.vector.tensor_tensor(dB3[:, :, :], dA3[:, :, :],
                                                                  Cb3[:, :, :], ALU.mult), "V")
                    nc.vector.tensor_tensor(dB3[:, 0:8, :], dB3[:, 0:8, :], dB3[:, 8:16, :], ALU.add)
                    nc.vector.tensor_tensor(dB3[:, 0:4, :], dB3[:, 0:4, :], dB3[:, 4:8, :], ALU.add)
                    nc.vector.tensor_tensor(dB3[:, 0:2, :], dB3[:, 0:2, :], dB3[:, 2:4, :], ALU.add)
                    nc.vector.tensor_tensor(yg[:, :], dB3[:, 0, :], dB3[:, 1, :], ALU.add)
                    nc.vector.scalar_tensor_tensor(yg[:, :], xc[:, gs], sb["dskip"][:, g:g + 1],
                                                   yg[:, :], ALU.mult, ALU.add)
                    ctx.inc(nc.vector.tensor_tensor(yout[:, gs], yg[:, :], zsb[:, gs],
                                                    ALU.mult), "V")
                return post
            pend = mk_post()
        pend()
        pend = None
        vy = ctx.cnt["V"]
        ctx.wait(nc.gpsimd, "V", vy)
        ystore = ctx.dgrp(nc.gpsimd, [(hb["youtscr"][:, W].rearrange("(g p) l -> p g l", p=128),
                                       yout[:, :].rearrange("p (g l) -> p g l", g=16))], "gw")
    ctx.dwait(nc.sync, ystore)

    def ytsl(k, cs):
        t = [vA, vB, sb["xc"].bitcast(BF16), sb["zsilu"].bitcast(BF16)][k // 4]
        return t[:, (k % 4) * 2048 + cs.start:(k % 4) * 2048 + cs.stop]

    ctx.wait(nc.sync, "V")
    ytoks = []
    for k in range(G16):
        ytoks.append(ctx.dgrp(nc.sync, [(ytsl(k, slice(0, L)),
                                         hb["youtscr"][k * 128:(k + 1) * 128, :])], "ld"))
    for t_ in ytoks:
        ctx.dwait(nc.tensor, t_)
    wbuf2, psW2 = sb["wbuf2"], sb["ps_w"]
    w_tw = [None, None]
    bank2 = [None, None]
    res_tok = [None, None]
    xres_rel = [None, None]
    for m in range(G8):
        wb = m % 2
        if w_tw[wb] is not None:
            ctx.wait(nc.sync, "TW", w_tw[wb])
        wtok = ctx.dgrp(nc.sync, [(wbuf2[:, wb * 2048:(wb + 1) * 2048].rearrange("p (kg m) -> p kg m", kg=16),
                                   hb["woutbf"][:, m * 128:(m + 1) * 128].rearrange("(kg p) m -> p kg m", p=128))],
                        "ld")
        ctx.dwait(nc.tensor, wtok)
        tw = None
        for sp in range(NS):
            Wsp = slice(sp * LC, (sp + 1) * LC)
            pbuf = (m * NS + sp) % 2
            if xres_rel[pbuf] is not None:
                ctx.wait(nc.sync, xres_rel[pbuf][0], xres_rel[pbuf][1])
            xr = ctx.dgrp(nc.sync, [(sb["statsq"][:, pbuf * LC:(pbuf + 1) * LC],
                                     hb["x0T"][m * 128:(m + 1) * 128, Wsp])], "ld")
            if bank2[pbuf] is not None:
                ctx.wait(nc.tensor, bank2[pbuf][0], bank2[pbuf][1])
            for k in range(G16):
                mm = nc.tensor.matmul(psW2[pbuf][:, 0:LC],
                                      wbuf2[:, wb * 2048 + k * 128:wb * 2048 + (k + 1) * 128],
                                      ytsl(k, Wsp), start=(k == 0), stop=(k == G16 - 1))
            tw = ctx.inc(mm, "TW")
            ctx.wait(nc.vector, "TW", tw)
            ctx.dwait(nc.vector, xr)
            if res_tok[pbuf] is not None:
                ctx.dwait(nc.vector, res_tok[pbuf])
            ve = ctx.inc(nc.vector.tensor_tensor(sb["statx"][:, pbuf * LC:(pbuf + 1) * LC],
                                                 sb["statsq"][:, pbuf * LC:(pbuf + 1) * LC],
                                                 psW2[pbuf][:, 0:LC], ALU.add), "VE")
            bank2[pbuf] = ("VE", ve)
            xres_rel[pbuf] = ("VE", ve)
            ctx.wait(nc.gpsimd, "VE", ve)
            res_tok[pbuf] = ctx.dgrp(nc.gpsimd, [(hb["xA"][m * 128:(m + 1) * 128, Wsp],
                                                  sb["statx"][:, pbuf * LC:(pbuf + 1) * LC])], "gw")
        w_tw[wb] = tw
    for t_ in res_tok:
        if t_ is not None:
            ctx.dwait(nc.sync, t_)
            ctx.dwait(nc.gpsimd, t_)


def emit_l0f(ctx, sb, hb):
    nc = ctx.nc
    NCh = L // LC
    emit_ln_stats(ctx, sb, hb["xA"], hb["st_r2"], hb["st_m2"], hb["sflat"])
    rowA = sb["scanA"][0:1, :]
    rowB = sb["scanB"][0:1, :]
    sc = sb["sc"]
    ctx.vsync(nc.vector.memset(sc[0:1, :], 0.0))
    wst = [sb["Bb"], sb["Cb"]]
    wtok = ctx.dgrp(nc.sync, [(wst[kt // 4][:, (kt % 4) * 1024:(kt % 4 + 1) * 1024],
                               hb["w0f"][kt * 128:(kt + 1) * 128, :]) for kt in range(8)], "ld")
    ctx.dwait(nc.vector, wtok)
    am = sb["tfix"]
    for kt in range(8):
        ctx.vsync(nc.vector.tensor_reduce(am[:, kt:kt + 1], wst[kt // 4][:, (kt % 4) * 1024:(kt % 4 + 1) * 1024],
                                axis=AX.X, op=ALU.max, apply_absolute_value=True))
    v = ctx.vsync(nc.vector.tensor_reduce(am[:, 8:9], am[:, 0:8], axis=AX.X, op=ALU.max))
    ctx.wait(nc.gpsimd, "V", v)
    t1 = ctx.dgrp(nc.gpsimd, [(hb["pmax"][0:128], am[:, 8:9])], "gw")
    ctx.dwait(nc.sync, t1)
    t2 = ctx.dgrp(nc.sync, [(sb["cc_row"][0:1, 0:128], hb["pmax"][0:128].unsqueeze(0))], "ld")
    ctx.dwait(nc.vector, t2)
    ctx.vsync(nc.vector.tensor_reduce(sc[0:1, 0:1], sb["cc_row"][0:1, 0:128], axis=AX.X, op=ALU.max))
    ctx.vsync(nc.vector.tensor_scalar(sc[0:1, 1:2], sc[0:1, 0:1], 1e-12, None, ALU.max))
    ctx.vsync(nc.vector.reciprocal(sc[0:1, 2:3], sc[0:1, 1:2]))
    ctx.vsync(nc.vector.tensor_scalar(sc[0:1, 3:4], sc[0:1, 2:3], 448.0, None, ALU.mult))
    ctx.vsync(nc.vector.tensor_scalar(sc[0:1, 4:5], sc[0:1, 3:4], 448.0, None, ALU.min))
    ctx.vsync(nc.vector.tensor_scalar(sc[0:1, 5:6], sc[0:1, 4:5], 0.5, None, ALU.mult))
    v = ctx.vsync(nc.vector.reciprocal(sc[0:1, 6:7], sc[0:1, 4:5]))
    ctx.wait(nc.sync, "V", v)
    t3 = ctx.dgrp(nc.sync, [(hb["scal"][0:16].unsqueeze(0), sc[0:1, 0:16])], "st")
    ctx.dwait(nc.sync, t3)
    t4 = ctx.dgrp(nc.sync, [(sb["scb"][:, :], hb["scal"][0:16].unsqueeze(0).partition_broadcast(128))], "ld")
    ctx.dwait(nc.vector, t4)
    for kt in range(8):
        wsl = wst[kt // 4][:, (kt % 4) * 1024:(kt % 4 + 1) * 1024]
        nc.vector.tensor_scalar(sb["q8t"][:, 0:1024], wsl, sb["scb"][:, 5:6], None, ALU.mult)
        v = ctx.inc(nc.vector.tensor_copy(sb["wqb"][:, kt * 1024:(kt + 1) * 1024], sb["q8t"][:, 0:1024]), "V")
    xcur, t1h = sb["xin"][:, 0:G8 * LC], sb["tmpc2"]
    rb, mb = sb["rb2"], sb["mb2"]
    ctx.vsync(nc.vector.memset(sb["amaxx"][:, :], 0.0))
    nacc = 0
    for c in range(NCh):
        W = slice(c * LC, (c + 1) * LC)
        ctx.wait(nc.sync, "V")
        xtok = ctx.dgrp(nc.sync, [(xcur[:, g * LC:(g + 1) * LC],
                                   hb["xA"][g * 128:(g + 1) * 128, W]) for g in range(G8)], "ld")
        btok = ctx.dgrp(nc.sync, [(rb[:, :], hb["st_r2"][W].unsqueeze(0).partition_broadcast(128)),
                                  (mb[:, :], hb["st_m2"][W].unsqueeze(0).partition_broadcast(128))], "ld")
        ctx.dwait(nc.vector, xtok)
        ctx.dwait(nc.vector, btok)
        for g in range(G8):
            gs = slice(g * LC, (g + 1) * LC)
            nc.vector.tensor_tensor(t1h[:, :], xcur[:, gs], rb[:, :], ALU.mult)
            nc.vector.tensor_tensor(t1h[:, :], t1h[:, :], mb[:, :], ALU.subtract)
            v = ctx.vsync(nc.vector.tensor_reduce(sb["amaxx"][:, 1 + nacc % 15:2 + nacc % 15], t1h[:, :],
                                                axis=AX.X, op=ALU.max, apply_absolute_value=True))
            nacc += 1
        if nacc >= 15:
            v = ctx.vsync(nc.vector.tensor_reduce(sb["amaxx"][:, 0:1], sb["amaxx"][:, 0:16],
                                                axis=AX.X, op=ALU.max))
            nacc = 0
    v = ctx.vsync(nc.vector.tensor_reduce(sb["amaxx"][:, 0:1], sb["amaxx"][:, 0:1 + nacc],
                                        axis=AX.X, op=ALU.max))
    ctx.wait(nc.gpsimd, "V", v)
    t1 = ctx.dgrp(nc.gpsimd, [(hb["pmax"][0:128], sb["amaxx"][:, 0:1])], "gw")
    ctx.dwait(nc.gpsimd, t1)
    t2 = ctx.dgrp(nc.gpsimd, [(hb["cc_in"][:, :], hb["pmax"][0:128].unsqueeze(0))], "gw")
    ctx.dwait(nc.gpsimd, t2)
    ctx.inc(nc.gpsimd.collective_compute("AllReduce", ALU.max, replica_groups=[list(range(NCORES))],
                                         ins=[hb["cc_in"][:, :]], outs=[hb["cc_out"][:, :]]), "CC")
    ctx.wait(nc.sync, "CC")
    t2 = ctx.dgrp(nc.sync, [(sb["cc_row"][0:1, 0:128], hb["cc_out"][:, :])], "ld")
    ctx.dwait(nc.vector, t2)
    ctx.vsync(nc.vector.tensor_reduce(sc[0:1, 8:9], sb["cc_row"][0:1, 0:128], axis=AX.X, op=ALU.max))
    ctx.vsync(nc.vector.tensor_scalar(sc[0:1, 9:10], sc[0:1, 8:9], 1e-12, None, ALU.max))
    ctx.vsync(nc.vector.reciprocal(sc[0:1, 10:11], sc[0:1, 9:10]))
    ctx.vsync(nc.vector.tensor_scalar(sc[0:1, 11:12], sc[0:1, 10:11], 448.0, None, ALU.mult))
    ctx.vsync(nc.vector.tensor_scalar(sc[0:1, 12:13], sc[0:1, 11:12], 448.0, None, ALU.min))
    ctx.vsync(nc.vector.tensor_scalar(sc[0:1, 13:14], sc[0:1, 12:13], 0.5, None, ALU.mult))
    ctx.vsync(nc.vector.reciprocal(sc[0:1, 14:15], sc[0:1, 12:13]))
    ctx.vsync(nc.vector.tensor_tensor(sc[0:1, 15:16], sc[0:1, 14:15], sc[0:1, 6:7], ALU.mult))
    ctx.vsync(nc.vector.tensor_scalar(sc[0:1, 15:16], sc[0:1, 15:16], 4.0, None, ALU.mult))
    nc.vector.tensor_scalar(rowA[0:1, L:2 * L], rowA[0:1, 0:L], sc[0:1, 13:14], None, ALU.mult)
    v = ctx.inc(nc.vector.tensor_scalar(rowB[0:1, 0:L], rowB[0:1, L:2 * L],
                                        sc[0:1, 13:14], None, ALU.mult), "V")
    ctx.wait(nc.sync, "V", v)
    t3 = ctx.dgrp(nc.sync, [(hb["st_r2b"][:].unsqueeze(0), rowA[0:1, L:2 * L]),
                            (hb["st_m2b"][:].unsqueeze(0), rowB[0:1, 0:L]),
                            (hb["scal"][0:16].unsqueeze(0), sc[0:1, 0:16])], "st")
    ctx.dwait(nc.sync, t3)
    t4 = ctx.dgrp(nc.sync, [(sb["scb"][:, :], hb["scal"][0:16].unsqueeze(0).partition_broadcast(128))], "ld")
    ctx.dwait(nc.vector, t4)
    ctx.dwait(nc.scalar, t4)
    if DEBUG:
        ctx.wait(nc.gpsimd, "V")
        dt_ = ctx.dgrp(nc.gpsimd, [(hb["d_wq"][kt * 128:(kt + 1) * 128, :],
                                    sb["wqb"][:, kt * 1024:(kt + 1) * 1024]) for kt in range(8)], "gw")
        ctx.dwait(nc.gpsimd, dt_)
    xcur = sb["xcur2"]
    psW = sb["ps_w"]
    ctx.wait(nc.tensor, "SE")
    ctx.wait(nc.tensor, "VE")
    bank_rel = [None, None]
    res_tok = [None, None]
    for c in range(NCh):
        W = slice(c * LC, (c + 1) * LC)
        ctx.wait(nc.sync, "V")
        ctx.wait(nc.sync, "VE")
        xtok = ctx.dgrp(nc.sync, [(xcur[:, g * LC:(g + 1) * LC],
                                   hb["xA"][g * 128:(g + 1) * 128, W]) for g in range(G8)], "ld")
        btok = ctx.dgrp(nc.sync, [(rb[:, :], hb["st_r2b"][W].unsqueeze(0).partition_broadcast(128)),
                                  (mb[:, :], hb["st_m2b"][W].unsqueeze(0).partition_broadcast(128))], "ld")
        ctx.dwait(nc.vector, xtok)
        ctx.dwait(nc.vector, btok)
        for g in range(G8):
            gs = slice(g * LC, (g + 1) * LC)
            nc.vector.tensor_tensor(t1h[:, :], xcur[:, gs], rb[:, :], ALU.mult)
            nc.vector.tensor_tensor(sb["q8h"][:, gs], t1h[:, :], mb[:, :], ALU.subtract)
            v = ctx.inc(nc.vector.tensor_copy(sb["hqb"][:, gs], sb["q8h"][:, gs]), "V")
        if DEBUG and c == 0:
            ctx.wait(nc.gpsimd, "V")
            dt_ = ctx.dgrp(nc.gpsimd, [(hb["d_hq"][g * 128:(g + 1) * 128, :],
                                        sb["hqb"][:, g * LC:(g + 1) * LC]) for g in range(G8)], "gw")
            ctx.dwait(nc.gpsimd, dt_)
        ctx.wait(nc.tensor, "V")
        for m in range(G8):
            buf = m % 2
            if bank_rel[buf] is not None:
                ctx.wait(nc.tensor, bank_rel[buf][0], bank_rel[buf][1])
            for k in range(G8):
                mm = nc.tensor.matmul(psW[buf][:, 0:LC],
                                      sb["wqb"][:, k * 1024 + m * 128:k * 1024 + (m + 1) * 128],
                                      sb["hqb"][:, k * LC:(k + 1) * LC], start=(k == 0), stop=(k == G8 - 1))
            ctx.inc(mm, "TW")
            ctx.wait(nc.scalar, "TW")
            se = ctx.inc(nc.scalar.activation(sb["obf"][:, buf * LC:(buf + 1) * LC], psW[buf][:, 0:LC],
                                              AF.Identity, bias=sb["b0f"][:, m:m + 1], scale=sb["scb"][:, 15:16]), "SE")
            bank_rel[buf] = ("SE", se)
            ctx.wait(nc.vector, "SE", se)
            if res_tok[buf] is not None:
                ctx.dwait(nc.vector, res_tok[buf])
            ve = ctx.inc(nc.vector.tensor_tensor(sb["stage3"][:, buf * LC:(buf + 1) * LC],
                                                 xcur[:, m * LC:(m + 1) * LC],
                                                 sb["obf"][:, buf * LC:(buf + 1) * LC], ALU.add), "VE")
            ctx.wait(nc.sync, "VE", ve)
            res_tok[buf] = ctx.dgrp(nc.sync, [(hb["xB"][m * 128:(m + 1) * 128, W],
                                               sb["stage3"][:, buf * LC:(buf + 1) * LC])], "st")
    for t_ in res_tok:
        if t_ is not None:
            ctx.dwait(nc.sync, t_)


def emit_l1a(ctx, sb, hb):
    nc = ctx.nc
    v_l0f_done = ctx.cnt["V"]
    NS = L // LC
    NQ = L // 128
    emit_ln_stats(ctx, sb, hb["xB"], hb["st_r3"], hb["st_m3"], hb["sflat"])
    h2a = sb["scanA"].bitcast(BF16)
    h2b = sb["scanB"].bitcast(BF16)

    def h2sl(k, cs):
        t = h2a if k < 4 else h2b
        return t[:, (k % 4) * 2048 + cs.start:(k % 4) * 2048 + cs.stop]

    xs2, rb, mb = sb["statx"], sb["rb2"], sb["mb2"]
    for s in range(NS):
        W = slice(s * LC, (s + 1) * LC)
        ctx.wait(nc.sync, "V")
        btok = ctx.dgrp(nc.sync, [(rb[:, :], hb["st_r3"][W].unsqueeze(0).partition_broadcast(128)),
                                  (mb[:, :], hb["st_m3"][W].unsqueeze(0).partition_broadcast(128))], "ld")
        ctx.dwait(nc.vector, btok)
        for g in range(G8):
            hh = g % 2
            xsl = xs2[:, hh * LC:(hh + 1) * LC]
            ctx.wait(nc.sync, "V")
            xtok = ctx.dgrp(nc.sync, [(xsl, hb["xB"][g * 128:(g + 1) * 128, W])], "ld")
            ctx.dwait(nc.vector, xtok)
            nc.vector.tensor_tensor(sb["tmpc2"][:, :], xsl, rb[:, :], ALU.mult)
            wslice = h2sl(g, W)

            ctx.inc(nc.vector.tensor_tensor(wslice, sb["tmpc2"][:, :], mb[:, :], ALU.subtract), "V")
            if (DEBUG_H2 & 16) and g == 7:
                ctx.wait(nc.sync, "V")
                d_ = ctx.dgrp(nc.sync, [(hb["d_dbg"][0], xsl), (hb["d_dbg"][1], sb["tmpc2"][:, 0:LC]),
                                        (hb["d_dbg"][2], rb[:, 0:LC]), (hb["d_dbg"][3], mb[:, 0:LC])], "ld")
                ctx.dwait(nc.sync, d_)
                ctx.wait(nc.gpsimd, "V")
                dg_ = ctx.dgrp(nc.gpsimd, [(hb["d_dbg"][4], wslice)], "gw")
                ctx.dwait(nc.gpsimd, dg_)
    v_h2_done = ctx.cnt["V"]
    if DEBUG_H2 & 1:
        ctx.wait(nc.gpsimd, "V")
        dtk = ctx.dgrp(nc.gpsimd, [(hb["d_h2f"][g], h2sl(g, slice(0, L))) for g in range(G8)], "gw")
        ctx.dwait(nc.gpsimd, dtk)
    wv = sb["Bb"].bitcast(BF16)
    vA = sb["xin"].bitcast(BF16)
    vB = sb["delta"].bitcast(BF16)
    ctx.wait(nc.gpsimd, "V")
    wvtok = ctx.dgrp(nc.gpsimd, [(wv[:, :].rearrange("p (k f) -> p k f", k=8),
                                  hb["wv"][:, :].rearrange("(k p) f -> p k f", p=128))], "gw")
    for t in range(16):
        ctx.inc(nc.vector.memset(vA[:, t * 520:t * 520 + 520].rearrange("p (h e) -> p h e", e=65)[:, :, 64:65], 1.0), "V")
        ctx.inc(nc.vector.memset(vB[:, t * 520:t * 520 + 520].rearrange("p (h e) -> p h e", e=65)[:, :, 64:65], 1.0), "V")
    ctx.dwait(nc.tensor, wvtok)
    ctx.wait(nc.tensor, "V")
    ctx.wait(nc.tensor, "SE")
    ctx.wait(nc.tensor, "VE")
    tw_pre_v = ctx.cnt["TW"]
    psV, ev_mark = [sb["ps_w"][0], sb["ps_w"][1]], []
    for lt in range(NQ):
        for nv in range(2):
            buf = (lt * 2 + nv) % 2
            if len(ev_mark) >= 2:
                ctx.wait(nc.tensor, "SE", ev_mark[-2])
            for k in range(G8):
                mm = nc.tensor.matmul(psV[buf][:, 0:LC],
                                      h2sl(k, slice(lt * 128, (lt + 1) * 128)),
                                      wv[:, k * 1024 + nv * 512:k * 1024 + (nv + 1) * 512],
                                      start=(k == 0), stop=(k == G8 - 1))
            t = ctx.inc(mm, "TW")
            ctx.wait(nc.scalar, "TW", t)
            dst = (vA if nv == 0 else vB)[:, lt * 520:(lt + 1) * 520].rearrange("p (h e) -> p h e", e=65)[:, :, 0:64]
            ev_mark.append(ctx.inc(nc.scalar.copy(dst, psV[buf][:, 0:LC].rearrange("p (h e) -> p h e", e=64)), "SE"))
    if DEBUG_H2 & 2:
        ctx.wait(nc.gpsimd, "SE")
        dtk = ctx.dgrp(nc.gpsimd, [(hb["d_v"][0], vA[:, 0:NQ * 520]), (hb["d_v"][1], vB[:, 0:NQ * 520])], "gw")
        ctx.dwait(nc.gpsimd, dtk)
    qf = sb["Bb"].bitcast(BF16)
    kf = sb["Cb"].bitcast(BF16)
    wqp = sb["yout"].bitcast(BF16) if False else sb["yout"]
    wkp = sb["wbuf2"]
    ctx.wait(nc.tensor, "SE")
    ctx.wait(nc.gpsimd, "TW", tw_pre_v)
    wqtok = ctx.dgrp(nc.gpsimd, [(wqp[:, :].rearrange("p (k f) -> p k f", k=8),
                                  hb["wqp"][:, :].rearrange("(k p) f -> p k f", p=128)),
                                 (wkp[:, :].rearrange("p (k f) -> p k f", k=8),
                                  hb["wkp"][:, :].rearrange("(k p) f -> p k f", p=128))], "gw")
    ctx.dwait(nc.tensor, wqtok)
    qk_mark = list(ev_mark[-2:])
    for which, wt, dst in ((0, wqp, qf), (1, wkp, kf)):
        for t in range(4):
            for s in range(NS):
                buf = len(qk_mark) % 2
                if len(qk_mark) >= 2:
                    ctx.wait(nc.tensor, "SE", qk_mark[-2])
                for k in range(G8):
                    mm = nc.tensor.matmul(psV[buf][:, 0:LC], wt[:, k * 512 + t * 128:k * 512 + (t + 1) * 128],
                                          h2sl(k, slice(s * LC, (s + 1) * LC)), start=(k == 0), stop=(k == G8 - 1))
                tm = ctx.inc(mm, "TW")
                ctx.wait(nc.scalar, "TW", tm)
                qk_mark.append(ctx.inc(nc.scalar.copy(dst[:, t * 2048 + s * LC:t * 2048 + (s + 1) * LC],
                                                      psV[buf][:, 0:LC]), "SE"))
    if DEBUG_H2 & 4:
        ctx.wait(nc.gpsimd, "SE")
        dtk = ctx.dgrp(nc.gpsimd, [(hb["d_qf"][t], qf[:, t * 2048:t * 2048 + L]) for t in range(4)] +
                                  [(hb["d_kf"][t], kf[:, t * 2048:t * 2048 + L]) for t in range(4)], "gw")
        ctx.dwait(nc.gpsimd, dtk)
    psS = [sb["ps_w"][0], sb["ps_w"][1], sb["ps_x"]]
    psN = [sb["ps_st0"], sb["ps_st1"]]
    afb = [sb["stage2"][:, 0:LC], sb["statx"][:, 0:LC], sb["statsq"][:, 0:LC]]
    atT = sb["atT"]
    atS = [atT[:, 0:512], atT[:, 512:1024], sb["Aneg"].bitcast(BF16)[:, 0:512]]
    import os
    _m = os.environ.get('ND3', 'sxa')
    NDs = 3 if 's' in _m else 2
    NDf = 3 if 'x' in _m else 2
    NDa = 3 if 'a' in _m else 2
    ND = min(NDs, NDf, NDa)
    c_sq, c_bi = 1.0 / np.sqrt(32.0), 1.0 / np.sqrt(2.0)
    sq_i = 0
    nm_mark = {}
    nm_at = {}
    av_at = {}
    pend_nums = []
    ctx.wait(nc.scalar, "V")
    ctx.wait(nc.tensor, "V")
    ctx.wait(nc.tensor, "SE")
    ctx.wait(nc.tensor, "S")
    QB = NQ // 4
    for qb in range(QB):
        qsl = slice(qb * 512, (qb + 1) * 512)
        for hd in range(16):
            off = 32 * (hd % 4)
            tq = hd // 4
            pnb = (qb * 16 + hd) % 2
            pn = psN[pnb]
            if nm_mark.get(pnb) is not None:
                ctx.wait(nc.tensor, "YV", nm_mark[pnb])
            kfh = kf[:, tq * 2048:(tq + 1) * 2048]
            qfh = qf[:, tq * 2048:(tq + 1) * 2048]
            for kc in range(4 * qb + 4):
                bufS = sq_i % NDs
                bufF = sq_i % NDf
                bufA = sq_i % NDa
                if sq_i >= ND:
                    ctx.wait(nc.tensor, "SQE", sq_i - ND)
                mm = nc.tensor.matmul(psS[bufS][:, 0:512],
                                      kfh[off:off + 16, kc * 128:(kc + 1) * 128],
                                      qfh[off:off + 16, qsl],
                                      start=True, stop=True, tile_position=(off, 0))
                tm = ctx.inc(mm, "T")
                ctx.wait(nc.scalar, "T", tm)
                if sq_i >= ND:
                    ctx.wait(nc.scalar, "AV", av_at[sq_i - ND])
                s1 = ctx.inc(nc.scalar.activation(afb[bufF], psS[bufS][:, 0:512],
                                                  AF.Square, bias=ctx.cst(c_bi), scale=c_sq), "SQE")
                ctx.wait(nc.vector, "SQE", s1)
                if sq_i >= NDa and (sq_i - NDa) in nm_at:
                    ctx.wait(nc.vector, "NM", nm_at[sq_i - NDa])
                adst = atS[bufA]
                jb = kc - 4 * qb
                av = ctx.inc(nc.vector.tensor_scalar(adst, afb[bufF], 0.5, None, ALU.add), "AV")
                if jb >= 0:
                    av = ctx.inc(nc.vector.scalar_tensor_tensor(
                        adst[:, jb * 128:(jb + 1) * 128], afb[bufF][:, jb * 128:(jb + 1) * 128],
                        ctx.cst(0.5), sb["masktri"][:, :], ALU.add, ALU.mult), "AV")
                av_at[sq_i] = av
                vsl = (vA if hd < 8 else vB)[:, kc * 520 + (hd % 8) * 65:kc * 520 + (hd % 8) * 65 + 65]

                def mk_num(kc=kc, bufA=bufA, av=av, vsl=vsl, si=sq_i, qb=qb, pn=pn):
                    def emit_num():
                        ctx.wait(nc.tensor, "AV", av)
                        last_nm = None
                        for j in range(4):
                            if kc <= 4 * qb + j:
                                last_nm = nc.tensor.matmul(pn[:, j * 65:j * 65 + 65],
                                                           atS[bufA][:, j * 128:(j + 1) * 128],
                                                           vsl, start=(kc == 0 and j == 0),
                                                           stop=(kc == 4 * qb + 3 and j == 3))
                        nm_at[si] = ctx.inc(last_nm, "NM")
                    return emit_num
                pend_nums.append(mk_num())
                if len(pend_nums) > 2:
                    pend_nums.pop(0)()
                sq_i += 1
            while pend_nums:
                pend_nums.pop(0)()
            tn = ctx.cnt["NM"]
            ctx.wait(nc.scalar, "NM", tn)
            sd = ctx.inc(nc.scalar.copy(sb["denf"][:, 0:4].rearrange("p (j o) -> p j o", o=1),
                                        pn[:, 0:260].rearrange("p (j e) -> p j e", e=65)[:, :, 64:65]), "S")
            ctx.wait(nc.vector, "S", sd)
            ctx.vsync(nc.vector.tensor_scalar(sb["denf"][:, 4:8], sb["denf"][:, 0:4], 1e-12, None, ALU.add))
            ctx.vsync(nc.vector.reciprocal(sb["denf"][:, 8:12], sb["denf"][:, 4:8]))
            yvv = None
            for j in range(4):
                qc = 4 * qb + j
                ysl = (sb["ysbA"] if hd < 8 else sb["ysbB"])[:, qc * 512 + (hd % 8) * 64:qc * 512 + (hd % 8) * 64 + 64]
                yvv = ctx.inc(nc.vector.tensor_scalar(ysl, pn[:, j * 65:j * 65 + 64],
                                                      sb["denf"][:, 8 + j:9 + j], None, ALU.mult), "YV")
            nc.vector.wait_ge(ctx.sems["YV"], yvv)
            nm_mark[pnb] = yvv
    yf = sb["xcb"]
    psT = [sb["ps_x"], sb["ps_d"][0], sb["ps_d"][1], sb["ps_st0"]]
    wo = sb["wbuf"]
    xres = sb["statsq"]
    ctx.wait(nc.sync, "V", v_h2_done)
    ctx.wait(nc.sync, "SQE")
    ctx.wait(nc.gpsimd, "TW")
    ctx.wait(nc.tensor, "YV")
    ctx.wait(nc.tensor, "SQE")
    tr_mark = []
    res_tok = [None, None]
    wo_rel = [None, None]
    wo_tw = [None, None]
    tr_i = 0
    for s in range(NS):
        W = slice(s * LC, (s + 1) * LC)
        for lt in range(4 * s, 4 * s + 4):
            for mt in range(8):
                ysrc = (sb["ysbA"] if mt < 4 else sb["ysbB"])
                h0_ = (2 * mt) % 8
                src_ap = ysrc[:, lt * 512 + h0_ * 64:lt * 512 + h0_ * 64 + 128]
                buf = tr_i % 4
                if len(tr_mark) >= 4:
                    ctx.wait(nc.tensor, "TRE", tr_mark[-4])
                mm = nc.tensor.transpose(psT[buf].bitcast(BF16)[:, 0:128], src_ap, sb["identb"][:, :])
                tm = ctx.inc(mm, "T")
                ctx.wait(nc.scalar, "T", tm)
                tr_mark.append(ctx.inc(nc.scalar.copy(yf[:, mt * 512 + (lt % 4) * 128:mt * 512 + (lt % 4) * 128 + 128],
                                                      psT[buf].bitcast(BF16)[:, 0:128]), "TRE"))
                tr_i += 1
        ctx.wait(nc.tensor, "TRE")
        for m in range(G8):
            buf = m % 2
            if wo_tw[buf] is not None:
                ctx.wait(nc.gpsimd, "TW", wo_tw[buf])
            wotok = ctx.dgrp(nc.gpsimd, [(wo[:, buf * 1024:(buf + 1) * 1024].rearrange("p (k f) -> p k f", k=8),
                                          hb["wo"][:, m * 128:(m + 1) * 128].rearrange("(k p) f -> p k f", p=128))], "gw")
            if wo_rel[buf] is not None:
                ctx.wait(nc.sync, wo_rel[buf][0], wo_rel[buf][1])
            xtok = ctx.dgrp(nc.sync, [(xres[:, buf * LC:(buf + 1) * LC], hb["xB"][m * 128:(m + 1) * 128, W])], "ld")
            ctx.dwait(nc.tensor, wotok)
            if wo_rel[buf] is not None:
                ctx.wait(nc.tensor, wo_rel[buf][0], wo_rel[buf][1])
            for k in range(G8):
                mm = nc.tensor.matmul(psV[buf][:, 0:LC], wo[:, buf * 1024 + k * 128:buf * 1024 + (k + 1) * 128],
                                      yf[:, k * 512:(k + 1) * 512], start=(k == 0), stop=(k == G8 - 1))
            tm = ctx.inc(mm, "TW")
            wo_tw[buf] = tm
            ctx.wait(nc.vector, "TW", tm)
            ctx.dwait(nc.vector, xtok)
            if res_tok[buf] is not None:
                ctx.dwait(nc.vector, res_tok[buf])
            ve = ctx.inc(nc.vector.tensor_tensor(sb["stage4"][:, buf * LC:(buf + 1) * LC],
                                                 xres[:, buf * LC:(buf + 1) * LC],
                                                 psV[buf][:, 0:LC], ALU.add), "VE")
            wo_rel[buf] = ("VE", ve)
            ctx.wait(nc.sync, "VE", ve)
            res_tok[buf] = ctx.dgrp(nc.sync, [(hb["xA"][m * 128:(m + 1) * 128, W],
                                               sb["stage4"][:, buf * LC:(buf + 1) * LC])], "st")
        ctx.wait(nc.tensor, "VE")
    for t_ in res_tok:
        if t_ is not None:
            ctx.dwait(nc.sync, t_)



def emit_l1f(ctx, sb, hb):
    nc = ctx.nc
    NS = L // LC
    ctx.wait(nc.tensor, "YV")
    ctx.wait(nc.tensor, "S")
    ctx.wait(nc.tensor, "VE")
    ctx.wait(nc.tensor, "TRE")
    ctx.wait(nc.tensor, "SQE")
    ctx.wait(nc.sync, "VE")
    ctx.wait(nc.vector, "NM")
    emit_ln_stats(ctx, sb, hb["xA"], hb["st_r4"], hb["st_m4"], hb["sflat"])
    ctx.inc(nc.vector.memset(sb["ones_bf"][:, :], 1.0), "V")
    ctx.inc(nc.vector.memset(sb["cc_row"][0:1, :], 1.0), "V")
    smtok = ctx.dgrp(nc.sync, [(sb["scb"][:, :], hb["g1b1"][:, :]),
                               (sb["bb1sb"][:, :], hb["bb1h"][:, :]),
                               (sb["amaxx"][:, 0:8], hb["bb2hd"][:, :])], "ld")
    ctx.dwait(nc.vector, smtok)
    ctx.inc(nc.vector.tensor_scalar(sb["bb2h"][:, :], sb["amaxx"][:, 0:8], 0.5, None, ALU.mult), "V")

    vA = sb["xin"].bitcast(BF16)
    vB = sb["delta"].bitcast(BF16)

    def hpsl(g, cs):
        t = vA if g < 4 else vB
        return t[:, (g % 4) * 2048 + cs.start:(g % 4) * 2048 + cs.stop]

    ysbA, ysbB = sb["ysbA"], sb["ysbB"]

    def hhsl(g, cs):
        t = ysbA if g < 4 else ysbB
        return t[:, (g % 4) * 2048 + cs.start:(g % 4) * 2048 + cs.stop]

    rb4, mb4 = sb["statx"][:, 0:LC], sb["statx"][:, LC:2 * LC]
    sqb_t = sb["xc"]
    ps0, ps1 = sb["ps_st0"], sb["ps_st1"]
    t1mark = []
    st_all = []
    st_tok = None
    ctx.wait(nc.tensor, "V")
    for s in range(NS):
        W = slice(s * LC, (s + 1) * LC)
        ctx.wait(nc.sync, "V")
        btok = ctx.dgrp(nc.sync, [(rb4, hb["st_r4"][W].unsqueeze(0).partition_broadcast(128)),
                                  (mb4, hb["st_m4"][W].unsqueeze(0).partition_broadcast(128))], "ld")
        ctx.dwait(nc.vector, btok)
        for g in range(G8):
            h = g % 2
            xsl = sb["statsq"][:, h * LC:(h + 1) * LC]
            sq = sqb_t[:, h * LC:(h + 1) * LC]
            ctx.wait(nc.sync, "V")
            xtok = ctx.dgrp(nc.sync, [(xsl, hb["xA"][g * 128:(g + 1) * 128, W])], "ld")
            ctx.dwait(nc.vector, xtok)
            nc.vector.tensor_tensor(sb["stage2"][:, :], xsl, rb4, ALU.mult)
            hv = ctx.inc(nc.vector.tensor_tensor(hpsl(g, W), sb["stage2"][:, :], mb4, ALU.subtract), "V")
            ctx.wait(nc.scalar, "V", hv)
            if len(t1mark) >= 2:
                ctx.wait(nc.scalar, "T", t1mark[-2])
            sqm = ctx.inc(nc.scalar.activation(sq, hpsl(g, W), AF.Square), "S")
            ctx.wait(nc.tensor, "V", hv)
            ctx.inc(nc.tensor.matmul(ps0[0:1, 0:LC], sb["ones_bf"][:, 0:1], hpsl(g, W),
                                     start=(g == 0), stop=(g == G8 - 1)), "T")
            ctx.wait(nc.tensor, "S", sqm)
            t1mark.append(ctx.inc(nc.tensor.matmul(ps1[0:1, 0:LC], sb["ones_f32"][:, 0:1], sq,
                                                   start=(g == 0), stop=(g == G8 - 1)), "T"))
        ctx.wait(nc.scalar, "T")
        if st_tok is not None:
            ctx.dwait(nc.scalar, st_tok)
        ctx.inc(nc.scalar.copy(sb["sums2"][0:1, 0:LC], ps0[0:1, 0:LC]), "S")
        se = ctx.inc(nc.scalar.copy(sb["sums2"][0:1, LC:2 * LC], ps1[0:1, 0:LC]), "S")
        ctx.wait(nc.tensor, "S", se)
        ctx.wait(nc.sync, "S", se)
        st_tok = ctx.dgrp(nc.sync, [
            (hb["sflat"][s * LC:(s + 1) * LC].unsqueeze(0), sb["sums2"][0:1, 0:LC]),
            (hb["sflat"][L + s * LC:L + (s + 1) * LC].unsqueeze(0), sb["sums2"][0:1, LC:2 * LC])], "st")
        st_all.append(st_tok)
    for t_ in st_all:
        ctx.dwait(nc.sync, t_)
    _ln_tail(ctx, sb, hb["st_r5"], hb["st_m5"], hb["sflat"])

    w1t = [sb["scanA"].bitcast(BF16), sb["scanB"].bitcast(BF16),
           sb["Bb"].bitcast(BF16), sb["Cb"].bitcast(BF16)]

    def w1sl(kb, cs):
        return w1t[kb // 2][:, (kb % 2) * 4096 + cs.start:(kb % 2) * 4096 + cs.stop]

    xq = sb["xcur"].bitcast(BF16)
    psB = sb["ps_w"]
    psD3 = [sb["ps_w"][0], sb["ps_w"][1], sb["ps_x"]]
    srow = [sb["sums2"][0:1, 0:1024], sb["statx"][0:1, 0:1024], sb["statsq"][0:1, 0:1024]]
    c128 = ctx.cst(128.0)
    bc_rel = [None, None, None]
    s_rel = [None, None, None]
    pi = 0
    hi = 0
    for kb in range(G8):
        ctx.wait(nc.gpsimd, "VE")
        qtok = ctx.dgrp(nc.gpsimd, [(xq[:, :], hb["q1h"][kb * 128:(kb + 1) * 128, :])], "gw")
        for piece in range(4):
            sl = pi % 3
            pi += 1
            if s_rel[sl] is not None:
                ctx.wait(nc.sync, "T", s_rel[sl])
            stok = ctx.dgrp(nc.sync, [(srow[sl],
                                       hb["s1h"][kb, piece * 1024:(piece + 1) * 1024].unsqueeze(0))], "ld")
            ctx.dwait(nc.tensor, stok)
            for half in range(2):
                buf = hi % 3
                hi += 1
                if bc_rel[buf] is not None:
                    ctx.wait(nc.tensor, bc_rel[buf][0], bc_rel[buf][1])
                tm = ctx.inc(nc.tensor.matmul(psD3[buf][:, 0:512], sb["cc_row"][0:1, :],
                                              srow[sl][0:1, half * 512:(half + 1) * 512],
                                              start=True, stop=True), "T")
                s_rel[sl] = tm
                ctx.wait(nc.vector, "T", tm)
                ctx.dwait(nc.vector, qtok)
                cs = slice(piece * 1024 + half * 512, piece * 1024 + half * 512 + 512)
                ve = ctx.inc(nc.vector.scalar_tensor_tensor(w1sl(kb, cs), xq[:, cs], c128,
                                                            psD3[buf][:, 0:512],
                                                            ALU.subtract, ALU.mult), "VE")
                bc_rel[buf] = ("VE", ve)

    w2st = []
    h0b = sb["h0"]
    h0_rel = [None, None]
    for kb2 in range(32):
        q = kb2 % 4
        ctx.wait(nc.gpsimd, "VE")
        qtok = ctx.dgrp(nc.gpsimd, [(xq[:, q * 1024:(q + 1) * 1024],
                                     hb["q2h"][kb2 * 128:(kb2 + 1) * 128, :])], "gw")
        sl = pi % 3
        pi += 1
        if s_rel[sl] is not None:
            ctx.wait(nc.sync, "T", s_rel[sl])
        stok = ctx.dgrp(nc.sync, [(srow[sl], hb["s2h"][kb2, :].unsqueeze(0))], "ld")
        ctx.dwait(nc.tensor, stok)
        hbuf = kb2 % 2
        if h0_rel[hbuf] is not None:
            ctx.dwait(nc.vector, h0_rel[hbuf])
        ve = None
        for half in range(2):
            buf = hi % 3
            hi += 1
            if bc_rel[buf] is not None:
                ctx.wait(nc.tensor, bc_rel[buf][0], bc_rel[buf][1])
            tm = ctx.inc(nc.tensor.matmul(psD3[buf][:, 0:512], sb["cc_row"][0:1, :],
                                          srow[sl][0:1, half * 512:(half + 1) * 512],
                                          start=True, stop=True), "T")
            s_rel[sl] = tm
            ctx.wait(nc.vector, "T", tm)
            ctx.dwait(nc.vector, qtok)
            ve = ctx.inc(nc.vector.scalar_tensor_tensor(
                h0b[:, hbuf * 1024 + half * 512:hbuf * 1024 + half * 512 + 512],
                xq[:, q * 1024 + half * 512:q * 1024 + half * 512 + 512], c128,
                psD3[buf][:, 0:512], ALU.subtract, ALU.mult), "VE")
            bc_rel[buf] = ("VE", ve)
        ctx.wait(nc.sync, "VE", ve)
        tok = ctx.dgrp(nc.sync, [(hb["w2scr"][kb2 * 128:(kb2 + 1) * 128, :],
                                  h0b[:, hbuf * 1024:(hbuf + 1) * 1024])], "st")
        h0_rel[hbuf] = tok
        w2st.append(tok)

    ctx.wait(nc.sync, "T")
    for s in range(NS):
        W = slice(s * LC, (s + 1) * LC)
        ctx.wait(nc.sync, "V")
        btok = ctx.dgrp(nc.sync, [(rb4, hb["st_r5"][W].unsqueeze(0).partition_broadcast(128)),
                                  (mb4, hb["st_m5"][W].unsqueeze(0).partition_broadcast(128))], "ld")
        ctx.dwait(nc.vector, btok)
        for g in range(G8):
            nc.vector.tensor_tensor(sb["stage2"][:, :], hpsl(g, W), rb4, ALU.mult)
            nc.vector.tensor_tensor(sb["stage2"][:, :], sb["stage2"][:, :], mb4, ALU.subtract)
            ctx.inc(nc.vector.tensor_scalar(hhsl(g, W), sb["stage2"][:, :],
                                            sb["scb"][:, g:g + 1], sb["scb"][:, 8 + g:9 + g],
                                            ALU.mult, ALU.add), "V")

    xcurb = xq
    q8hb = sb["q8h"].bitcast(BF16)

    def h3sl(j):
        if j < 8:
            return sb["yout"][:, j * 512:(j + 1) * 512]
        if j < 16:
            return sb["xcb"][:, (j - 8) * 512:(j - 7) * 512]
        if j < 24:
            return xcurb[:, (j - 16) * 512:(j - 15) * 512]
        if j < 28:
            return h0b[:, (j - 24) * 512:(j - 23) * 512]
        return q8hb[:, (j - 28) * 512:(j - 27) * 512]

    psZ = [sb["ps_st0"], sb["ps_st1"]]
    ctx.wait(nc.tensor, "V")
    ctx.wait(nc.tensor, "VE")
    ctx.wait(nc.scalar, "VE")
    ctx.wait(nc.gpsimd, "TW")
    ctx.wait(nc.sync, "V")
    for t_ in w2st:
        ctx.dwait(nc.gpsimd, t_)
    ctx.dwait(nc.scalar, w2st[-1])
    ctx.dwait(nc.scalar, w2st[-2])
    bank_rel = [None, None]
    w_tw = [None, None, None]
    wslots = [sb["wbuf2"][:, 0:2048], sb["wbuf2"][:, 2048:4096], sb["wbuf"][:, 0:2048]]
    wi = 0
    xres_rel = [None, None]
    fc2_rel = [None, None]
    out_slots = [sb["stage2"][:, :], sb["atT"].bitcast(F32)[:, 0:LC]]
    out_gate = [None, None]
    st_out = []
    for s in range(NS):
        W = slice(s * LC, (s + 1) * LC)
        for m in range(32):
            buf = m % 2
            if bank_rel[buf] is not None:
                ctx.wait(nc.tensor, bank_rel[buf][0], bank_rel[buf][1])
            for kb in range(G8):
                mm = nc.tensor.matmul(psB[buf][:, 0:LC], w1sl(kb, slice(m * 128, (m + 1) * 128)),
                                      hhsl(kb, W), start=(kb == 0), stop=(kb == G8 - 1))
            tm = ctx.inc(mm, "TW")
            ctx.wait(nc.scalar, "TW", tm)
            se = ctx.inc(nc.scalar.activation(h3sl(m), psB[buf][:, 0:LC], AF.Silu,
                                              bias=sb["bb1sb"][:, m:m + 1]), "SE")
            bank_rel[buf] = ("SE", se)
        se_done = ctx.cnt["SE"]
        for m2 in range(G8):
            buf2 = m2 % 2
            if xres_rel[buf2] is not None:
                ctx.wait(nc.sync, xres_rel[buf2][0], xres_rel[buf2][1])
            xr = ctx.dgrp(nc.sync, [(sb["statsq"][:, buf2 * LC:(buf2 + 1) * LC],
                                     hb["xA"][m2 * 128:(m2 + 1) * 128, W])], "ld")
            tm2 = None
            for half in range(2):
                wb = wi % 3
                wi += 1
                if w_tw[wb] is not None:
                    ctx.wait(nc.gpsimd, "TW", w_tw[wb])
                wtok = ctx.dgrp(nc.gpsimd, [
                    (wslots[wb].rearrange("p (k f) -> p k f", f=128),
                     hb["w2scr"][half * 2048:(half + 1) * 2048,
                                 m2 * 128:(m2 + 1) * 128].rearrange("(k p) f -> p k f", p=128))], "gw")
                ctx.dwait(nc.tensor, wtok)
                if half == 0:
                    ctx.wait(nc.tensor, "SE", se_done)
                    if fc2_rel[buf2] is not None:
                        ctx.wait(nc.tensor, fc2_rel[buf2][0], fc2_rel[buf2][1])
                for t in range(16):
                    kb2 = half * 16 + t
                    mm = nc.tensor.matmul(psZ[buf2][:, 0:LC],
                                          wslots[wb][:, t * 128:(t + 1) * 128],
                                          h3sl(kb2), start=(kb2 == 0), stop=(kb2 == 31))
                tm2 = ctx.inc(mm, "TW")
                w_tw[wb] = tm2
            ctx.wait(nc.vector, "TW", tm2)
            ctx.dwait(nc.vector, xr)
            u = sb["statx"][:, buf2 * LC:(buf2 + 1) * LC]
            ve = ctx.inc(nc.vector.scalar_tensor_tensor(u, psZ[buf2][:, 0:LC], ctx.cst(0.5),
                                                        sb["statsq"][:, buf2 * LC:(buf2 + 1) * LC],
                                                        ALU.mult, ALU.add), "VE")
            fc2_rel[buf2] = ("VE", ve)
            xres_rel[buf2] = ("VE", ve)
            if out_gate[buf2] is not None:
                ctx.dwait(nc.vector, out_gate[buf2])
            yv = ctx.inc(nc.vector.scalar_tensor_tensor(out_slots[buf2], hpsl(m2, W),
                                                        sb["bb2h"][:, m2:m2 + 1], u,
                                                        ALU.add, ALU.add), "YV")
            ctx.wait(nc.sync, "YV", yv)
            out_gate[buf2] = ctx.dgrp(nc.sync, [(hb["out"][m2 * 128:(m2 + 1) * 128, W], out_slots[buf2])], "st")
            st_out.append(out_gate[buf2])
    for t_ in st_out[-4:]:
        ctx.dwait(nc.sync, t_)


def build(stage="full"):
    nc = bass.Bass("TRN2", target_bir_lowering=False, detect_race_conditions=RACECHECK)
    ctx = Ctx(nc)
    ctx.mkpool("ld", 8)
    ctx.mkpool("st", 6)
    ctx.mkpool("gw", 4)
    hb = {}
    hb["x0T"] = nc.dram_tensor("x0T", [D, L], F32, kind="ExternalInput").ap()
    hb["win"] = nc.dram_tensor("win", [D, 2 * DI], F32, kind="ExternalInput").ap()
    hb["wx"] = nc.dram_tensor("wx", [DI, 96], F32, kind="ExternalInput").ap()
    hb["wdt"] = nc.dram_tensor("wdt", [DTR, DI], F32, kind="ExternalInput").ap()
    hb["dtb"] = nc.dram_tensor("dtb", [128, 16], F32, kind="ExternalInput").ap()
    hb["wout"] = nc.dram_tensor("wout", [DI, D], F32, kind="ExternalInput").ap()
    hb["winbf"] = nc.dram_tensor("winbf", [D, 2 * DI], BF16).ap()
    hb["xscr"] = nc.dram_tensor("xscr", [DI, L], BF16).ap()
    hb["zscr"] = nc.dram_tensor("zscr", [DI, L], BF16).ap()
    hb["youtscr"] = nc.dram_tensor("youtscr", [DI, L], BF16).ap()
    hb["woutbf"] = nc.dram_tensor("woutbf", [DI, D], BF16).ap()
    hb["dwk"] = nc.dram_tensor("dwk", [128, 64], F32, kind="ExternalInput").ap()
    hb["dwb"] = nc.dram_tensor("dwb", [128, 16], F32, kind="ExternalInput").ap()
    hb["alog"] = nc.dram_tensor("alog", [128, 256], F32, kind="ExternalInput").ap()
    hb["dskip"] = nc.dram_tensor("dskip", [128, 16], F32, kind="ExternalInput").ap()
    hb["out"] = nc.dram_tensor("out", [D, L], F32, kind="ExternalOutput").ap()
    hb["xA"] = nc.dram_tensor("xA", [D, L], F32).ap()
    hb["xB"] = nc.dram_tensor("xB", [D, L], F32).ap()
    hb["Bflat"] = nc.dram_tensor("Bflat", [NST, L], BF16).ap()
    hb["Cflat"] = nc.dram_tensor("Cflat", [NST, L], BF16).ap()
    kd = "ExternalOutput" if DEBUG else "Internal"
    hb["st_r1"] = nc.dram_tensor("st_r1", [L], F32, kind=kd).ap()
    hb["st_r2"] = nc.dram_tensor("st_r2", [L], F32).ap()
    hb["st_m2"] = nc.dram_tensor("st_m2", [L], F32).ap()
    hb["st_r2b"] = nc.dram_tensor("st_r2b", [L], F32).ap()
    hb["st_r3"] = nc.dram_tensor("st_r3", [L], F32).ap()
    hb["st_m3"] = nc.dram_tensor("st_m3", [L], F32).ap()
    hb["st_r4"] = nc.dram_tensor("st_r4", [L], F32).ap()
    hb["st_m4"] = nc.dram_tensor("st_m4", [L], F32).ap()
    hb["st_r5"] = nc.dram_tensor("st_r5", [L], F32).ap()
    hb["st_m5"] = nc.dram_tensor("st_m5", [L], F32).ap()
    hb["q1h"] = nc.dram_tensor("q1h", [D, 4096], U8, kind="ExternalInput").ap()
    hb["s1h"] = nc.dram_tensor("s1h", [8, 4096], F32, kind="ExternalInput").ap()
    hb["q2h"] = nc.dram_tensor("q2h", [4096, D], U8, kind="ExternalInput").ap()
    hb["s2h"] = nc.dram_tensor("s2h", [32, D], F32, kind="ExternalInput").ap()
    hb["g1b1"] = nc.dram_tensor("g1b1", [128, 16], F32, kind="ExternalInput").ap()
    hb["bb1h"] = nc.dram_tensor("bb1h", [128, 32], F32, kind="ExternalInput").ap()
    hb["bb2hd"] = nc.dram_tensor("bb2hd", [128, 8], F32, kind="ExternalInput").ap()
    hb["w2scr"] = nc.dram_tensor("w2scr", [4096, D], BF16).ap()
    hb["wv"] = nc.dram_tensor("wv", [D, D], F32, kind="ExternalInput").ap()
    hb["wo"] = nc.dram_tensor("wo", [D, D], F32, kind="ExternalInput").ap()
    hb["wqp"] = nc.dram_tensor("wqp", [D, 512], F32, kind="ExternalInput").ap()
    hb["wkp"] = nc.dram_tensor("wkp", [D, 512], F32, kind="ExternalInput").ap()
    hb["maskt"] = nc.dram_tensor("maskt", [128, 128], F32, kind="ExternalInput").ap()
    hb["identb_h"] = nc.dram_tensor("identb_h", [128, 128], F32, kind="ExternalInput").ap()
    hb["st_m2b"] = nc.dram_tensor("st_m2b", [L], F32).ap()
    hb["pmax"] = nc.dram_tensor("pmax", [128], F32).ap()
    hb["scal"] = nc.dram_tensor("scal", [16], F32, kind=kd).ap()
    hb["cc_in"] = nc.dram_tensor("cc_in", [1, 128], F32).ap()
    hb["cc_out"] = nc.dram_tensor("cc_out", [1, 128], F32).ap()
    hb["w0f"] = nc.dram_tensor("w0f", [D, D], F32, kind="ExternalInput").ap()
    hb["b0fT"] = nc.dram_tensor("b0fT", [128, 8], F32, kind="ExternalInput").ap()
    hb["st_m1"] = nc.dram_tensor("st_m1", [L], F32, kind=kd).ap()
    hb["sflat"] = nc.dram_tensor("sflat", [2 * L], F32, kind=kd).ap()
    if DEBUG:
        for n in ("d_xc", "d_delta", "d_z", "d_yout", "d_h0", "d_xin"):
            hb[n] = nc.dram_tensor(n, [DI, L], F32, kind="ExternalOutput").ap()
        hb["d_wq"] = nc.dram_tensor("d_wq", [D, D], F32, kind="ExternalOutput").ap()
    if DEBUG_H2:
        hb["d_h2"] = nc.dram_tensor("d_h2", [128, L], F32, kind="ExternalOutput").ap()
        hb["d_h2f"] = nc.dram_tensor("d_h2f", [8, 128, L], F32, kind="ExternalOutput").ap()
        hb["d_v"] = nc.dram_tensor("d_v", [2, 128, (L // 128) * 520], F32, kind="ExternalOutput").ap()
        hb["d_qf"] = nc.dram_tensor("d_qf", [4, 128, L], F32, kind="ExternalOutput").ap()
        hb["d_kf"] = nc.dram_tensor("d_kf", [4, 128, L], F32, kind="ExternalOutput").ap()
        hb["d_ysb"] = nc.dram_tensor("d_ysb", [2, 128, (L // 128) * 512], F32, kind="ExternalOutput").ap()
        hb["d_dbg"] = nc.dram_tensor("d_dbg", [5, 128, LC], F32, kind="ExternalOutput").ap()

        hb["d_hq"] = nc.dram_tensor("d_hq", [D, LC], F32, kind="ExternalOutput").ap()

    sb = {}
    def sbuf(name, shape, dt=F32):
        sb[name] = nc.alloc_sbuf_tensor("sb_" + name, shape, dt).ap()

    sbuf("ones_f32", [128, 1])
    sbuf("consts", [128, 32])
    ctx.const_tile = sb["consts"]
    for c_ in (EPS_LN, 1e-12, 448.0, 0.5, 224.0, 128.0):
        ctx.cst(c_)
    sbuf("Aneg", [128, 256]); sbuf("dwk", [128, 64]); sbuf("dwb", [128, 16])
    sbuf("dtbias", [128, 16]); sbuf("dskip", [128, 16])
    sbuf("wx_sb", [128, 16 * 96], BF16); sbuf("wdt_sb", [64, DI], BF16)
    sbuf("statx", [128, 2 * LC]); sbuf("statsq", [128, 2 * LC]); sbuf("sums2", [1, 2 * LC])
    sbuf("xcur", [128, G8 * LC0]); sbuf("stage2", [128, 2 * LC0]); sbuf("h0", [128, G8 * LC0], BF16)
    sbuf("rb", [128, LC0]); sbuf("mb", [128, LC0])
    sbuf("xin", [128, 4160]); sbuf("zsilu", [128, G16 * LC0])
    sbuf("xc", [128, G16 * LC0]); sbuf("xcb", [128, G16 * LC0], BF16)
    sbuf("delta", [128, 4160]); sbuf("tmpc", [128, LC0]); sbuf("tfix", [128, 16])
    sbuf("wbuf", [128, 2 * 1024], BF16); sbuf("wbuf2", [128, 2 * 2048], BF16)
    sbuf("dt16", [64, LC0], BF16); sbuf("bc96", [96, LC0])
    sbuf("Bb", [128, 16 * LC0]); sbuf("Cb", [128, 16 * LC0])
    sbuf("scanA", [128, 16 * LC0]); sbuf("scanB", [128, 16 * LC0])
    sbuf("carry", [128, 256]); sbuf("yg", [128, LC0]); sbuf("yout", [128, G16 * LC0], BF16)
    sbuf("sc", [1, 16]); sbuf("cc_row", [1, 128]); sbuf("amaxx", [128, 16]); sbuf("scb", [128, 16])
    sbuf("attnf", [128, 256]); sbuf("atT", [128, 1024], BF16); sbuf("denf", [128, 12])
    sbuf("masktri", [128, 128]); sbuf("identb", [128, 128], BF16)
    sb["ysbA"] = sb["zsilu"].bitcast(BF16)
    sb["ysbB"] = sb["xc"].bitcast(BF16)
    sb["stage4"] = sb["statx"]
    sbuf("q8t", [128, 1024], FP8); sbuf("q8h", [128, G8 * LC], FP8)
    sbuf("b0f", [128, 8])
    sbuf("ones_bf", [128, 1], BF16); sbuf("bb1sb", [128, 32]); sbuf("bb2h", [128, 8])
    sb["xcur2"] = sb["scanA"]
    sb["tmpc2"] = sb["statx"][:, 0:LC]
    sb["rb2"] = sb["statsq"][:, 0:LC]
    sb["mb2"] = sb["statsq"][:, LC:2 * LC]
    sb["hqb"] = sb["xc"].bitcast(BF16)
    sb["wqb"] = sb["delta"].bitcast(BF16)
    sb["obf"] = sb["h0"]
    sb["stage3"] = sb["zsilu"]

    sb["ps_st0"] = nc.alloc_psum_tensor("ps_st0", [128, 512], F32).ap()
    sb["ps_st1"] = nc.alloc_psum_tensor("ps_st1", [128, 512], F32).ap()
    sb["ps_w"] = [nc.alloc_psum_tensor(f"ps_w{i}", [128, 512], F32).ap() for i in range(2)]
    sb["ps_x"] = nc.alloc_psum_tensor("ps_x", [128, 512], F32).ap()
    sb["ps_d"] = [nc.alloc_psum_tensor(f"ps_d{i}", [128, 256], F32).ap() for i in range(2)]

    ctx.vsync(nc.vector.memset(sb["ones_f32"][:, :], 1.0))
    small = [(sb["b0f"][:, :], hb["b0fT"][:, :]),
             (sb["Aneg"][:, :], hb["alog"][:, :]),
             (sb["dwk"][:, :], hb["dwk"][:, :]),
             (sb["dwb"][:, :], hb["dwb"][:, :]),
             (sb["dtbias"][:, :], hb["dtb"][:, :]),
             (sb["dskip"][:, :], hb["dskip"][:, :])]
    small.append((sb["masktri"][:, :], hb["maskt"][:, :]))
    tok_small = ctx.dgrp(nc.sync, small, "ld")
    idtok = ctx.dgrp(nc.gpsimd, [(sb["identb"][:, :], hb["identb_h"][:, :])], "gw")
    ctx.dwait(nc.tensor, idtok)
    if not NOGW:
        hb["w2d_tok"] = ctx.dgrp(ctx.nc.gpsimd, [(hb["winbf"], hb["win"]), (hb["woutbf"], hb["wout"])], "gw")
        wtok = ctx.dgrp(nc.gpsimd, [
            (sb["wx_sb"][:, :].rearrange("p (g f) -> p g f", f=96),
             hb["wx"].rearrange("(g p) f -> p g f", p=128)),
            (sb["wdt_sb"][:, :], hb["wdt"][:, :])], "gw")
    ctx.dwait(nc.scalar, tok_small)
    s = ctx.inc(nc.scalar.activation(sb["Aneg"][:, :], sb["Aneg"][:, :], AF.Exp), "S")
    ctx.wait(nc.vector, "S", s)
    ctx.inc(nc.vector.tensor_scalar(sb["Aneg"][:, :], sb["Aneg"][:, :], -1.0, None, ALU.mult), "V")
    if not NOGW:
        ctx.dwait(nc.tensor, wtok)

    if CUT >= 1:
        emit_ln_stats(ctx, sb, hb["x0T"], hb["st_r1"], hb["st_m1"], hb["sflat"]) if CUT < 2 else None
    if CUT >= 2:
        emit_ln_stats(ctx, sb, hb["x0T"], hb["st_r1"], hb["st_m1"], hb["sflat"])
        emit_l0a(ctx, sb, hb)
    if stage in ("l0f", "l1a", "l1f", "full"):
        emit_l0f(ctx, sb, hb)
    if stage in ("l1a", "l1f", "full"):
        emit_l1a(ctx, sb, hb)
    if stage in ("l1f", "full"):
        emit_l1f(ctx, sb, hb)

    copy_src = {"l0a": "xA", "l0f": "xB", "l1a": "xA"}.get(stage)
    if copy_src is not None:
        tcp = sb["statx"]
        for c in range(L // LC):
            W = slice(c * LC, (c + 1) * LC)
            for g in range(G8):
                tok = ctx.dgrp(nc.sync, [(tcp[:, 0:LC], hb[copy_src][g * 128:(g + 1) * 128, W])], "ld")
                ctx.dwait(nc.sync, tok)
                tok = ctx.dgrp(nc.sync, [(hb["out"][g * 128:(g + 1) * 128, W], tcp[:, 0:LC])], "st")
                ctx.dwait(nc.sync, tok)
    return nc


def host_prep(inputs, core):
    m = {}
    m["x0T"] = np.ascontiguousarray(np.asarray(inputs["L0_attn_hidden_states"])[core].T)
    m["win"] = np.asarray(inputs["L0_attn_Win"])
    m["wx"] = np.asarray(inputs["L0_attn_Wx"])
    m["wdt"] = np.asarray(inputs["L0_attn_Wdt"])
    m["dtb"] = np.ascontiguousarray(np.asarray(inputs["L0_attn_dt_bias"]).reshape(16, 128).T)
    m["wout"] = np.asarray(inputs["L0_attn_Wout"])
    m["dwk"] = np.ascontiguousarray(
        np.asarray(inputs["L0_attn_dw_kernel"]).T.reshape(16, 128, 4).transpose(1, 0, 2).reshape(128, 64))
    m["dwb"] = np.ascontiguousarray(np.asarray(inputs["L0_attn_dw_bias"]).reshape(16, 128).T)
    m["alog"] = np.ascontiguousarray(
        np.asarray(inputs["L0_attn_A_log"]).reshape(16, 128, 16).transpose(1, 0, 2).reshape(128, 256))
    m["w0f"] = np.ascontiguousarray(np.asarray(inputs["L0_ffn_weight"]).T)
    m["wv"] = np.ascontiguousarray(np.asarray(inputs["L1_attn_Wv"]).T)
    m["wo"] = np.ascontiguousarray(np.asarray(inputs["L1_attn_Wo"]).T)
    m["q1h"] = np.ascontiguousarray(np.asarray(inputs["L1_ffn_fc1_q"]).astype(np.uint8))
    m["s1h"] = np.asarray(inputs["L1_ffn_fc1_scales"])
    m["q2h"] = np.ascontiguousarray(np.asarray(inputs["L1_ffn_fc2_q"]).astype(np.uint8))
    m["s2h"] = np.asarray(inputs["L1_ffn_fc2_scales"])
    m["g1b1"] = np.ascontiguousarray(np.concatenate([
        np.asarray(inputs["L1_ffn_ln_gamma"]).reshape(8, 128).T,
        np.asarray(inputs["L1_ffn_ln_beta"]).reshape(8, 128).T], axis=1))
    m["bb1h"] = np.ascontiguousarray(np.asarray(inputs["L1_ffn_fc1_bias"]).reshape(32, 128).T)
    m["bb2hd"] = np.ascontiguousarray(np.asarray(inputs["L1_ffn_fc2_bias"]).reshape(8, 128).T)
    def padqk(w):
        wt = np.asarray(w).T.reshape(1024, 16, 16)
        out = np.zeros((1024, 16, 32), np.float32)
        out[:, :, :16] = wt
        return np.ascontiguousarray(out.reshape(1024, 512))
    m["wqp"] = padqk(inputs["L1_attn_Wq"])
    m["wkp"] = padqk(inputs["L1_attn_Wk"])
    m["maskt"] = np.triu(np.ones((128, 128), np.float32))
    m["identb_h"] = np.eye(128, dtype=np.float32)
    m["b0fT"] = np.ascontiguousarray(np.asarray(inputs["L0_ffn_bias"]).reshape(8, 128).T)
    m["dskip"] = np.ascontiguousarray(np.asarray(inputs["L0_attn_Dskip"]).reshape(16, 128).T)
    return m


def kernel(**inputs):
    nc = build(STAGE)
    in_maps = [host_prep(inputs, b) for b in range(8)]
    res = run_bass_kernel_spmd(nc, in_maps, core_ids=list(range(8)))
    out = np.stack([np.ascontiguousarray(r["out"].T) for r in res.results])
    return out.astype(np.float32)


# version 12 (final)
# speedup vs baseline: 1.0276x; 1.0153x over previous
import numpy as np
import concourse.bass as bass
import concourse.mybir as mybir
from concourse.bass_utils import run_bass_kernel_spmd

F32 = mybir.dt.float32
BF16 = mybir.dt.bfloat16
FP8 = mybir.dt.float8e4
I32 = mybir.dt.int32
U8 = mybir.dt.uint8
AF = mybir.ActivationFunctionType
ALU = mybir.AluOpType
AX = mybir.AxisListType

L = 2048
D = 1024
DI = 2048
NST = 16
DTR = 64
G8 = 8
G16 = 16
LC0 = 256
LC = 512
EPS_LN = 1e-5
STAGE = "full"
DEBUG = False
DEBUG_H2 = False
RACECHECK = False
CUT = 99
NOGW = False
NCORES = 8


class Ctx:
    def __init__(self, nc):
        self.nc = nc
        self.sems = {}
        self.cnt = {}
        self.consts = {}
        self.const_tile = None
        self.pools = {}

    def cst(self, value):
        if value not in self.consts:
            i = len(self.consts)
            assert i < 32
            self.inc(self.nc.vector.memset(self.const_tile[:, i:i + 1], float(value)), "V")
            self.consts[value] = self.const_tile[:, i:i + 1]
        return self.consts[value]

    def sem(self, name):
        if name not in self.sems:
            self.sems[name] = self.nc.alloc_semaphore("sem_" + name)
            self.cnt[name] = 0
        return self.sems[name]

    def inc(self, instr, name, amt=1):
        instr.then_inc(self.sem(name), amt)
        self.cnt[name] += amt
        return self.cnt[name]

    def wait(self, eng, name, val=None):
        self.sem(name)
        v = self.cnt[name] if val is None else val
        if v > 0:
            eng.wait_ge(self.sems[name], v)

    def mkpool(self, name, n):
        self.pools[name] = [[[self.nc.alloc_semaphore(f"dp_{name}{i}"), 0] for i in range(n)], 0]

    def dgrp(self, q, transfers, pool, slow=False):
        pl = self.pools[pool]
        slot = pl[1]
        sem, used = pl[0][slot]
        pl[1] = (slot + 1) % len(pl[0])
        if used:
            q.wait_ge(sem, used * 16)
        for out, in_ in transfers:
            q.dma_start(out=out, in_=in_, allow_slow_non_contiguous=slow).then_inc(sem, 16)
            used += 1
        pl[0][slot][1] = used
        return (sem, used * 16)

    def dwait(self, eng, tok):
        eng.wait_ge(tok[0], tok[1])

    def vsync(self, instr):
        v = self.inc(instr, "V")
        self.nc.vector.wait_ge(self.sems["V"], v)
        return v

    @staticmethod
    def colsT(tile, flat, ncols):
        return [(tile[:, g:g + 1], flat[g * 128:(g + 1) * 128].rearrange("(p o) -> p o", o=1))
                for g in range(ncols)]

    @staticmethod
    def colsS(flat, tile, ncols):
        return [(flat[g * 128:(g + 1) * 128].rearrange("(p o) -> p o", o=1), tile[:, g:g + 1])
                for g in range(ncols)]


def emit_ln_stats(ctx, sb, x_src_hbm, rstd_flat, mrstd_flat, sflat, extra=None):
    nc = ctx.nc
    NCh = L // LC
    xb, sqb, sums = sb["statx"], sb["statsq"], sb["sums2"]
    ps0, ps1 = sb["ps_st0"], sb["ps_st1"]
    ctx.wait(nc.tensor, "V")
    t1mark = []
    st_tok = None
    st_all = []
    for c in range(NCh):
        W = slice(c * LC, (c + 1) * LC)
        for g in range(G8):
            h = g % 2
            xs = xb[:, h * LC:(h + 1) * LC]
            sq = sqb[:, h * LC:(h + 1) * LC]
            if len(t1mark) >= 2:
                ctx.wait(nc.sync, "T", t1mark[-2])
            tok = ctx.dgrp(nc.sync, [(xs, x_src_hbm[g * 128:(g + 1) * 128, W])], "ld")
            ctx.dwait(nc.scalar, tok)
            if len(t1mark) >= 2:
                ctx.wait(nc.scalar, "T", t1mark[-2])
            sqm = ctx.inc(nc.scalar.activation(sq, xs, AF.Square), "S")
            ctx.dwait(nc.tensor, tok)
            ctx.inc(nc.tensor.matmul(ps0[0:1, 0:LC], sb["ones_f32"][:, 0:1], xs,
                                     start=(g == 0), stop=(g == G8 - 1)), "T")
            ctx.wait(nc.tensor, "S", sqm)
            t1mark.append(ctx.inc(nc.tensor.matmul(ps1[0:1, 0:LC], sb["ones_f32"][:, 0:1], sq,
                                                   start=(g == 0), stop=(g == G8 - 1)), "T"))
        ctx.wait(nc.scalar, "T")
        if st_tok is not None:
            ctx.dwait(nc.scalar, st_tok)
        ctx.inc(nc.scalar.copy(sums[0:1, 0:LC], ps0[0:1, 0:LC]), "S")
        se = ctx.inc(nc.scalar.copy(sums[0:1, LC:2 * LC], ps1[0:1, 0:LC]), "S")
        ctx.wait(nc.tensor, "S", se)
        ctx.wait(nc.sync, "S", se)
        st_tok = ctx.dgrp(nc.sync, [
            (sflat[c * LC:(c + 1) * LC].unsqueeze(0), sums[0:1, 0:LC]),
            (sflat[L + c * LC:L + (c + 1) * LC].unsqueeze(0), sums[0:1, LC:2 * LC])], "st")
        st_all.append(st_tok)
    for t_ in st_all:
        ctx.dwait(nc.sync, t_)
    _ln_tail(ctx, sb, rstd_flat, mrstd_flat, sflat, extra)


def _ln_tail(ctx, sb, rstd_flat, mrstd_flat, sflat, extra=None):
    nc = ctx.nc
    rowA, rowB = sb["scanA"][0:1, :], sb["scanB"][0:1, :]
    tok = ctx.dgrp(nc.sync, [(rowA[0:1, 0:2 * L], sflat[:].unsqueeze(0))], "ld")
    ctx.dwait(nc.vector, tok)
    mu_row = rowB[0:1, 0:L]
    var_row = rowB[0:1, L:2 * L]
    nc.vector.tensor_scalar(mu_row, rowA[0:1, 0:L], 1.0 / D, None, ALU.mult)
    nc.vector.tensor_scalar(var_row, rowA[0:1, L:2 * L], 1.0 / D, None, ALU.mult)
    nc.vector.tensor_tensor(rowA[0:1, 0:L], mu_row, mu_row, ALU.mult)
    v = ctx.inc(nc.vector.tensor_tensor(var_row, var_row, rowA[0:1, 0:L], ALU.subtract), "V")
    ctx.wait(nc.scalar, "V", v)
    s = ctx.inc(nc.scalar.activation(rowA[0:1, L:2 * L], var_row, AF.Sqrt, bias=ctx.cst(EPS_LN)[0:1, :]), "S")
    ctx.wait(nc.vector, "S", s)
    rstd_row = rowA[0:1, 0:L]
    v = ctx.vsync(nc.vector.reciprocal(rstd_row, rowA[0:1, L:2 * L]))
    mr_row = var_row
    v = ctx.inc(nc.vector.tensor_tensor(mr_row, mu_row, rstd_row, ALU.mult), "V")
    ctx.wait(nc.sync, "V", v)
    tok = ctx.dgrp(nc.sync, [(rstd_flat[:].unsqueeze(0), rstd_row),
                             (mrstd_flat[:].unsqueeze(0), mr_row)], "st")
    if extra is not None:
        extra(tok, dict(rstd=rstd_row, mu=mu_row, mrstd=mr_row, rowB=rowB, rowA=rowA))
    else:
        ctx.dwait(nc.sync, tok)


def emit_l0a(ctx, sb, hb):
    nc = ctx.nc
    NCh = L // LC0
    NS = L // LC
    xin = sb["xin"]
    ctx.vsync(nc.vector.memset(sb["carry"].bitcast(BF16)[:, 0:256], 0.0))
    vA = sb["xin"].bitcast(BF16)
    vB = sb["delta"].bitcast(BF16)

    def h1sl(g, cs):
        t = vA if g < 4 else vB
        return t[:, (g % 4) * 2048 + cs.start:(g % 4) * 2048 + cs.stop]

    rb4, mb4 = sb["statx"][:, 0:LC], sb["statx"][:, LC:2 * LC]
    for s in range(NS):
        W = slice(s * LC, (s + 1) * LC)
        ctx.wait(nc.sync, "V")
        btok = ctx.dgrp(nc.sync, [(rb4, hb["st_r1"][W].unsqueeze(0).partition_broadcast(128)),
                                  (mb4, hb["st_m1"][W].unsqueeze(0).partition_broadcast(128))], "ld")
        ctx.dwait(nc.vector, btok)
        for g in range(G8):
            xsl = sb["statsq"][:, (g % 2) * LC:((g % 2) + 1) * LC]
            ctx.wait(nc.sync, "V")
            xtok = ctx.dgrp(nc.sync, [(xsl, hb["x0T"][g * 128:(g + 1) * 128, W])], "ld")
            ctx.dwait(nc.vector, xtok)
            nc.vector.tensor_tensor(sb["stage2"][:, :], xsl, rb4, ALU.mult)
            ctx.inc(nc.vector.tensor_tensor(h1sl(g, W), sb["stage2"][:, :], mb4, ALU.subtract), "V")

    wbuf, psW = sb["wbuf"], sb["ps_w"]
    h0b = sb["h0"]
    ctx.wait(nc.tensor, "V")
    st_tok4 = [None, None, None, None]
    bank_rel = [None, None]
    tw_mark = []
    ei = 0
    for mt in range(32):
        wb = mt % 2
        if len(tw_mark) >= 2:
            ctx.wait(nc.sync, "TW", tw_mark[-2])
        if mt == 0:
            ctx.dwait(nc.sync, hb["w2d_tok"])
        wtok = ctx.dgrp(nc.sync, [(wbuf[:, wb * 1024:(wb + 1) * 1024].rearrange("p (kg m) -> p kg m", kg=8),
                                   hb["winbf"][:, mt * 128:(mt + 1) * 128].rearrange("(kg p) m -> p kg m", p=128))],
                        "ld")
        ctx.dwait(nc.tensor, wtok)
        tw = None
        for sp in range(NS):
            Wsp = slice(sp * LC, (sp + 1) * LC)
            pbuf = (mt * NS + sp) % 2
            if bank_rel[pbuf] is not None:
                ctx.wait(nc.tensor, bank_rel[pbuf][0], bank_rel[pbuf][1])
            for k in range(G8):
                mm = nc.tensor.matmul(psW[pbuf][:, 0:LC],
                                      wbuf[:, wb * 1024 + k * 128:wb * 1024 + (k + 1) * 128],
                                      h1sl(k, Wsp), start=(k == 0), stop=(k == G8 - 1))
            tw = ctx.inc(mm, "TW")
            ctx.wait(nc.scalar, "TW", tw)
            slot = ei % 4
            ei += 1
            if st_tok4[slot] is not None:
                ctx.dwait(nc.scalar, st_tok4[slot])
            dst = h0b[:, slot * 512:(slot + 1) * 512]
            if mt < 16:
                se = ctx.inc(nc.scalar.copy(dst, psW[pbuf][:, 0:LC]), "SE")
            else:
                se = ctx.inc(nc.scalar.activation(dst, psW[pbuf][:, 0:LC], AF.Silu), "SE")
            bank_rel[pbuf] = ("SE", se)
            ctx.wait(nc.gpsimd, "SE", se)
            tgt = hb["xscr"] if mt < 16 else hb["zscr"]
            st_tok4[slot] = ctx.dgrp(nc.gpsimd, [(tgt[(mt % 16) * 128:(mt % 16 + 1) * 128, Wsp], dst)], "gw")
        tw_mark.append(tw)
    for t_ in st_tok4:
        if t_ is not None:
            ctx.dwait(nc.sync, t_)
            ctx.dwait(nc.gpsimd, t_)

    xinb = vA[:, 0:G16 * (LC0 + 3)]
    zsbf = sb["zsilu"].bitcast(BF16)
    ctx.wait(nc.vector, "TW")
    ctx.vsync(nc.vector.memset(xinb[:, :], 0.0))
    dv_rel = [None, None]
    gp_rel = [0, 0]
    fix_mark = [0, 0]
    pend = None
    ystore = None
    v_gate = ctx.cnt["V"]
    for c in range(NCh):
        W = slice(c * LC0, (c + 1) * LC0)
        t1h = sb["tmpc"]
        zsb = zsbf[:, (c % 2) * G16 * LC0:(c % 2 + 1) * G16 * LC0]
        ctx.wait(nc.sync, "V", v_gate)
        xtok = ctx.dgrp(nc.sync, [(xinb[:, g * (LC0 + 3) + 3:(g + 1) * (LC0 + 3)],
                                   hb["xscr"][g * 128:(g + 1) * 128, W]) for g in range(G16)], "ld")
        ztok = ctx.dgrp(nc.sync, [(zsb[:, :].rearrange("p (g l) -> p g l", g=16),
                                   hb["zscr"][:, W].rearrange("(g p) l -> p g l", p=128))], "ld")
        ctx.dwait(nc.vector, xtok)
        ctx.dwait(nc.vector, ztok)
        if ystore is not None:
            ctx.dwait(nc.vector, ystore)
        xc, xcb, dwk, psX = sb["xc"], sb["xcb"], sb["dwk"], sb["ps_x"]
        t1s = [sb["tmpc"], sb["yg"]]
        ctx.wait(nc.vector, "SE")
        ctx.wait(nc.tensor, "SE")
        conv_s = [None, None]
        for g in range(G16):
            base = g * (LC0 + 3)
            t1 = t1s[g % 2]
            if conv_s[g % 2] is not None:
                ctx.wait(nc.vector, "S", conv_s[g % 2])
            nc.vector.tensor_scalar(t1[:, :], xinb[:, base:base + LC0], dwk[:, g * 4:g * 4 + 1], None, ALU.mult)
            for i in (1, 2, 3):
                ins = nc.vector.scalar_tensor_tensor(t1[:, :], xinb[:, base + i:base + i + LC0],
                                                     dwk[:, g * 4 + i:g * 4 + i + 1], t1[:, :],
                                                     ALU.mult, ALU.add)
            v = ctx.inc(ins, "V")
            ctx.wait(nc.scalar, "V", v)
            s = ctx.inc(nc.scalar.activation(xc[:, g * LC0:(g + 1) * LC0], t1[:, :], AF.Silu,
                                             bias=sb["dwb"][:, g:g + 1]), "S")
            conv_s[g % 2] = s
            ctx.wait(nc.scalar, "S", s)
            s2 = ctx.inc(nc.scalar.copy(xcb[:, g * LC0:(g + 1) * LC0], xc[:, g * LC0:(g + 1) * LC0]), "S")
            ctx.wait(nc.tensor, "S", s2)
            mm = nc.tensor.matmul(psX[0:96, 0:LC0], sb["wx_sb"][:, g * 96:(g + 1) * 96],
                                  xcb[:, g * LC0:(g + 1) * LC0], start=(g == 0), stop=(g == G16 - 1))
        ctx.inc(mm, "T")
        for g in range(G16):
            base = g * (LC0 + 3)
            ctx.inc(nc.vector.tensor_copy(xinb[:, base:base + 3], xinb[:, base + LC0:base + LC0 + 3]), "V")
        v_gate = ctx.cnt["V"]
        dt16, bc96 = sb["dt16"], sb["bc96"]
        ctx.wait(nc.scalar, "T")
        ctx.inc(nc.scalar.copy(dt16[:, :], psX[0:64, 0:LC0]), "SE")
        se = ctx.inc(nc.scalar.copy(bc96[64:96, 0:LC0], psX[64:96, 0:LC0]), "SE")
        ctx.wait(nc.gpsimd, "SE", se)
        fl_tok = ctx.dgrp(nc.gpsimd, [(hb["Bflat"][:, W], bc96[64:80, 0:LC0]),
                                      (hb["Cflat"][:, W], bc96[80:96, 0:LC0])], "gw")
        if CUT < 5:
            continue
        psD, delta = sb["ps_d"], sb["delta"]
        s_mark = []
        ctx.wait(nc.tensor, "SE")
        for g in range(G16):
            if len(s_mark) >= 2:
                ctx.wait(nc.tensor, "S", s_mark[-2])
            t = ctx.inc(nc.tensor.matmul(psD[g % 2][:, 0:LC0], sb["wdt_sb"][:, g * 128:(g + 1) * 128],
                                         dt16[:, :], start=True, stop=True), "T")
            ctx.wait(nc.scalar, "T", t)
            nc.scalar.activation(psD[g % 2][:, 0:LC0], psD[g % 2][:, 0:LC0], AF.Exp,
                                 bias=sb["dtbias"][:, g:g + 1])
            s_mark.append(ctx.inc(nc.scalar.activation(delta[:, g * LC0:(g + 1) * LC0],
                                                       psD[g % 2][:, 0:LC0], AF.Ln, bias=1.0), "S"))
        Bb, Cb = sb["Bb"], sb["Cb"]
        ctx.dwait(nc.sync, fl_tok)
        ctx.wait(nc.sync, "V")
        bc_tok = ctx.dgrp(nc.sync, [
            (sb["Bb"].bitcast(BF16)[:, 0:4096].rearrange("p (n l) -> p n l", n=16),
             hb["Bflat"][:, W].unsqueeze(0).partition_broadcast(128)),
            (sb["Cb"].bitcast(BF16)[:, 0:4096].rearrange("p (n l) -> p n l", n=16),
             hb["Cflat"][:, W].unsqueeze(0).partition_broadcast(128))], "ld")
        if CUT < 6:
            continue
        yg, yout = sb["yg"], sb["yout"]
        dAv = sb["scanA"].bitcast(BF16)
        dBv = sb["scanB"].bitcast(BF16)
        dAb = [dAv[:, 0:4096], dAv[:, 4096:8192]]
        dBb = [dBv[:, 0:4096], dBv[:, 4096:8192]]
        cscr = sb["carry"].bitcast(BF16)
        carry_b = cscr[:, 0:256]
        du_b = cscr[:, 256:512]
        tfix_b = sb["tfix"].bitcast(BF16)[:, 0:16]
        Bb3 = sb["Bb"].bitcast(BF16)[:, 0:4096].rearrange("p (n l) -> p n l", n=16)
        Cb3 = sb["Cb"].bitcast(BF16)[:, 0:4096].rearrange("p (n l) -> p n l", n=16)
        ctx.wait(nc.vector, "S")
        ctx.dwait(nc.vector, bc_tok)
        ctx.dwait(nc.gpsimd, bc_tok)
        for g in range(G16):
            gs = slice(g * LC0, (g + 1) * LC0)
            buf = g % 2
            dA = dAb[buf]
            dA3 = dA.rearrange("p (n l) -> p n l", n=16)
            dB3 = dBb[buf].rearrange("p (n l) -> p n l", n=16)
            cg = carry_b[:, g * 16:(g + 1) * 16]
            if dv_rel[buf] is not None:
                ctx.wait(nc.scalar, "V", dv_rel[buf])
                ctx.wait(nc.scalar, "P", gp_rel[buf])
            for n in range(NST):
                s = ctx.inc(nc.scalar.activation(dA3[:, n, :], delta[:, gs], AF.Exp,
                                                 scale=sb["Aneg"][:, g * 16 + n:g * 16 + n + 1]), "S")
            nc.vector.tensor_tensor(du_b[:, :], delta[:, gs], xc[:, gs], ALU.mult)
            dbu = ctx.inc(nc.vector.tensor_tensor(
                dB3[:, :, :], du_b[:, :].unsqueeze(1).broadcast_to([128, 16, LC0]),
                Bb3[:, :, :], ALU.mult), "V")
            ctx.wait(nc.gpsimd, "S", s)
            ctx.wait(nc.gpsimd, "V", dbu)
            nc.gpsimd.tensor_tensor(tfix_b[:, :], dA3[:, :, 0], cg, ALU.mult)
            nc.gpsimd.tensor_tensor(dB3[:, :, 0], dB3[:, :, 0], tfix_b[:, :], ALU.add)
            fix_mark[buf] = ctx.inc(nc.gpsimd.memset(dA3[:, :, 0], 0.0), "P")
            if pend is not None:
                pend()
                pend = None

            def mk_post(g=g, buf=buf, dA=dA, dA3=dA3, dB3=dB3, cg=cg, gs=gs):
                def post():
                    ctx.wait(nc.vector, "P", fix_mark[buf])
                    vscan = ctx.inc(nc.vector.tensor_tensor_scan(dA[:, :], dA[:, :], dBb[buf][:, :],
                                                                 0.0, ALU.mult, ALU.add), "V")
                    ctx.wait(nc.gpsimd, "V", vscan)
                    gp_rel[buf] = ctx.inc(nc.gpsimd.tensor_copy(cg, dA3[:, :, LC0 - 1]), "P")
                    dv_rel[buf] = ctx.inc(nc.vector.tensor_tensor(dB3[:, :, :], dA3[:, :, :],
                                                                  Cb3[:, :, :], ALU.mult), "V")
                    nc.vector.tensor_tensor(dB3[:, 0:8, :], dB3[:, 0:8, :], dB3[:, 8:16, :], ALU.add)
                    nc.vector.tensor_tensor(dB3[:, 0:4, :], dB3[:, 0:4, :], dB3[:, 4:8, :], ALU.add)
                    nc.vector.tensor_tensor(dB3[:, 0:2, :], dB3[:, 0:2, :], dB3[:, 2:4, :], ALU.add)
                    nc.vector.tensor_tensor(yg[:, :], dB3[:, 0, :], dB3[:, 1, :], ALU.add)
                    nc.vector.scalar_tensor_tensor(yg[:, :], xc[:, gs], sb["dskip"][:, g:g + 1],
                                                   yg[:, :], ALU.mult, ALU.add)
                    ctx.inc(nc.vector.tensor_tensor(yout[:, gs], yg[:, :], zsb[:, gs],
                                                    ALU.mult), "V")
                return post
            pend = mk_post()
        pend()
        pend = None
        vy = ctx.cnt["V"]
        ctx.wait(nc.gpsimd, "V", vy)
        ystore = ctx.dgrp(nc.gpsimd, [(hb["youtscr"][:, W].rearrange("(g p) l -> p g l", p=128),
                                       yout[:, :].rearrange("p (g l) -> p g l", g=16))], "gw")
    ctx.dwait(nc.sync, ystore)

    def ytsl(k, cs):
        t = [vA, vB, sb["xc"].bitcast(BF16), sb["zsilu"].bitcast(BF16)][k // 4]
        return t[:, (k % 4) * 2048 + cs.start:(k % 4) * 2048 + cs.stop]

    ctx.wait(nc.sync, "V")
    ytoks = []
    for k in range(G16):
        ytoks.append(ctx.dgrp(nc.sync, [(ytsl(k, slice(0, L)),
                                         hb["youtscr"][k * 128:(k + 1) * 128, :])], "ld"))
    for t_ in ytoks:
        ctx.dwait(nc.tensor, t_)
    wbuf2, psW2 = sb["wbuf2"], sb["ps_w"]
    w_tw = [None, None]
    bank2 = [None, None]
    res_tok = [None, None]
    xres_rel = [None, None]
    for m in range(G8):
        wb = m % 2
        if w_tw[wb] is not None:
            ctx.wait(nc.sync, "TW", w_tw[wb])
        wtok = ctx.dgrp(nc.sync, [(wbuf2[:, wb * 2048:(wb + 1) * 2048].rearrange("p (kg m) -> p kg m", kg=16),
                                   hb["woutbf"][:, m * 128:(m + 1) * 128].rearrange("(kg p) m -> p kg m", p=128))],
                        "ld")
        ctx.dwait(nc.tensor, wtok)
        tw = None
        for sp in range(NS):
            Wsp = slice(sp * LC, (sp + 1) * LC)
            pbuf = (m * NS + sp) % 2
            if xres_rel[pbuf] is not None:
                ctx.wait(nc.sync, xres_rel[pbuf][0], xres_rel[pbuf][1])
            xr = ctx.dgrp(nc.sync, [(sb["statsq"][:, pbuf * LC:(pbuf + 1) * LC],
                                     hb["x0T"][m * 128:(m + 1) * 128, Wsp])], "ld")
            if bank2[pbuf] is not None:
                ctx.wait(nc.tensor, bank2[pbuf][0], bank2[pbuf][1])
            for k in range(G16):
                mm = nc.tensor.matmul(psW2[pbuf][:, 0:LC],
                                      wbuf2[:, wb * 2048 + k * 128:wb * 2048 + (k + 1) * 128],
                                      ytsl(k, Wsp), start=(k == 0), stop=(k == G16 - 1))
            tw = ctx.inc(mm, "TW")
            ctx.wait(nc.vector, "TW", tw)
            ctx.dwait(nc.vector, xr)
            if res_tok[pbuf] is not None:
                ctx.dwait(nc.vector, res_tok[pbuf])
            ve = ctx.inc(nc.vector.tensor_tensor(sb["statx"][:, pbuf * LC:(pbuf + 1) * LC],
                                                 sb["statsq"][:, pbuf * LC:(pbuf + 1) * LC],
                                                 psW2[pbuf][:, 0:LC], ALU.add), "VE")
            bank2[pbuf] = ("VE", ve)
            xres_rel[pbuf] = ("VE", ve)
            ctx.wait(nc.gpsimd, "VE", ve)
            res_tok[pbuf] = ctx.dgrp(nc.gpsimd, [(hb["xA"][m * 128:(m + 1) * 128, Wsp],
                                                  sb["statx"][:, pbuf * LC:(pbuf + 1) * LC])], "gw")
        w_tw[wb] = tw
    for t_ in res_tok:
        if t_ is not None:
            ctx.dwait(nc.sync, t_)
            ctx.dwait(nc.gpsimd, t_)


def emit_l0f(ctx, sb, hb):
    nc = ctx.nc
    NCh = L // LC
    emit_ln_stats(ctx, sb, hb["xA"], hb["st_r2"], hb["st_m2"], hb["sflat"])
    rowA = sb["scanA"][0:1, :]
    rowB = sb["scanB"][0:1, :]
    sc = sb["sc"]
    ctx.vsync(nc.vector.memset(sc[0:1, :], 0.0))
    wst = [sb["Bb"], sb["Cb"]]
    wtok = ctx.dgrp(nc.sync, [(wst[kt // 4][:, (kt % 4) * 1024:(kt % 4 + 1) * 1024],
                               hb["w0f"][kt * 128:(kt + 1) * 128, :]) for kt in range(8)], "ld")
    ctx.dwait(nc.vector, wtok)
    am = sb["tfix"]
    for kt in range(8):
        ctx.vsync(nc.vector.tensor_reduce(am[:, kt:kt + 1], wst[kt // 4][:, (kt % 4) * 1024:(kt % 4 + 1) * 1024],
                                axis=AX.X, op=ALU.max, apply_absolute_value=True))
    v = ctx.vsync(nc.vector.tensor_reduce(am[:, 8:9], am[:, 0:8], axis=AX.X, op=ALU.max))
    ctx.wait(nc.gpsimd, "V", v)
    t1 = ctx.dgrp(nc.gpsimd, [(hb["pmax"][0:128], am[:, 8:9])], "gw")
    ctx.dwait(nc.sync, t1)
    t2 = ctx.dgrp(nc.sync, [(sb["cc_row"][0:1, 0:128], hb["pmax"][0:128].unsqueeze(0))], "ld")
    ctx.dwait(nc.vector, t2)
    ctx.vsync(nc.vector.tensor_reduce(sc[0:1, 0:1], sb["cc_row"][0:1, 0:128], axis=AX.X, op=ALU.max))
    ctx.vsync(nc.vector.tensor_scalar(sc[0:1, 1:2], sc[0:1, 0:1], 1e-12, None, ALU.max))
    ctx.vsync(nc.vector.reciprocal(sc[0:1, 2:3], sc[0:1, 1:2]))
    ctx.vsync(nc.vector.tensor_scalar(sc[0:1, 3:4], sc[0:1, 2:3], 448.0, None, ALU.mult))
    ctx.vsync(nc.vector.tensor_scalar(sc[0:1, 4:5], sc[0:1, 3:4], 448.0, None, ALU.min))
    ctx.vsync(nc.vector.tensor_scalar(sc[0:1, 5:6], sc[0:1, 4:5], 0.5, None, ALU.mult))
    v = ctx.vsync(nc.vector.reciprocal(sc[0:1, 6:7], sc[0:1, 4:5]))
    ctx.wait(nc.sync, "V", v)
    t3 = ctx.dgrp(nc.sync, [(hb["scal"][0:16].unsqueeze(0), sc[0:1, 0:16])], "st")
    ctx.dwait(nc.sync, t3)
    t4 = ctx.dgrp(nc.sync, [(sb["scb"][:, :], hb["scal"][0:16].unsqueeze(0).partition_broadcast(128))], "ld")
    ctx.dwait(nc.vector, t4)
    for kt in range(8):
        wsl = wst[kt // 4][:, (kt % 4) * 1024:(kt % 4 + 1) * 1024]
        nc.vector.tensor_scalar(sb["q8t"][:, 0:1024], wsl, sb["scb"][:, 5:6], None, ALU.mult)
        v = ctx.inc(nc.vector.tensor_copy(sb["wqb"][:, kt * 1024:(kt + 1) * 1024], sb["q8t"][:, 0:1024]), "V")
    xcur, t1h = sb["xin"][:, 0:G8 * LC], sb["tmpc2"]
    rb, mb = sb["rb2"], sb["mb2"]
    ctx.vsync(nc.vector.memset(sb["amaxx"][:, :], 0.0))
    nacc = 0
    for c in range(NCh):
        W = slice(c * LC, (c + 1) * LC)
        ctx.wait(nc.sync, "V")
        xtok = ctx.dgrp(nc.sync, [(xcur[:, g * LC:(g + 1) * LC],
                                   hb["xA"][g * 128:(g + 1) * 128, W]) for g in range(G8)], "ld")
        btok = ctx.dgrp(nc.sync, [(rb[:, :], hb["st_r2"][W].unsqueeze(0).partition_broadcast(128)),
                                  (mb[:, :], hb["st_m2"][W].unsqueeze(0).partition_broadcast(128))], "ld")
        ctx.dwait(nc.vector, xtok)
        ctx.dwait(nc.vector, btok)
        for g in range(G8):
            gs = slice(g * LC, (g + 1) * LC)
            nc.vector.tensor_tensor(t1h[:, :], xcur[:, gs], rb[:, :], ALU.mult)
            nc.vector.tensor_tensor(t1h[:, :], t1h[:, :], mb[:, :], ALU.subtract)
            v = ctx.vsync(nc.vector.tensor_reduce(sb["amaxx"][:, 1 + nacc % 15:2 + nacc % 15], t1h[:, :],
                                                axis=AX.X, op=ALU.max, apply_absolute_value=True))
            nacc += 1
        if nacc >= 15:
            v = ctx.vsync(nc.vector.tensor_reduce(sb["amaxx"][:, 0:1], sb["amaxx"][:, 0:16],
                                                axis=AX.X, op=ALU.max))
            nacc = 0
    v = ctx.vsync(nc.vector.tensor_reduce(sb["amaxx"][:, 0:1], sb["amaxx"][:, 0:1 + nacc],
                                        axis=AX.X, op=ALU.max))
    ctx.wait(nc.gpsimd, "V", v)
    t1 = ctx.dgrp(nc.gpsimd, [(hb["pmax"][0:128], sb["amaxx"][:, 0:1])], "gw")
    ctx.dwait(nc.gpsimd, t1)
    t2 = ctx.dgrp(nc.gpsimd, [(hb["cc_in"][:, :], hb["pmax"][0:128].unsqueeze(0))], "gw")
    ctx.dwait(nc.gpsimd, t2)
    ctx.inc(nc.gpsimd.collective_compute("AllReduce", ALU.max, replica_groups=[list(range(NCORES))],
                                         ins=[hb["cc_in"][:, :]], outs=[hb["cc_out"][:, :]]), "CC")
    ctx.wait(nc.sync, "CC")
    t2 = ctx.dgrp(nc.sync, [(sb["cc_row"][0:1, 0:128], hb["cc_out"][:, :])], "ld")
    ctx.dwait(nc.vector, t2)
    ctx.vsync(nc.vector.tensor_reduce(sc[0:1, 8:9], sb["cc_row"][0:1, 0:128], axis=AX.X, op=ALU.max))
    ctx.vsync(nc.vector.tensor_scalar(sc[0:1, 9:10], sc[0:1, 8:9], 1e-12, None, ALU.max))
    ctx.vsync(nc.vector.reciprocal(sc[0:1, 10:11], sc[0:1, 9:10]))
    ctx.vsync(nc.vector.tensor_scalar(sc[0:1, 11:12], sc[0:1, 10:11], 448.0, None, ALU.mult))
    ctx.vsync(nc.vector.tensor_scalar(sc[0:1, 12:13], sc[0:1, 11:12], 448.0, None, ALU.min))
    ctx.vsync(nc.vector.tensor_scalar(sc[0:1, 13:14], sc[0:1, 12:13], 0.5, None, ALU.mult))
    ctx.vsync(nc.vector.reciprocal(sc[0:1, 14:15], sc[0:1, 12:13]))
    ctx.vsync(nc.vector.tensor_tensor(sc[0:1, 15:16], sc[0:1, 14:15], sc[0:1, 6:7], ALU.mult))
    ctx.vsync(nc.vector.tensor_scalar(sc[0:1, 15:16], sc[0:1, 15:16], 4.0, None, ALU.mult))
    nc.vector.tensor_scalar(rowA[0:1, L:2 * L], rowA[0:1, 0:L], sc[0:1, 13:14], None, ALU.mult)
    v = ctx.inc(nc.vector.tensor_scalar(rowB[0:1, 0:L], rowB[0:1, L:2 * L],
                                        sc[0:1, 13:14], None, ALU.mult), "V")
    ctx.wait(nc.sync, "V", v)
    t3 = ctx.dgrp(nc.sync, [(hb["st_r2b"][:].unsqueeze(0), rowA[0:1, L:2 * L]),
                            (hb["st_m2b"][:].unsqueeze(0), rowB[0:1, 0:L]),
                            (hb["scal"][0:16].unsqueeze(0), sc[0:1, 0:16])], "st")
    ctx.dwait(nc.sync, t3)
    t4 = ctx.dgrp(nc.sync, [(sb["scb"][:, :], hb["scal"][0:16].unsqueeze(0).partition_broadcast(128))], "ld")
    ctx.dwait(nc.vector, t4)
    ctx.dwait(nc.scalar, t4)
    if DEBUG:
        ctx.wait(nc.gpsimd, "V")
        dt_ = ctx.dgrp(nc.gpsimd, [(hb["d_wq"][kt * 128:(kt + 1) * 128, :],
                                    sb["wqb"][:, kt * 1024:(kt + 1) * 1024]) for kt in range(8)], "gw")
        ctx.dwait(nc.gpsimd, dt_)
    xc2 = [sb["scanA"], sb["scanB"]]
    psW = sb["ps_w"]
    ctx.wait(nc.tensor, "SE")
    ctx.wait(nc.tensor, "VE")
    bank_rel = [None, None]
    res_tok = [None, None]
    ve_par = [0, 0]
    v_hq = 0
    for c in range(NCh):
        W = slice(c * LC, (c + 1) * LC)
        xcur = xc2[c % 2]
        ctx.wait(nc.sync, "V", v_hq)
        ctx.wait(nc.sync, "VE", ve_par[c % 2])
        xtok = ctx.dgrp(nc.sync, [(xcur[:, g * LC:(g + 1) * LC],
                                   hb["xA"][g * 128:(g + 1) * 128, W]) for g in range(G8)], "ld")
        btok = ctx.dgrp(nc.sync, [(rb[:, :], hb["st_r2b"][W].unsqueeze(0).partition_broadcast(128)),
                                  (mb[:, :], hb["st_m2b"][W].unsqueeze(0).partition_broadcast(128))], "ld")
        ctx.dwait(nc.vector, xtok)
        ctx.dwait(nc.vector, btok)
        for g in range(G8):
            gs = slice(g * LC, (g + 1) * LC)
            nc.vector.tensor_tensor(t1h[:, :], xcur[:, gs], rb[:, :], ALU.mult)
            nc.vector.tensor_tensor(sb["q8h"][:, gs], t1h[:, :], mb[:, :], ALU.subtract)
            v = ctx.inc(nc.vector.tensor_copy(sb["hqb"][:, gs], sb["q8h"][:, gs]), "V")
        v_hq = ctx.cnt["V"]
        if DEBUG and c == 0:
            ctx.wait(nc.gpsimd, "V")
            dt_ = ctx.dgrp(nc.gpsimd, [(hb["d_hq"][g * 128:(g + 1) * 128, :],
                                        sb["hqb"][:, g * LC:(g + 1) * LC]) for g in range(G8)], "gw")
            ctx.dwait(nc.gpsimd, dt_)
        ctx.wait(nc.tensor, "V")
        for m in range(G8):
            buf = m % 2
            if bank_rel[buf] is not None:
                ctx.wait(nc.tensor, bank_rel[buf][0], bank_rel[buf][1])
            for k in range(G8):
                mm = nc.tensor.matmul(psW[buf][:, 0:LC],
                                      sb["wqb"][:, k * 1024 + m * 128:k * 1024 + (m + 1) * 128],
                                      sb["hqb"][:, k * LC:(k + 1) * LC], start=(k == 0), stop=(k == G8 - 1))
            ctx.inc(mm, "TW")
            ctx.wait(nc.scalar, "TW")
            se = ctx.inc(nc.scalar.activation(sb["obf"][:, buf * LC:(buf + 1) * LC], psW[buf][:, 0:LC],
                                              AF.Identity, bias=sb["b0f"][:, m:m + 1], scale=sb["scb"][:, 15:16]), "SE")
            bank_rel[buf] = ("SE", se)
            ctx.wait(nc.vector, "SE", se)
            if res_tok[buf] is not None:
                ctx.dwait(nc.vector, res_tok[buf])
            ve = ctx.inc(nc.vector.tensor_tensor(sb["stage3"][:, buf * LC:(buf + 1) * LC],
                                                 xcur[:, m * LC:(m + 1) * LC],
                                                 sb["obf"][:, buf * LC:(buf + 1) * LC], ALU.add), "VE")
            ctx.wait(nc.sync, "VE", ve)
            res_tok[buf] = ctx.dgrp(nc.sync, [(hb["xB"][m * 128:(m + 1) * 128, W],
                                               sb["stage3"][:, buf * LC:(buf + 1) * LC])], "st")
        ve_par[c % 2] = ctx.cnt["VE"]
    for t_ in res_tok:
        if t_ is not None:
            ctx.dwait(nc.sync, t_)


def emit_l1a(ctx, sb, hb):
    nc = ctx.nc
    v_l0f_done = ctx.cnt["V"]
    NS = L // LC
    NQ = L // 128
    emit_ln_stats(ctx, sb, hb["xB"], hb["st_r3"], hb["st_m3"], hb["sflat"])
    h2a = sb["scanA"].bitcast(BF16)
    h2b = sb["scanB"].bitcast(BF16)

    def h2sl(k, cs):
        t = h2a if k < 4 else h2b
        return t[:, (k % 4) * 2048 + cs.start:(k % 4) * 2048 + cs.stop]

    xs2, rb, mb = sb["statx"], sb["rb2"], sb["mb2"]
    for s in range(NS):
        W = slice(s * LC, (s + 1) * LC)
        ctx.wait(nc.sync, "V")
        btok = ctx.dgrp(nc.sync, [(rb[:, :], hb["st_r3"][W].unsqueeze(0).partition_broadcast(128)),
                                  (mb[:, :], hb["st_m3"][W].unsqueeze(0).partition_broadcast(128))], "ld")
        ctx.dwait(nc.vector, btok)
        for g in range(G8):
            hh = g % 2
            xsl = xs2[:, hh * LC:(hh + 1) * LC]
            ctx.wait(nc.sync, "V")
            xtok = ctx.dgrp(nc.sync, [(xsl, hb["xB"][g * 128:(g + 1) * 128, W])], "ld")
            ctx.dwait(nc.vector, xtok)
            nc.vector.tensor_tensor(sb["tmpc2"][:, :], xsl, rb[:, :], ALU.mult)
            wslice = h2sl(g, W)

            ctx.inc(nc.vector.tensor_tensor(wslice, sb["tmpc2"][:, :], mb[:, :], ALU.subtract), "V")
            if (DEBUG_H2 & 16) and g == 7:
                ctx.wait(nc.sync, "V")
                d_ = ctx.dgrp(nc.sync, [(hb["d_dbg"][0], xsl), (hb["d_dbg"][1], sb["tmpc2"][:, 0:LC]),
                                        (hb["d_dbg"][2], rb[:, 0:LC]), (hb["d_dbg"][3], mb[:, 0:LC])], "ld")
                ctx.dwait(nc.sync, d_)
                ctx.wait(nc.gpsimd, "V")
                dg_ = ctx.dgrp(nc.gpsimd, [(hb["d_dbg"][4], wslice)], "gw")
                ctx.dwait(nc.gpsimd, dg_)
    v_h2_done = ctx.cnt["V"]
    if DEBUG_H2 & 1:
        ctx.wait(nc.gpsimd, "V")
        dtk = ctx.dgrp(nc.gpsimd, [(hb["d_h2f"][g], h2sl(g, slice(0, L))) for g in range(G8)], "gw")
        ctx.dwait(nc.gpsimd, dtk)
    wv = sb["Bb"].bitcast(BF16)
    vA = sb["xin"].bitcast(BF16)
    vB = sb["delta"].bitcast(BF16)
    ctx.wait(nc.gpsimd, "V")
    wvtok = ctx.dgrp(nc.gpsimd, [(wv[:, :].rearrange("p (k f) -> p k f", k=8),
                                  hb["wv"][:, :].rearrange("(k p) f -> p k f", p=128))], "gw")
    for t in range(16):
        ctx.inc(nc.vector.memset(vA[:, t * 520:t * 520 + 520].rearrange("p (h e) -> p h e", e=65)[:, :, 64:65], 1.0), "V")
        ctx.inc(nc.vector.memset(vB[:, t * 520:t * 520 + 520].rearrange("p (h e) -> p h e", e=65)[:, :, 64:65], 1.0), "V")
    ctx.dwait(nc.tensor, wvtok)
    ctx.wait(nc.tensor, "V")
    ctx.wait(nc.tensor, "SE")
    ctx.wait(nc.tensor, "VE")
    tw_pre_v = ctx.cnt["TW"]
    psV, ev_mark = [sb["ps_w"][0], sb["ps_w"][1]], []
    for lt in range(NQ):
        for nv in range(2):
            buf = (lt * 2 + nv) % 2
            if len(ev_mark) >= 2:
                ctx.wait(nc.tensor, "SE", ev_mark[-2])
            for k in range(G8):
                mm = nc.tensor.matmul(psV[buf][:, 0:LC],
                                      h2sl(k, slice(lt * 128, (lt + 1) * 128)),
                                      wv[:, k * 1024 + nv * 512:k * 1024 + (nv + 1) * 512],
                                      start=(k == 0), stop=(k == G8 - 1))
            t = ctx.inc(mm, "TW")
            ctx.wait(nc.scalar, "TW", t)
            dst = (vA if nv == 0 else vB)[:, lt * 520:(lt + 1) * 520].rearrange("p (h e) -> p h e", e=65)[:, :, 0:64]
            ev_mark.append(ctx.inc(nc.scalar.copy(dst, psV[buf][:, 0:LC].rearrange("p (h e) -> p h e", e=64)), "SE"))
    if DEBUG_H2 & 2:
        ctx.wait(nc.gpsimd, "SE")
        dtk = ctx.dgrp(nc.gpsimd, [(hb["d_v"][0], vA[:, 0:NQ * 520]), (hb["d_v"][1], vB[:, 0:NQ * 520])], "gw")
        ctx.dwait(nc.gpsimd, dtk)
    qf = sb["Bb"].bitcast(BF16)
    kf = sb["Cb"].bitcast(BF16)
    wqp = sb["yout"].bitcast(BF16) if False else sb["yout"]
    wkp = sb["wbuf2"]
    ctx.wait(nc.tensor, "SE")
    ctx.wait(nc.gpsimd, "TW", tw_pre_v)
    wqtok = ctx.dgrp(nc.gpsimd, [(wqp[:, :].rearrange("p (k f) -> p k f", k=8),
                                  hb["wqp"][:, :].rearrange("(k p) f -> p k f", p=128)),
                                 (wkp[:, :].rearrange("p (k f) -> p k f", k=8),
                                  hb["wkp"][:, :].rearrange("(k p) f -> p k f", p=128))], "gw")
    ctx.dwait(nc.tensor, wqtok)
    qk_mark = list(ev_mark[-2:])
    for which, wt, dst in ((0, wqp, qf), (1, wkp, kf)):
        for t in range(4):
            for s in range(NS):
                buf = len(qk_mark) % 2
                if len(qk_mark) >= 2:
                    ctx.wait(nc.tensor, "SE", qk_mark[-2])
                for k in range(G8):
                    mm = nc.tensor.matmul(psV[buf][:, 0:LC], wt[:, k * 512 + t * 128:k * 512 + (t + 1) * 128],
                                          h2sl(k, slice(s * LC, (s + 1) * LC)), start=(k == 0), stop=(k == G8 - 1))
                tm = ctx.inc(mm, "TW")
                ctx.wait(nc.scalar, "TW", tm)
                qk_mark.append(ctx.inc(nc.scalar.copy(dst[:, t * 2048 + s * LC:t * 2048 + (s + 1) * LC],
                                                      psV[buf][:, 0:LC]), "SE"))
    if DEBUG_H2 & 4:
        ctx.wait(nc.gpsimd, "SE")
        dtk = ctx.dgrp(nc.gpsimd, [(hb["d_qf"][t], qf[:, t * 2048:t * 2048 + L]) for t in range(4)] +
                                  [(hb["d_kf"][t], kf[:, t * 2048:t * 2048 + L]) for t in range(4)], "gw")
        ctx.dwait(nc.gpsimd, dtk)
    psS = [sb["ps_w"][0], sb["ps_w"][1], sb["ps_x"]]
    psN = [sb["ps_st0"], sb["ps_st1"]]
    afb = [sb["stage2"][:, 0:LC], sb["statx"][:, 0:LC], sb["statsq"][:, 0:LC]]
    atT = sb["atT"]
    atS = [atT[:, 0:512], atT[:, 512:1024], sb["Aneg"].bitcast(BF16)[:, 0:512]]
    import os
    _m = os.environ.get('ND3', 'sxa')
    NDs = 3 if 's' in _m else 2
    NDf = 3 if 'x' in _m else 2
    NDa = 3 if 'a' in _m else 2
    ND = min(NDs, NDf, NDa)
    c_sq, c_bi = 1.0 / np.sqrt(32.0), 1.0 / np.sqrt(2.0)
    sq_i = 0
    nm_mark = {}
    nm_at = {}
    av_at = {}
    pend_nums = []
    ctx.wait(nc.scalar, "V")
    ctx.wait(nc.tensor, "V")
    ctx.wait(nc.tensor, "SE")
    ctx.wait(nc.tensor, "S")
    QB = NQ // 4
    for qb in range(QB):
        qsl = slice(qb * 512, (qb + 1) * 512)
        for hd in range(16):
            off = 32 * (hd % 4)
            tq = hd // 4
            pnb = (qb * 16 + hd) % 2
            pn = psN[pnb]
            if nm_mark.get(pnb) is not None:
                ctx.wait(nc.tensor, "YV", nm_mark[pnb])
            kfh = kf[:, tq * 2048:(tq + 1) * 2048]
            qfh = qf[:, tq * 2048:(tq + 1) * 2048]
            for kc in range(4 * qb + 4):
                bufS = sq_i % NDs
                bufF = sq_i % NDf
                bufA = sq_i % NDa
                if sq_i >= ND:
                    ctx.wait(nc.tensor, "SQE", sq_i - ND)
                mm = nc.tensor.matmul(psS[bufS][:, 0:512],
                                      kfh[off:off + 16, kc * 128:(kc + 1) * 128],
                                      qfh[off:off + 16, qsl],
                                      start=True, stop=True, tile_position=(off, 0))
                tm = ctx.inc(mm, "T")
                ctx.wait(nc.scalar, "T", tm)
                if sq_i >= ND:
                    ctx.wait(nc.scalar, "AV", av_at[sq_i - ND])
                s1 = ctx.inc(nc.scalar.activation(afb[bufF], psS[bufS][:, 0:512],
                                                  AF.Square, bias=ctx.cst(c_bi), scale=c_sq), "SQE")
                ctx.wait(nc.vector, "SQE", s1)
                if sq_i >= NDa and (sq_i - NDa) in nm_at:
                    ctx.wait(nc.vector, "NM", nm_at[sq_i - NDa])
                adst = atS[bufA]
                jb = kc - 4 * qb
                av = ctx.inc(nc.vector.tensor_scalar(adst, afb[bufF], 0.5, None, ALU.add), "AV")
                if jb >= 0:
                    av = ctx.inc(nc.vector.scalar_tensor_tensor(
                        adst[:, jb * 128:(jb + 1) * 128], afb[bufF][:, jb * 128:(jb + 1) * 128],
                        ctx.cst(0.5), sb["masktri"][:, :], ALU.add, ALU.mult), "AV")
                av_at[sq_i] = av
                vsl = (vA if hd < 8 else vB)[:, kc * 520 + (hd % 8) * 65:kc * 520 + (hd % 8) * 65 + 65]

                def mk_num(kc=kc, bufA=bufA, av=av, vsl=vsl, si=sq_i, qb=qb, pn=pn):
                    def emit_num():
                        ctx.wait(nc.tensor, "AV", av)
                        last_nm = None
                        for j in range(4):
                            if kc <= 4 * qb + j:
                                last_nm = nc.tensor.matmul(pn[:, j * 65:j * 65 + 65],
                                                           atS[bufA][:, j * 128:(j + 1) * 128],
                                                           vsl, start=(kc == 0 and j == 0),
                                                           stop=(kc == 4 * qb + 3 and j == 3))
                        nm_at[si] = ctx.inc(last_nm, "NM")
                    return emit_num
                pend_nums.append(mk_num())
                if len(pend_nums) > 2:
                    pend_nums.pop(0)()
                sq_i += 1
            while pend_nums:
                pend_nums.pop(0)()
            tn = ctx.cnt["NM"]
            ctx.wait(nc.scalar, "NM", tn)
            sd = ctx.inc(nc.scalar.copy(sb["denf"][:, 0:4].rearrange("p (j o) -> p j o", o=1),
                                        pn[:, 0:260].rearrange("p (j e) -> p j e", e=65)[:, :, 64:65]), "S")
            ctx.wait(nc.vector, "S", sd)
            ctx.vsync(nc.vector.tensor_scalar(sb["denf"][:, 4:8], sb["denf"][:, 0:4], 1e-12, None, ALU.add))
            ctx.vsync(nc.vector.reciprocal(sb["denf"][:, 8:12], sb["denf"][:, 4:8]))
            yvv = None
            for j in range(4):
                qc = 4 * qb + j
                ysl = (sb["ysbA"] if hd < 8 else sb["ysbB"])[:, qc * 512 + (hd % 8) * 64:qc * 512 + (hd % 8) * 64 + 64]
                yvv = ctx.inc(nc.vector.tensor_scalar(ysl, pn[:, j * 65:j * 65 + 64],
                                                      sb["denf"][:, 8 + j:9 + j], None, ALU.mult), "YV")
            nc.vector.wait_ge(ctx.sems["YV"], yvv)
            nm_mark[pnb] = yvv
    yf = sb["xcb"]
    psT = [sb["ps_x"], sb["ps_d"][0], sb["ps_d"][1], sb["ps_st0"]]
    wo = sb["wbuf"]
    xres = sb["statsq"]
    ctx.wait(nc.sync, "V", v_h2_done)
    ctx.wait(nc.sync, "SQE")
    ctx.wait(nc.gpsimd, "TW")
    ctx.wait(nc.tensor, "YV")
    ctx.wait(nc.tensor, "SQE")
    tr_mark = []
    res_tok = [None, None]
    wo_rel = [None, None]
    wo_tw = [None, None]
    tr_i = 0
    for s in range(NS):
        W = slice(s * LC, (s + 1) * LC)
        for lt in range(4 * s, 4 * s + 4):
            for mt in range(8):
                ysrc = (sb["ysbA"] if mt < 4 else sb["ysbB"])
                h0_ = (2 * mt) % 8
                src_ap = ysrc[:, lt * 512 + h0_ * 64:lt * 512 + h0_ * 64 + 128]
                buf = tr_i % 4
                if len(tr_mark) >= 4:
                    ctx.wait(nc.tensor, "TRE", tr_mark[-4])
                mm = nc.tensor.transpose(psT[buf].bitcast(BF16)[:, 0:128], src_ap, sb["identb"][:, :])
                tm = ctx.inc(mm, "T")
                ctx.wait(nc.scalar, "T", tm)
                tr_mark.append(ctx.inc(nc.scalar.copy(yf[:, mt * 512 + (lt % 4) * 128:mt * 512 + (lt % 4) * 128 + 128],
                                                      psT[buf].bitcast(BF16)[:, 0:128]), "TRE"))
                tr_i += 1
        ctx.wait(nc.tensor, "TRE")
        for m in range(G8):
            buf = m % 2
            if wo_tw[buf] is not None:
                ctx.wait(nc.gpsimd, "TW", wo_tw[buf])
            wotok = ctx.dgrp(nc.gpsimd, [(wo[:, buf * 1024:(buf + 1) * 1024].rearrange("p (k f) -> p k f", k=8),
                                          hb["wo"][:, m * 128:(m + 1) * 128].rearrange("(k p) f -> p k f", p=128))], "gw")
            if wo_rel[buf] is not None:
                ctx.wait(nc.sync, wo_rel[buf][0], wo_rel[buf][1])
            xtok = ctx.dgrp(nc.sync, [(xres[:, buf * LC:(buf + 1) * LC], hb["xB"][m * 128:(m + 1) * 128, W])], "ld")
            ctx.dwait(nc.tensor, wotok)
            if wo_rel[buf] is not None:
                ctx.wait(nc.tensor, wo_rel[buf][0], wo_rel[buf][1])
            for k in range(G8):
                mm = nc.tensor.matmul(psV[buf][:, 0:LC], wo[:, buf * 1024 + k * 128:buf * 1024 + (k + 1) * 128],
                                      yf[:, k * 512:(k + 1) * 512], start=(k == 0), stop=(k == G8 - 1))
            tm = ctx.inc(mm, "TW")
            wo_tw[buf] = tm
            ctx.wait(nc.vector, "TW", tm)
            ctx.dwait(nc.vector, xtok)
            if res_tok[buf] is not None:
                ctx.dwait(nc.vector, res_tok[buf])
            ve = ctx.inc(nc.vector.tensor_tensor(sb["stage4"][:, buf * LC:(buf + 1) * LC],
                                                 xres[:, buf * LC:(buf + 1) * LC],
                                                 psV[buf][:, 0:LC], ALU.add), "VE")
            wo_rel[buf] = ("VE", ve)
            ctx.wait(nc.sync, "VE", ve)
            res_tok[buf] = ctx.dgrp(nc.sync, [(hb["xA"][m * 128:(m + 1) * 128, W],
                                               sb["stage4"][:, buf * LC:(buf + 1) * LC])], "st")
        ctx.wait(nc.tensor, "VE")
    for t_ in res_tok:
        if t_ is not None:
            ctx.dwait(nc.sync, t_)



def emit_l1f(ctx, sb, hb):
    nc = ctx.nc
    NS = L // LC
    ctx.wait(nc.tensor, "YV")
    ctx.wait(nc.tensor, "S")
    ctx.wait(nc.tensor, "VE")
    ctx.wait(nc.tensor, "TRE")
    ctx.wait(nc.tensor, "SQE")
    ctx.wait(nc.sync, "VE")
    ctx.wait(nc.vector, "NM")
    emit_ln_stats(ctx, sb, hb["xA"], hb["st_r4"], hb["st_m4"], hb["sflat"])
    ctx.inc(nc.vector.memset(sb["ones_bf"][:, :], 1.0), "V")
    ctx.inc(nc.vector.memset(sb["cc_row"][0:1, :], 1.0), "V")
    smtok = ctx.dgrp(nc.sync, [(sb["scb"][:, :], hb["g1b1"][:, :]),
                               (sb["bb1sb"][:, :], hb["bb1h"][:, :]),
                               (sb["amaxx"][:, 0:8], hb["bb2hd"][:, :])], "ld")
    ctx.dwait(nc.vector, smtok)
    ctx.inc(nc.vector.tensor_scalar(sb["bb2h"][:, :], sb["amaxx"][:, 0:8], 0.5, None, ALU.mult), "V")

    vA = sb["xin"].bitcast(BF16)
    vB = sb["delta"].bitcast(BF16)

    def hpsl(g, cs):
        t = vA if g < 4 else vB
        return t[:, (g % 4) * 2048 + cs.start:(g % 4) * 2048 + cs.stop]

    ysbA, ysbB = sb["ysbA"], sb["ysbB"]

    def hhsl(g, cs):
        t = ysbA if g < 4 else ysbB
        return t[:, (g % 4) * 2048 + cs.start:(g % 4) * 2048 + cs.stop]

    rb4, mb4 = sb["statx"][:, 0:LC], sb["statx"][:, LC:2 * LC]
    sqb_t = sb["xc"]
    ps0, ps1 = sb["ps_st0"], sb["ps_st1"]
    t1mark = []
    st_all = []
    st_tok = None
    ctx.wait(nc.tensor, "V")
    for s in range(NS):
        W = slice(s * LC, (s + 1) * LC)
        ctx.wait(nc.sync, "V")
        btok = ctx.dgrp(nc.sync, [(rb4, hb["st_r4"][W].unsqueeze(0).partition_broadcast(128)),
                                  (mb4, hb["st_m4"][W].unsqueeze(0).partition_broadcast(128))], "ld")
        ctx.dwait(nc.vector, btok)
        for g in range(G8):
            h = g % 2
            xsl = sb["statsq"][:, h * LC:(h + 1) * LC]
            sq = sqb_t[:, h * LC:(h + 1) * LC]
            ctx.wait(nc.sync, "V")
            xtok = ctx.dgrp(nc.sync, [(xsl, hb["xA"][g * 128:(g + 1) * 128, W])], "ld")
            ctx.dwait(nc.vector, xtok)
            nc.vector.tensor_tensor(sb["stage2"][:, :], xsl, rb4, ALU.mult)
            hv = ctx.inc(nc.vector.tensor_tensor(hpsl(g, W), sb["stage2"][:, :], mb4, ALU.subtract), "V")
            ctx.wait(nc.scalar, "V", hv)
            if len(t1mark) >= 2:
                ctx.wait(nc.scalar, "T", t1mark[-2])
            sqm = ctx.inc(nc.scalar.activation(sq, hpsl(g, W), AF.Square), "S")
            ctx.wait(nc.tensor, "V", hv)
            ctx.inc(nc.tensor.matmul(ps0[0:1, 0:LC], sb["ones_bf"][:, 0:1], hpsl(g, W),
                                     start=(g == 0), stop=(g == G8 - 1)), "T")
            ctx.wait(nc.tensor, "S", sqm)
            t1mark.append(ctx.inc(nc.tensor.matmul(ps1[0:1, 0:LC], sb["ones_f32"][:, 0:1], sq,
                                                   start=(g == 0), stop=(g == G8 - 1)), "T"))
        ctx.wait(nc.scalar, "T")
        if st_tok is not None:
            ctx.dwait(nc.scalar, st_tok)
        ctx.inc(nc.scalar.copy(sb["sums2"][0:1, 0:LC], ps0[0:1, 0:LC]), "S")
        se = ctx.inc(nc.scalar.copy(sb["sums2"][0:1, LC:2 * LC], ps1[0:1, 0:LC]), "S")
        ctx.wait(nc.tensor, "S", se)
        ctx.wait(nc.sync, "S", se)
        st_tok = ctx.dgrp(nc.sync, [
            (hb["sflat"][s * LC:(s + 1) * LC].unsqueeze(0), sb["sums2"][0:1, 0:LC]),
            (hb["sflat"][L + s * LC:L + (s + 1) * LC].unsqueeze(0), sb["sums2"][0:1, LC:2 * LC])], "st")
        st_all.append(st_tok)
    for t_ in st_all:
        ctx.dwait(nc.sync, t_)
    _ln_tail(ctx, sb, hb["st_r5"], hb["st_m5"], hb["sflat"])

    w1t = [sb["scanA"].bitcast(BF16), sb["scanB"].bitcast(BF16),
           sb["Bb"].bitcast(BF16), sb["Cb"].bitcast(BF16)]

    def w1sl(kb, cs):
        return w1t[kb // 2][:, (kb % 2) * 4096 + cs.start:(kb % 2) * 4096 + cs.stop]

    xq = sb["xcur"].bitcast(BF16)
    psB = sb["ps_w"]
    c128 = ctx.cst(128.0)
    bc_rel = [None, None]
    s_rel = None
    for kb in range(G8):
        ctx.wait(nc.gpsimd, "VE")
        qtok = ctx.dgrp(nc.gpsimd, [(xq[:, :], hb["q1h"][kb * 128:(kb + 1) * 128, :])], "gw")
        for piece in range(4):
            if s_rel is not None:
                ctx.wait(nc.sync, "T", s_rel)
            stok = ctx.dgrp(nc.sync, [(sb["sums2"][0:1, :],
                                       hb["s1h"][kb, piece * 1024:(piece + 1) * 1024].unsqueeze(0))], "ld")
            ctx.dwait(nc.tensor, stok)
            for half in range(2):
                buf = half
                if bc_rel[buf] is not None:
                    ctx.wait(nc.tensor, bc_rel[buf][0], bc_rel[buf][1])
                tm = ctx.inc(nc.tensor.matmul(psB[buf][:, 0:512], sb["cc_row"][0:1, :],
                                              sb["sums2"][0:1, half * 512:(half + 1) * 512],
                                              start=True, stop=True), "T")
                s_rel = tm
                ctx.wait(nc.vector, "T", tm)
                ctx.dwait(nc.vector, qtok)
                cs = slice(piece * 1024 + half * 512, piece * 1024 + half * 512 + 512)
                ve = ctx.inc(nc.vector.scalar_tensor_tensor(w1sl(kb, cs), xq[:, cs], c128,
                                                            psB[buf][:, 0:512],
                                                            ALU.subtract, ALU.mult), "VE")
                bc_rel[buf] = ("VE", ve)

    w2st = []
    h0b = sb["h0"]
    h0_rel = [None, None]
    for kb2 in range(32):
        q = kb2 % 4
        ctx.wait(nc.gpsimd, "VE")
        qtok = ctx.dgrp(nc.gpsimd, [(xq[:, q * 1024:(q + 1) * 1024],
                                     hb["q2h"][kb2 * 128:(kb2 + 1) * 128, :])], "gw")
        if s_rel is not None:
            ctx.wait(nc.sync, "T", s_rel)
        stok = ctx.dgrp(nc.sync, [(sb["sums2"][0:1, :], hb["s2h"][kb2, :].unsqueeze(0))], "ld")
        ctx.dwait(nc.tensor, stok)
        hbuf = kb2 % 2
        if h0_rel[hbuf] is not None:
            ctx.dwait(nc.vector, h0_rel[hbuf])
        ve = None
        for half in range(2):
            buf = half
            if bc_rel[buf] is not None:
                ctx.wait(nc.tensor, bc_rel[buf][0], bc_rel[buf][1])
            tm = ctx.inc(nc.tensor.matmul(psB[buf][:, 0:512], sb["cc_row"][0:1, :],
                                          sb["sums2"][0:1, half * 512:(half + 1) * 512],
                                          start=True, stop=True), "T")
            s_rel = tm
            ctx.wait(nc.vector, "T", tm)
            ctx.dwait(nc.vector, qtok)
            ve = ctx.inc(nc.vector.scalar_tensor_tensor(
                h0b[:, hbuf * 1024 + half * 512:hbuf * 1024 + half * 512 + 512],
                xq[:, q * 1024 + half * 512:q * 1024 + half * 512 + 512], c128,
                psB[buf][:, 0:512], ALU.subtract, ALU.mult), "VE")
            bc_rel[buf] = ("VE", ve)
        ctx.wait(nc.sync, "VE", ve)
        tok = ctx.dgrp(nc.sync, [(hb["w2scr"][kb2 * 128:(kb2 + 1) * 128, :],
                                  h0b[:, hbuf * 1024:(hbuf + 1) * 1024])], "st")
        h0_rel[hbuf] = tok
        w2st.append(tok)

    for s in range(NS):
        W = slice(s * LC, (s + 1) * LC)
        ctx.wait(nc.sync, "V")
        btok = ctx.dgrp(nc.sync, [(rb4, hb["st_r5"][W].unsqueeze(0).partition_broadcast(128)),
                                  (mb4, hb["st_m5"][W].unsqueeze(0).partition_broadcast(128))], "ld")
        ctx.dwait(nc.vector, btok)
        for g in range(G8):
            nc.vector.tensor_tensor(sb["stage2"][:, :], hpsl(g, W), rb4, ALU.mult)
            nc.vector.tensor_tensor(sb["stage2"][:, :], sb["stage2"][:, :], mb4, ALU.subtract)
            ctx.inc(nc.vector.tensor_scalar(hhsl(g, W), sb["stage2"][:, :],
                                            sb["scb"][:, g:g + 1], sb["scb"][:, 8 + g:9 + g],
                                            ALU.mult, ALU.add), "V")

    xcurb = xq
    q8hb = sb["q8h"].bitcast(BF16)

    def h3sl(j):
        if j < 8:
            return sb["yout"][:, j * 512:(j + 1) * 512]
        if j < 16:
            return sb["xcb"][:, (j - 8) * 512:(j - 7) * 512]
        if j < 24:
            return xcurb[:, (j - 16) * 512:(j - 15) * 512]
        if j < 28:
            return h0b[:, (j - 24) * 512:(j - 23) * 512]
        return q8hb[:, (j - 28) * 512:(j - 27) * 512]

    psZ = [sb["ps_st0"], sb["ps_st1"]]
    ctx.wait(nc.tensor, "V")
    ctx.wait(nc.tensor, "VE")
    ctx.wait(nc.scalar, "VE")
    ctx.wait(nc.gpsimd, "TW")
    ctx.wait(nc.sync, "V")
    for t_ in w2st:
        ctx.dwait(nc.gpsimd, t_)
    ctx.dwait(nc.scalar, w2st[-1])
    ctx.dwait(nc.scalar, w2st[-2])
    bank_rel = [None, None]
    w_tw = [None, None, None]
    wslots = [sb["wbuf2"][:, 0:2048], sb["wbuf2"][:, 2048:4096], sb["wbuf"][:, 0:2048]]
    wi = 0
    xres_rel = [None, None]
    fc2_rel = [None, None]
    out_slots = [sb["stage2"][:, :], sb["atT"].bitcast(F32)[:, 0:LC]]
    out_gate = [None, None]
    st_out = []
    for s in range(NS):
        W = slice(s * LC, (s + 1) * LC)
        for m in range(32):
            buf = m % 2
            if bank_rel[buf] is not None:
                ctx.wait(nc.tensor, bank_rel[buf][0], bank_rel[buf][1])
            for kb in range(G8):
                mm = nc.tensor.matmul(psB[buf][:, 0:LC], w1sl(kb, slice(m * 128, (m + 1) * 128)),
                                      hhsl(kb, W), start=(kb == 0), stop=(kb == G8 - 1))
            tm = ctx.inc(mm, "TW")
            ctx.wait(nc.scalar, "TW", tm)
            se = ctx.inc(nc.scalar.activation(h3sl(m), psB[buf][:, 0:LC], AF.Silu,
                                              bias=sb["bb1sb"][:, m:m + 1]), "SE")
            bank_rel[buf] = ("SE", se)
        se_done = ctx.cnt["SE"]
        for m2 in range(G8):
            buf2 = m2 % 2
            if xres_rel[buf2] is not None:
                ctx.wait(nc.sync, xres_rel[buf2][0], xres_rel[buf2][1])
            xr = ctx.dgrp(nc.sync, [(sb["statsq"][:, buf2 * LC:(buf2 + 1) * LC],
                                     hb["xA"][m2 * 128:(m2 + 1) * 128, W])], "ld")
            tm2 = None
            for half in range(2):
                wb = wi % 3
                wi += 1
                if w_tw[wb] is not None:
                    ctx.wait(nc.gpsimd, "TW", w_tw[wb])
                wtok = ctx.dgrp(nc.gpsimd, [
                    (wslots[wb].rearrange("p (k f) -> p k f", f=128),
                     hb["w2scr"][half * 2048:(half + 1) * 2048,
                                 m2 * 128:(m2 + 1) * 128].rearrange("(k p) f -> p k f", p=128))], "gw")
                ctx.dwait(nc.tensor, wtok)
                if half == 0:
                    ctx.wait(nc.tensor, "SE", se_done)
                    if fc2_rel[buf2] is not None:
                        ctx.wait(nc.tensor, fc2_rel[buf2][0], fc2_rel[buf2][1])
                for t in range(16):
                    kb2 = half * 16 + t
                    mm = nc.tensor.matmul(psZ[buf2][:, 0:LC],
                                          wslots[wb][:, t * 128:(t + 1) * 128],
                                          h3sl(kb2), start=(kb2 == 0), stop=(kb2 == 31))
                tm2 = ctx.inc(mm, "TW")
                w_tw[wb] = tm2
            ctx.wait(nc.vector, "TW", tm2)
            ctx.dwait(nc.vector, xr)
            u = sb["statx"][:, buf2 * LC:(buf2 + 1) * LC]
            ve = ctx.inc(nc.vector.scalar_tensor_tensor(u, psZ[buf2][:, 0:LC], ctx.cst(0.5),
                                                        sb["statsq"][:, buf2 * LC:(buf2 + 1) * LC],
                                                        ALU.mult, ALU.add), "VE")
            fc2_rel[buf2] = ("VE", ve)
            xres_rel[buf2] = ("VE", ve)
            if out_gate[buf2] is not None:
                ctx.dwait(nc.vector, out_gate[buf2])
            yv = ctx.inc(nc.vector.scalar_tensor_tensor(out_slots[buf2], hpsl(m2, W),
                                                        sb["bb2h"][:, m2:m2 + 1], u,
                                                        ALU.add, ALU.add), "YV")
            ctx.wait(nc.sync, "YV", yv)
            out_gate[buf2] = ctx.dgrp(nc.sync, [(hb["out"][m2 * 128:(m2 + 1) * 128, W], out_slots[buf2])], "st")
            st_out.append(out_gate[buf2])
    for t_ in st_out[-4:]:
        ctx.dwait(nc.sync, t_)


def build(stage="full"):
    nc = bass.Bass("TRN2", target_bir_lowering=False, detect_race_conditions=RACECHECK)
    ctx = Ctx(nc)
    ctx.mkpool("ld", 8)
    ctx.mkpool("st", 6)
    ctx.mkpool("gw", 4)
    hb = {}
    hb["x0T"] = nc.dram_tensor("x0T", [D, L], F32, kind="ExternalInput").ap()
    hb["win"] = nc.dram_tensor("win", [D, 2 * DI], F32, kind="ExternalInput").ap()
    hb["wx"] = nc.dram_tensor("wx", [DI, 96], F32, kind="ExternalInput").ap()
    hb["wdt"] = nc.dram_tensor("wdt", [DTR, DI], F32, kind="ExternalInput").ap()
    hb["dtb"] = nc.dram_tensor("dtb", [128, 16], F32, kind="ExternalInput").ap()
    hb["wout"] = nc.dram_tensor("wout", [DI, D], F32, kind="ExternalInput").ap()
    hb["winbf"] = nc.dram_tensor("winbf", [D, 2 * DI], BF16).ap()
    hb["xscr"] = nc.dram_tensor("xscr", [DI, L], BF16).ap()
    hb["zscr"] = nc.dram_tensor("zscr", [DI, L], BF16).ap()
    hb["youtscr"] = nc.dram_tensor("youtscr", [DI, L], BF16).ap()
    hb["woutbf"] = nc.dram_tensor("woutbf", [DI, D], BF16).ap()
    hb["dwk"] = nc.dram_tensor("dwk", [128, 64], F32, kind="ExternalInput").ap()
    hb["dwb"] = nc.dram_tensor("dwb", [128, 16], F32, kind="ExternalInput").ap()
    hb["alog"] = nc.dram_tensor("alog", [128, 256], F32, kind="ExternalInput").ap()
    hb["dskip"] = nc.dram_tensor("dskip", [128, 16], F32, kind="ExternalInput").ap()
    hb["out"] = nc.dram_tensor("out", [D, L], F32, kind="ExternalOutput").ap()
    hb["xA"] = nc.dram_tensor("xA", [D, L], F32).ap()
    hb["xB"] = nc.dram_tensor("xB", [D, L], F32).ap()
    hb["Bflat"] = nc.dram_tensor("Bflat", [NST, L], BF16).ap()
    hb["Cflat"] = nc.dram_tensor("Cflat", [NST, L], BF16).ap()
    kd = "ExternalOutput" if DEBUG else "Internal"
    hb["st_r1"] = nc.dram_tensor("st_r1", [L], F32, kind=kd).ap()
    hb["st_r2"] = nc.dram_tensor("st_r2", [L], F32).ap()
    hb["st_m2"] = nc.dram_tensor("st_m2", [L], F32).ap()
    hb["st_r2b"] = nc.dram_tensor("st_r2b", [L], F32).ap()
    hb["st_r3"] = nc.dram_tensor("st_r3", [L], F32).ap()
    hb["st_m3"] = nc.dram_tensor("st_m3", [L], F32).ap()
    hb["st_r4"] = nc.dram_tensor("st_r4", [L], F32).ap()
    hb["st_m4"] = nc.dram_tensor("st_m4", [L], F32).ap()
    hb["st_r5"] = nc.dram_tensor("st_r5", [L], F32).ap()
    hb["st_m5"] = nc.dram_tensor("st_m5", [L], F32).ap()
    hb["q1h"] = nc.dram_tensor("q1h", [D, 4096], U8, kind="ExternalInput").ap()
    hb["s1h"] = nc.dram_tensor("s1h", [8, 4096], F32, kind="ExternalInput").ap()
    hb["q2h"] = nc.dram_tensor("q2h", [4096, D], U8, kind="ExternalInput").ap()
    hb["s2h"] = nc.dram_tensor("s2h", [32, D], F32, kind="ExternalInput").ap()
    hb["g1b1"] = nc.dram_tensor("g1b1", [128, 16], F32, kind="ExternalInput").ap()
    hb["bb1h"] = nc.dram_tensor("bb1h", [128, 32], F32, kind="ExternalInput").ap()
    hb["bb2hd"] = nc.dram_tensor("bb2hd", [128, 8], F32, kind="ExternalInput").ap()
    hb["w2scr"] = nc.dram_tensor("w2scr", [4096, D], BF16).ap()
    hb["wv"] = nc.dram_tensor("wv", [D, D], F32, kind="ExternalInput").ap()
    hb["wo"] = nc.dram_tensor("wo", [D, D], F32, kind="ExternalInput").ap()
    hb["wqp"] = nc.dram_tensor("wqp", [D, 512], F32, kind="ExternalInput").ap()
    hb["wkp"] = nc.dram_tensor("wkp", [D, 512], F32, kind="ExternalInput").ap()
    hb["maskt"] = nc.dram_tensor("maskt", [128, 128], F32, kind="ExternalInput").ap()
    hb["identb_h"] = nc.dram_tensor("identb_h", [128, 128], F32, kind="ExternalInput").ap()
    hb["st_m2b"] = nc.dram_tensor("st_m2b", [L], F32).ap()
    hb["pmax"] = nc.dram_tensor("pmax", [128], F32).ap()
    hb["scal"] = nc.dram_tensor("scal", [16], F32, kind=kd).ap()
    hb["cc_in"] = nc.dram_tensor("cc_in", [1, 128], F32).ap()
    hb["cc_out"] = nc.dram_tensor("cc_out", [1, 128], F32).ap()
    hb["w0f"] = nc.dram_tensor("w0f", [D, D], F32, kind="ExternalInput").ap()
    hb["b0fT"] = nc.dram_tensor("b0fT", [128, 8], F32, kind="ExternalInput").ap()
    hb["st_m1"] = nc.dram_tensor("st_m1", [L], F32, kind=kd).ap()
    hb["sflat"] = nc.dram_tensor("sflat", [2 * L], F32, kind=kd).ap()
    if DEBUG:
        for n in ("d_xc", "d_delta", "d_z", "d_yout", "d_h0", "d_xin"):
            hb[n] = nc.dram_tensor(n, [DI, L], F32, kind="ExternalOutput").ap()
        hb["d_wq"] = nc.dram_tensor("d_wq", [D, D], F32, kind="ExternalOutput").ap()
    if DEBUG_H2:
        hb["d_h2"] = nc.dram_tensor("d_h2", [128, L], F32, kind="ExternalOutput").ap()
        hb["d_h2f"] = nc.dram_tensor("d_h2f", [8, 128, L], F32, kind="ExternalOutput").ap()
        hb["d_v"] = nc.dram_tensor("d_v", [2, 128, (L // 128) * 520], F32, kind="ExternalOutput").ap()
        hb["d_qf"] = nc.dram_tensor("d_qf", [4, 128, L], F32, kind="ExternalOutput").ap()
        hb["d_kf"] = nc.dram_tensor("d_kf", [4, 128, L], F32, kind="ExternalOutput").ap()
        hb["d_ysb"] = nc.dram_tensor("d_ysb", [2, 128, (L // 128) * 512], F32, kind="ExternalOutput").ap()
        hb["d_dbg"] = nc.dram_tensor("d_dbg", [5, 128, LC], F32, kind="ExternalOutput").ap()

        hb["d_hq"] = nc.dram_tensor("d_hq", [D, LC], F32, kind="ExternalOutput").ap()

    sb = {}
    def sbuf(name, shape, dt=F32):
        sb[name] = nc.alloc_sbuf_tensor("sb_" + name, shape, dt).ap()

    sbuf("ones_f32", [128, 1])
    sbuf("consts", [128, 32])
    ctx.const_tile = sb["consts"]
    for c_ in (EPS_LN, 1e-12, 448.0, 0.5, 224.0, 128.0):
        ctx.cst(c_)
    sbuf("Aneg", [128, 256]); sbuf("dwk", [128, 64]); sbuf("dwb", [128, 16])
    sbuf("dtbias", [128, 16]); sbuf("dskip", [128, 16])
    sbuf("wx_sb", [128, 16 * 96], BF16); sbuf("wdt_sb", [64, DI], BF16)
    sbuf("statx", [128, 2 * LC]); sbuf("statsq", [128, 2 * LC]); sbuf("sums2", [1, 2 * LC])
    sbuf("xcur", [128, G8 * LC0]); sbuf("stage2", [128, 2 * LC0]); sbuf("h0", [128, G8 * LC0], BF16)
    sbuf("rb", [128, LC0]); sbuf("mb", [128, LC0])
    sbuf("xin", [128, 4160]); sbuf("zsilu", [128, G16 * LC0])
    sbuf("xc", [128, G16 * LC0]); sbuf("xcb", [128, G16 * LC0], BF16)
    sbuf("delta", [128, 4160]); sbuf("tmpc", [128, LC0]); sbuf("tfix", [128, 16])
    sbuf("wbuf", [128, 2 * 1024], BF16); sbuf("wbuf2", [128, 2 * 2048], BF16)
    sbuf("dt16", [64, LC0], BF16); sbuf("bc96", [96, LC0])
    sbuf("Bb", [128, 16 * LC0]); sbuf("Cb", [128, 16 * LC0])
    sbuf("scanA", [128, 16 * LC0]); sbuf("scanB", [128, 16 * LC0])
    sbuf("carry", [128, 256]); sbuf("yg", [128, LC0]); sbuf("yout", [128, G16 * LC0], BF16)
    sbuf("sc", [1, 16]); sbuf("cc_row", [1, 128]); sbuf("amaxx", [128, 16]); sbuf("scb", [128, 16])
    sbuf("attnf", [128, 256]); sbuf("atT", [128, 1024], BF16); sbuf("denf", [128, 12])
    sbuf("masktri", [128, 128]); sbuf("identb", [128, 128], BF16)
    sb["ysbA"] = sb["zsilu"].bitcast(BF16)
    sb["ysbB"] = sb["xc"].bitcast(BF16)
    sb["stage4"] = sb["statx"]
    sbuf("q8t", [128, 1024], FP8); sbuf("q8h", [128, G8 * LC], FP8)
    sbuf("b0f", [128, 8])
    sbuf("ones_bf", [128, 1], BF16); sbuf("bb1sb", [128, 32]); sbuf("bb2h", [128, 8])
    sb["xcur2"] = sb["scanA"]
    sb["tmpc2"] = sb["statx"][:, 0:LC]
    sb["rb2"] = sb["statsq"][:, 0:LC]
    sb["mb2"] = sb["statsq"][:, LC:2 * LC]
    sb["hqb"] = sb["xc"].bitcast(BF16)
    sb["wqb"] = sb["delta"].bitcast(BF16)
    sb["obf"] = sb["h0"]
    sb["stage3"] = sb["zsilu"]

    sb["ps_st0"] = nc.alloc_psum_tensor("ps_st0", [128, 512], F32).ap()
    sb["ps_st1"] = nc.alloc_psum_tensor("ps_st1", [128, 512], F32).ap()
    sb["ps_w"] = [nc.alloc_psum_tensor(f"ps_w{i}", [128, 512], F32).ap() for i in range(2)]
    sb["ps_x"] = nc.alloc_psum_tensor("ps_x", [128, 512], F32).ap()
    sb["ps_d"] = [nc.alloc_psum_tensor(f"ps_d{i}", [128, 256], F32).ap() for i in range(2)]

    ctx.vsync(nc.vector.memset(sb["ones_f32"][:, :], 1.0))
    small = [(sb["b0f"][:, :], hb["b0fT"][:, :]),
             (sb["Aneg"][:, :], hb["alog"][:, :]),
             (sb["dwk"][:, :], hb["dwk"][:, :]),
             (sb["dwb"][:, :], hb["dwb"][:, :]),
             (sb["dtbias"][:, :], hb["dtb"][:, :]),
             (sb["dskip"][:, :], hb["dskip"][:, :])]
    small.append((sb["masktri"][:, :], hb["maskt"][:, :]))
    tok_small = ctx.dgrp(nc.sync, small, "ld")
    idtok = ctx.dgrp(nc.gpsimd, [(sb["identb"][:, :], hb["identb_h"][:, :])], "gw")
    ctx.dwait(nc.tensor, idtok)
    if not NOGW:
        hb["w2d_tok"] = ctx.dgrp(ctx.nc.gpsimd, [(hb["winbf"], hb["win"]), (hb["woutbf"], hb["wout"])], "gw")
        wtok = ctx.dgrp(nc.gpsimd, [
            (sb["wx_sb"][:, :].rearrange("p (g f) -> p g f", f=96),
             hb["wx"].rearrange("(g p) f -> p g f", p=128)),
            (sb["wdt_sb"][:, :], hb["wdt"][:, :])], "gw")
    ctx.dwait(nc.scalar, tok_small)
    s = ctx.inc(nc.scalar.activation(sb["Aneg"][:, :], sb["Aneg"][:, :], AF.Exp), "S")
    ctx.wait(nc.vector, "S", s)
    ctx.inc(nc.vector.tensor_scalar(sb["Aneg"][:, :], sb["Aneg"][:, :], -1.0, None, ALU.mult), "V")
    if not NOGW:
        ctx.dwait(nc.tensor, wtok)

    if CUT >= 1:
        emit_ln_stats(ctx, sb, hb["x0T"], hb["st_r1"], hb["st_m1"], hb["sflat"]) if CUT < 2 else None
    if CUT >= 2:
        emit_ln_stats(ctx, sb, hb["x0T"], hb["st_r1"], hb["st_m1"], hb["sflat"])
        emit_l0a(ctx, sb, hb)
    if stage in ("l0f", "l1a", "l1f", "full"):
        emit_l0f(ctx, sb, hb)
    if stage in ("l1a", "l1f", "full"):
        emit_l1a(ctx, sb, hb)
    if stage in ("l1f", "full"):
        emit_l1f(ctx, sb, hb)

    copy_src = {"l0a": "xA", "l0f": "xB", "l1a": "xA"}.get(stage)
    if copy_src is not None:
        tcp = sb["statx"]
        for c in range(L // LC):
            W = slice(c * LC, (c + 1) * LC)
            for g in range(G8):
                tok = ctx.dgrp(nc.sync, [(tcp[:, 0:LC], hb[copy_src][g * 128:(g + 1) * 128, W])], "ld")
                ctx.dwait(nc.sync, tok)
                tok = ctx.dgrp(nc.sync, [(hb["out"][g * 128:(g + 1) * 128, W], tcp[:, 0:LC])], "st")
                ctx.dwait(nc.sync, tok)
    return nc


def host_prep(inputs, core):
    m = {}
    m["x0T"] = np.ascontiguousarray(np.asarray(inputs["L0_attn_hidden_states"])[core].T)
    m["win"] = np.asarray(inputs["L0_attn_Win"])
    m["wx"] = np.asarray(inputs["L0_attn_Wx"])
    m["wdt"] = np.asarray(inputs["L0_attn_Wdt"])
    m["dtb"] = np.ascontiguousarray(np.asarray(inputs["L0_attn_dt_bias"]).reshape(16, 128).T)
    m["wout"] = np.asarray(inputs["L0_attn_Wout"])
    m["dwk"] = np.ascontiguousarray(
        np.asarray(inputs["L0_attn_dw_kernel"]).T.reshape(16, 128, 4).transpose(1, 0, 2).reshape(128, 64))
    m["dwb"] = np.ascontiguousarray(np.asarray(inputs["L0_attn_dw_bias"]).reshape(16, 128).T)
    m["alog"] = np.ascontiguousarray(
        np.asarray(inputs["L0_attn_A_log"]).reshape(16, 128, 16).transpose(1, 0, 2).reshape(128, 256))
    m["w0f"] = np.ascontiguousarray(np.asarray(inputs["L0_ffn_weight"]).T)
    m["wv"] = np.ascontiguousarray(np.asarray(inputs["L1_attn_Wv"]).T)
    m["wo"] = np.ascontiguousarray(np.asarray(inputs["L1_attn_Wo"]).T)
    m["q1h"] = np.ascontiguousarray(np.asarray(inputs["L1_ffn_fc1_q"]).astype(np.uint8))
    m["s1h"] = np.asarray(inputs["L1_ffn_fc1_scales"])
    m["q2h"] = np.ascontiguousarray(np.asarray(inputs["L1_ffn_fc2_q"]).astype(np.uint8))
    m["s2h"] = np.asarray(inputs["L1_ffn_fc2_scales"])
    m["g1b1"] = np.ascontiguousarray(np.concatenate([
        np.asarray(inputs["L1_ffn_ln_gamma"]).reshape(8, 128).T,
        np.asarray(inputs["L1_ffn_ln_beta"]).reshape(8, 128).T], axis=1))
    m["bb1h"] = np.ascontiguousarray(np.asarray(inputs["L1_ffn_fc1_bias"]).reshape(32, 128).T)
    m["bb2hd"] = np.ascontiguousarray(np.asarray(inputs["L1_ffn_fc2_bias"]).reshape(8, 128).T)
    def padqk(w):
        wt = np.asarray(w).T.reshape(1024, 16, 16)
        out = np.zeros((1024, 16, 32), np.float32)
        out[:, :, :16] = wt
        return np.ascontiguousarray(out.reshape(1024, 512))
    m["wqp"] = padqk(inputs["L1_attn_Wq"])
    m["wkp"] = padqk(inputs["L1_attn_Wk"])
    m["maskt"] = np.triu(np.ones((128, 128), np.float32))
    m["identb_h"] = np.eye(128, dtype=np.float32)
    m["b0fT"] = np.ascontiguousarray(np.asarray(inputs["L0_ffn_bias"]).reshape(8, 128).T)
    m["dskip"] = np.ascontiguousarray(np.asarray(inputs["L0_attn_Dskip"]).reshape(16, 128).T)
    return m


def kernel(**inputs):
    nc = build(STAGE)
    in_maps = [host_prep(inputs, b) for b in range(8)]
    res = run_bass_kernel_spmd(nc, in_maps, core_ids=list(range(8)))
    out = np.stack([np.ascontiguousarray(r["out"].T) for r in res.results])
    return out.astype(np.float32)
